# version 11
# baseline (speedup 1.0000x reference)
import numpy as np
from contextlib import ExitStack

import concourse.bass as bass
import concourse.mybir as mybir
from concourse.bass_utils import run_bass_kernel_spmd

F32 = mybir.dt.float32
BF16 = mybir.dt.bfloat16
I32 = mybir.dt.int32
AF = mybir.ActivationFunctionType
ALU = mybir.AluOpType
AX = mybir.AxisListType

DBG = False
D = 2048
KC = 16
T = 1024
TPAD = 3
NH = 64
HD = 64
NS = 128
DIN = 4096
NCH = 34
MCH = 8
GH = 4
GDK = 512
GDV = 256
GCH = 8
FH = 8192
EPS = 1e-5
MAGIC = 0x5F3759DF
LNS512 = float(np.log(1.0 / np.sqrt(512.0)))


class Em:

    ENGS = ("pe", "ve", "ac", "po", "sy")

    def __init__(self, nc, stack):
        self.nc = nc
        self.stack = stack
        self.eng = {"pe": nc.tensor, "ve": nc.vector, "ac": nc.scalar,
                    "po": nc.gpsimd, "sy": nc.sync}
        self.sem = {}
        for e in self.ENGS:
            self.sem[e] = stack.enter_context(nc.semaphore(f"sem_{e}"))
        self.sem["dsy"] = stack.enter_context(nc.semaphore("sem_dsy"))
        self.sem["dpo"] = stack.enter_context(nc.semaphore("sem_dpo"))
        self.sem["cc"] = stack.enter_context(nc.semaphore("sem_cc"))
        self.cnt = {k: 0 for k in self.sem}
        self.last = {}
        self.dirty = set()
        self._names = {}
        self.pb = [stack.enter_context(nc.psum_tensor(f"pbank{i}", [128, 512], F32))
                   for i in range(8)]

    def bank(self, i, w=512):
        return self.pb[i][:, :w]

    def bank_b(self, i, w=512):
        return self.pb[i][:].bitcast(BF16)[:, :w]

    def name(self, prefix):
        n = self._names.get(prefix, 0)
        self._names[prefix] = n + 1
        return f"{prefix}{n}"

    def sb(self, prefix, shape, dtype):
        st = getattr(self, "_scope", None) or self.stack
        return st.enter_context(
            self.nc.sbuf_tensor(self.name(prefix), shape, dtype))

    def push_scope(self):
        self._scope = ExitStack()
        return self._scope

    def pop_scope(self):
        self._scope.close()
        self._scope = None

    def _t(self, e, inst):
        self.last[e] = inst
        self.dirty.add(e)
        return inst

    def mm(self, out, lhsT, rhs, start=True, stop=True):
        return self._t("pe", self.nc.tensor.matmul(_ap(out), _ap(lhsT), _ap(rhs), start=start, stop=stop))

    def trans(self, out, in_, ident):
        return self._t("pe", self.nc.tensor.transpose(_ap(out), _ap(in_), _ap(ident)))

    def tt(self, out, a, b, op):
        return self._t("ve", self.nc.vector.tensor_tensor(_ap(out), _ap(a), _ap(b), op))

    def ts(self, out, a, s1, s2=None, op0=ALU.mult, op1=None):
        if op1 is None:
            return self._t("ve", self.nc.vector.tensor_scalar(_ap(out), _ap(a), _ap(s1), None, op0))
        return self._t("ve", self.nc.vector.tensor_scalar(_ap(out), _ap(a), _ap(s1), _ap(s2), op0, op1))

    def stt(self, out, in0, scalar, in1, op0, op1):
        return self._t("ve", self.nc.vector.scalar_tensor_tensor(_ap(out), _ap(in0), _ap(scalar), _ap(in1), op0, op1))

    def vcopy(self, out, in_):
        return self._t("ve", self.nc.vector.tensor_copy(_ap(out), _ap(in_)))

    def reduce(self, out, in_, op):
        return self._t("ve", self.nc.vector.tensor_reduce(_ap(out), _ap(in_), AX.X, op))

    def memset(self, ap, val, eng="ve"):
        return self._t(eng, self.eng[eng].memset(_ap(ap), val))

    def act(self, out, in_, func, bias=0.0, scale=1.0, accum_out=None):
        return self._t("ac", self.nc.scalar.activation(
            _ap(out), _ap(in_), func, bias=_ap(bias) if bias is not None else bias,
            scale=_ap(scale) if scale is not None else scale, accum_out=_ap(accum_out)))

    def acopy(self, out, in_):
        return self._t("ac", self.nc.scalar.copy(_ap(out), _ap(in_)))

    def pott(self, out, a, b, op):
        return self._t("po", self.nc.gpsimd.tensor_tensor(_ap(out), _ap(a), _ap(b), op))

    def postt(self, out, in0, scalar, in1, op0, op1):
        return self._t("po", self.nc.gpsimd.scalar_tensor_tensor(_ap(out), _ap(in0), _ap(scalar), _ap(in1), op0, op1))

    def dma(self, q, out, in_):
        inst = self.eng[q].dma_start(out=_ap(out), in_=_ap(in_))
        ds = "d" + q
        inst.then_inc(self.sem[ds], 16)
        self.cnt[ds] += 16
        if self.last.get(q) is not None:
            self.last[q] = None
        self.dirty.add(q)
        return inst

    def _flush(self, producers):
        for e in producers:
            if e in self.dirty and self.last.get(e) is not None:
                self.last[e].then_inc(self.sem[e], 1)
                self.cnt[e] += 1
                self.last[e] = None
            self.dirty.discard(e)

    def barrier(self, waiters=None, producers=None):
        producers = list(producers) if producers is not None else list(self.ENGS)
        waiters = list(waiters) if waiters is not None else list(self.ENGS)
        self._flush(producers)
        sems = []
        for p in producers:
            if p in ("sy", "po") and self.cnt["d" + p]:
                sems.append("d" + p)
            if self.cnt.get(p):
                sems.append(p)
        for w in waiters:
            for s in sems:
                self.eng[w].wait_ge(self.sem[s], self.cnt[s])
        return self

    def fence_out(self):
        self._flush(list(self.ENGS))
        for s in ("dsy", "dpo"):
            if self.cnt[s]:
                self.eng["sy"].wait_ge(self.sem[s], self.cnt[s])

    def wait_val(self, w, sem_name, value):
        self.eng[w].wait_ge(self.sem[sem_name], value)

    def mark(self, e):
        self._flush([e])
        return self.cnt[e]

    def wait_cnt(self, w, sem_name, value):
        if value > 0:
            self.eng[w].wait_ge(self.sem[sem_name], value)


def _ap(x):
    from concourse.bass_types import AP as _AP
    if x is None or isinstance(x, (_AP, int, float)):
        return x
    return x[:]


def rep_ap(ap, axis, n):
    sh = list(ap.shape)
    sh.insert(axis, n)
    return ap.unsqueeze(axis).broadcast_to(sh)


def emit_rsqrt(em, out, in_, sa, sb_, scale=1.0, eps=0.0, iters=4):
    em.ts(sb_, in_, scale, eps, ALU.mult, ALU.add)
    g = out
    em.ts(g.bitcast(I32), sb_.bitcast(I32), 1, None, ALU.arith_shift_right)
    em.ts(g.bitcast(I32), g.bitcast(I32), MAGIC, None, ALU.subtract)
    em.ts(g.bitcast(I32), g.bitcast(I32), -1, None, ALU.mult)
    for _ in range(iters):
        em.tt(sa, g, g, ALU.mult)
        em.tt(sa, sa, sb_, ALU.mult)
        em.ts(sa, sa, -0.5, 1.5, ALU.mult, ALU.add)
        em.tt(g, g, sa, ALU.mult)


def emit_ln(em, x_sb, ln_sb, cst, scr, ncols, b0=0, taps=None):
    tiles = []
    t0 = 0
    while t0 < ncols:
        tw = min(512, ncols - t0)
        if tw < 128 and tiles:
            t0 = ncols - 512
            tw = 512
            tiles = [t for t in tiles if t[0] < t0] + [(t0, tw)]
            break
        tiles.append((t0, tw))
        t0 += tw
    ps_s, ps_q, ps_b, ps_c = (em.bank(b0), em.bank(b0 + 1), em.bank(b0 + 2), em.bank(b0 + 3))
    sqf = scr["sqf"]
    r = [scr["row"][:, i, :] for i in range(4)]
    sc = scr["sc512"]
    for (t0, tw) in tiles:
        for k in range(KC):
            em.mm(ps_s[0:1, :tw], cst["ones_col"], x_sb[:, k, t0:t0 + tw],
                  start=(k == 0), stop=(k == KC - 1))
        for k in range(KC):
            em.act(sqf[:, k, :tw], x_sb[:, k, t0:t0 + tw], AF.Square)
        em.barrier(["pe"], ["ac"])
        for k in range(KC):
            em.mm(ps_q[0:1, :tw], cst["ones_col_b"], sqf[:, k, :tw],
                  start=(k == 0), stop=(k == KC - 1))
        em.barrier(["ve"], ["pe"])
        if taps is not None and t0 == ncols - 512:
            em.vcopy(scr["sc512"][0:1, :], ps_s[0:1, :tw])
            em.barrier(["sy"], ["ve"])
            em.dma("sy", taps[0][:, 0, :], scr["sc512"][0:1, :])
            em.vcopy(scr["sc512"][0:1, :], ps_q[0:1, :tw])
            em.barrier(["sy"], ["ve"])
            em.dma("sy", taps[0][:, 1, :], scr["sc512"][0:1, :])
            em.vcopy(scr["sc512"][:, :], sqf[:, 0, :tw])
            em.barrier(["sy"], ["ve"])
            em.dma("sy", taps[1], scr["sc512"][:, :])
            em.barrier(["ve"], ["sy"])
        em.ts(r[0][:, :tw], ps_s[0:1, :tw], 1.0 / D)
        em.ts(r[1][:, :tw], ps_q[0:1, :tw], 1.0 / D)
        em.tt(r[2][:, :tw], r[0][:, :tw], r[0][:, :tw], ALU.mult)
        em.tt(r[1][:, :tw], r[1][:, :tw], r[2][:, :tw], ALU.subtract)
        emit_rsqrt(em, r[2][:, :tw], r[1][:, :tw], r[3][:, :tw], r[1][:, :tw], eps=EPS)
        em.tt(r[3][:, :tw], r[0][:, :tw], r[2][:, :tw], ALU.mult)
        em.barrier(["pe"], ["ve"])
        em.mm(ps_b[:, :tw], cst["ones_row"], r[2][:, :tw])
        em.mm(ps_c[:, :tw], cst["ones_row"], r[3][:, :tw])
        em.barrier(["ve"], ["pe"])
        for k in range(KC):
            em.tt(sc[:, :tw], x_sb[:, k, t0:t0 + tw], ps_b[:, :tw], ALU.mult)
            em.tt(ln_sb[:, k, t0:t0 + tw], sc[:, :tw], ps_c[:, :tw], ALU.subtract)
        em.barrier()


def emit_proj(em, w_sb, ln_sb, mo, ko, t0, tw, out_cb, banks=(4, 5), kc_base=0):
    for m in range(mo):
        b = banks[m % len(banks)]
        for k in range(ko):
            em.mm(em.bank(b, tw), w_sb[:, (m * ko + k) * 128:(m * ko + k + 1) * 128],
                  ln_sb[:, kc_base + k, t0:t0 + tw],
                  start=(k == 0), stop=(k == ko - 1))
        out_cb(m, em.bank(b, tw), b)


def prep_lhsT(W, pad_to=None):
    K, M = W.shape
    if pad_to is not None and M < pad_to:
        W = np.concatenate([W, np.zeros((K, pad_to - M), W.dtype)], axis=1)
        M = pad_to
    Ko, Mo = K // 128, M // 128
    Wr = np.asarray(W, np.float32).reshape(Ko, 128, Mo, 128)
    return np.ascontiguousarray(Wr.transpose(2, 1, 0, 3).reshape(Mo, 128, Ko * 128))


def make_consts():
    i = np.arange(128)
    tril1 = (i[:, None] <= i[None, :]).astype(np.float32)
    trilstrict = (i[:, None] > i[None, :]).astype(np.float32)
    trilneg = np.where(i[:, None] <= i[None, :], 0.0, -1e30).astype(np.float32)
    ident = np.eye(128, dtype=np.float32)
    esel = np.zeros((64, 64 * 128), np.float32)
    for h in range(64):
        esel[h, h * 128:(h + 1) * 128] = -1.0
    eexp = np.zeros((64, 64 * 64), np.float32)
    for h in range(64):
        eexp[h, h * 64:(h + 1) * 64] = 1.0
    return {
        "c_tril1": tril1, "c_trilstrict": trilstrict, "c_trilneg": trilneg,
        "c_triu01_b": tril1, "c_ident": ident, "c_ident_b": ident,
        "c_tri16": (-tril1 / 16.0).astype(np.float32),
        "c_esel": esel, "c_eexp_b": eexp,
        "c_ones_col": np.ones((128, 1), np.float32),
        "c_ones_col_b": np.ones((128, 1), np.float32),
        "c_ones_row": np.ones((1, 128), np.float32),
    }


CONST_SPECS = {
    "c_tril1": ([128, 128], F32), "c_trilstrict": ([128, 128], F32),
    "c_trilneg": ([128, 128], F32), "c_triu01_b": ([128, 128], BF16),
    "c_ident": ([128, 128], F32), "c_ident_b": ([128, 128], BF16),
    "c_tri16": ([128, 128], F32), "c_esel": ([64, 8192], F32),
    "c_eexp_b": ([64, 4096], BF16),
    "c_ones_col": ([128, 1], F32), "c_ones_col_b": ([128, 1], BF16),
    "c_ones_row": ([1, 128], F32),
}


def declare_consts(em, nc, names=None):
    cst = {}
    stage = []
    sc = ExitStack()
    for name, (shape, dt) in CONST_SPECS.items():
        if names is not None and name not in names:
            continue
        d = nc.dram_tensor(name, shape, F32, kind="ExternalInput").ap()
        if dt == BF16:
            f = sc.enter_context(nc.sbuf_tensor(name + "_f", shape, F32, side="right"))
            t = em.sb(name + "_sb", shape, BF16)
            em.dma("sy", f[:], d)
            stage.append((t, f))
        else:
            t = em.sb(name + "_sb", shape, dt)
            em.dma("sy", t[:], d)
        cst[name[2:]] = t[:]
    if stage:
        em.barrier(["ve"], ["sy"])
        for t, f in stage:
            em.vcopy(t[:], f[:])
        em.barrier()
    sc.close()
    return cst


def shard_inputs(hs):
    outs = []
    for core in range(8):
        b, pos = core // 4, core % 4
        t0 = pos * T
        sl = hs[b, max(0, t0 - TPAD):t0 + T, :]
        x = np.asarray(sl, np.float32).T
        if x.shape[1] < T + TPAD:
            x = np.concatenate([np.zeros((D, T + TPAD - x.shape[1]), np.float32), x], axis=1)
        outs.append(np.ascontiguousarray(x))
    return outs


def load_x(em, nc, name, ncols, kind="ExternalInput"):
    d = nc.dram_tensor(name, [D, ncols], F32, kind=kind).ap()
    t = em.sb(name + "_sb", [128, KC, ncols], F32)
    for k in range(KC):
        em.dma("sy", t[:, k, :], d[k * 128:(k + 1) * 128, :])
    return d, t


def load_weight_slab(em, nc, name, mo, kcols, dt_in=F32):
    d = nc.dram_tensor(name, [mo, 128, kcols], dt_in, kind="ExternalInput").ap()
    t = em.sb(name + "_sb", [128, mo, kcols], BF16)
    for m in range(mo):
        em.dma("po", t[:, m, :], d[m])
    return t




SEQ_GROUPS = [[0, 1, 2, 3], [4, 5, 6, 7]]


def shard_inputs(hs):
    outs = []
    for core in range(8):
        b, pos = core // 4, core % 4
        t0 = pos * T
        sl = hs[b, max(0, t0 - TPAD):t0 + T, :]
        x = np.asarray(sl, np.float32).T
        if x.shape[1] < T + TPAD:
            x = np.concatenate([np.zeros((D, T + TPAD - x.shape[1]), np.float32), x], axis=1)
        outs.append(np.ascontiguousarray(x))
    return outs


def load_x(em, nc, name, ncols, sbuf=None, side=None):
    d = nc.dram_tensor(name, [D, ncols], F32, kind="ExternalInput").ap()
    t = sbuf if sbuf is not None else em.stack.enter_context(
        nc.sbuf_tensor(em.name(name + "_sb"), [128, KC, ncols], F32, side=side))
    for k in range(KC):
        em.dma("sy", t[:, k, :], d[k * 128:(k + 1) * 128, :])
    return d, t


CONSTS_A = ["c_tril1", "c_trilstrict", "c_trilneg", "c_ident", "c_ident_b",
            "c_eexp_b", "c_ones_col", "c_ones_col_b", "c_ones_row"]


def declare_ioA(nc, ext=True):
    ki = dict(kind="ExternalInput") if ext else dict(kind="ExternalInput")
    ko = dict(kind="ExternalOutput") if ext else dict(kind="Internal")
    TL = T + TPAD
    ios = {}
    ios["wA_d"] = nc.dram_tensor("wA", [NCH + 1, 128, D], F32, **ki).ap()
    ios["cw_d"] = nc.dram_tensor("cw", [128, NCH, 4], F32, **ki).ap()
    ios["cb_d"] = nc.dram_tensor("cb", [128, NCH], F32, **ki).ap()
    ios["arow_d"] = nc.dram_tensor("arow", [1, NH], F32, **ki).ap()
    ios["drow_d"] = nc.dram_tensor("drow", [1, NH], F32, **ki).ap()
    ios["dtb_d"] = nc.dram_tensor("dtb", [128, 1], F32, **ki).ap()
    ios["xT_d"] = nc.dram_tensor("xT", [D, TL], F32, **ki).ap()
    ios["ln0_o"] = nc.dram_tensor("ln0_o", [D, TL], BF16, **ko).ap()
    ios["y_o"] = nc.dram_tensor("y_o", [MCH, 128, DIN], BF16, **ko).ap()
    ios["S_o"] = nc.dram_tensor("S_o", [128, 4096], F32, **ko).ap()
    ios["offs_o"] = nc.dram_tensor("offs_o", [1, NH], F32, **ko).ap()
    ios["expg_o"] = nc.dram_tensor("expg_o", [MCH, 128, NH], F32, **ko).ap()
    ios["csave_o"] = nc.dram_tensor("csave_o", [128, MCH, 128], BF16, **ko).ap()
    if ext:
        ios["xbc_dbg"] = nc.dram_tensor("xbc_dbg", [128, NCH, TL], BF16, **ko).ap()
        ios["ssd_dbg"] = nc.dram_tensor("ssd_dbg", [3, 128, DIN], BF16, **ko).ap()
        ios["bt_dbg"] = nc.dram_tensor("bt_dbg", [128, 128], BF16, **ko).ap()
        ios["dtsp_dbg"] = nc.dram_tensor("dtsp_dbg", [128, T], F32, **ko).ap()
    ios["dacs_dram"] = nc.dram_tensor("dacs_scr", [64, 128], F32).ap()
    return ios


def build_launchA(stage="full"):
    nc = bass.Bass("TRN2", target_bir_lowering=False)
    stack = ExitStack()
    with stack:
        em = Em(nc, stack)
        cst = declare_consts(em, nc, names=CONSTS_A)
        ios = declare_ioA(nc, True)
        emit_stageA(em, nc, cst, ios, stage)
    return nc


def emit_stageA(em, nc, cst, ios, stage="full", gin=None):
    TL = T + TPAD
    wA_d = ios["wA_d"]; cw_d = ios["cw_d"]; cb_d = ios["cb_d"]
    arow_d = ios["arow_d"]; drow_d = ios["drow_d"]; dtb_d = ios["dtb_d"]
    ln0_o = ios["ln0_o"]; y_o = ios["y_o"]; S_o = ios["S_o"]
    offs_o = ios["offs_o"]; expg_o = ios["expg_o"]; csave_o = ios["csave_o"]
    xbc_dbg = ios.get("xbc_dbg"); ssd_dbg = ios.get("ssd_dbg")
    bt_dbg = ios.get("bt_dbg"); dtsp_dbg = ios.get("dtsp_dbg")
    dacs_dram = ios["dacs_dram"]
    if True:

        cw = em.sb("cw", [128, NCH, 4], F32)
        cb = em.sb("cb", [128, NCH], F32)
        dtb = em.sb("dtb", [128, 1], F32)
        a_b = em.sb("a_b", [128, NH], F32)
        d_b = em.sb("d_b", [128, NH], F32)
        dtsp = em.sb("dtsp", [128, T], F32)
        rowm = em.sb("rowm", [1, 2, NH], F32)
        em.dma("sy", cw[:, :, :], cw_d)
        em.dma("sy", cb[:, :], cb_d)
        em.dma("sy", dtb[:], dtb_d)
        em.dma("sy", rowm[:, 0, :], arow_d)

        sc_outer = ExitStack()
        ln0 = sc_outer.enter_context(
            nc.sbuf_tensor(em.name("ln0"), [128, KC, TL], BF16, side="right"))
        sc_inner = ExitStack()
        x_sb = sc_inner.enter_context(
            nc.sbuf_tensor(em.name("x_sb"), [128, KC, TL], F32, side="right"))
        scr = {
            "sqf": sc_inner.enter_context(nc.sbuf_tensor(em.name("sqf"), [128, KC, 512], BF16, side="right")),
            "row": sc_inner.enter_context(nc.sbuf_tensor(em.name("rowscr"), [1, 4, 512], F32, side="right")),
            "sc512": sc_inner.enter_context(nc.sbuf_tensor(em.name("sc512"), [128, 512], F32, side="right")),
        }
        xT_d = ios["xT_d"]
        for k in range(KC):
            em.dma("sy", x_sb[:, k, :], xT_d[k * 128:(k + 1) * 128, :])

        drow_sb = em.sb("drow_sb", [1, NH], F32)
        em.dma("sy", drow_sb[:], drow_d)
        em.barrier()
        em.mm(em.bank(0, NH), cst["ones_row"], rowm[:, 0, :])
        em.mm(em.bank(1, NH), cst["ones_row"], drow_sb[:])
        em.barrier(["ve"], ["pe"])
        em.vcopy(a_b[:, :], em.bank(0, NH))
        em.vcopy(d_b[:, :], em.bank(1, NH))
        em.memset(rowm[:, 1, :], 0.0)
        em.barrier()

        emit_ln(em, x_sb, ln0, cst, scr, TL)
        for k in range(KC):
            em.dma("sy", ln0_o[k * 128:(k + 1) * 128, :], ln0[:, k, :])
        em.barrier()
        sc_inner.close()
        if stage == "ln":
            em.fence_out()
            return nc

        xbcraw = em.sb("xbcraw", [128, NCH, TL], BF16)
        wg = em.sb("wg", [128, 2, D], BF16)
        S = em.sb("S", [128, 4096], F32)
        S_b = em.sb("S_bf", [128, 4096], BF16)
        x_t = em.sb("x_t", [128, DIN], BF16)
        dtx = em.sb("dtx", [128, DIN], BF16)
        ddtx = em.sb("ddtx", [128, DIN], BF16)
        csave_sb = em.sb("csave_sb", [128, MCH, 128], BF16)
        conv_acc = em.sb("conv_acc", [128, 512], F32)
        conv_acc2 = em.sb("conv_acc2", [128, 512], F32)
        em.memset(S[:, :], 0.0)
        em.memset(S_b[:, :], 0.0)

        MO = NCH + 1
        if stage.startswith("proj"):
            MO = int(stage[4:])
        em.dma("po", wg[:, 0, :], wA_d[0])
        nd0 = em.cnt["dpo"] - 16
        pe_done, ve_done, seq = {}, {}, 0
        for m in range(MO):
            if m + 1 < MO:
                em.wait_cnt("po", "pe", pe_done.get(m - 1, 0))
                em.dma("po", wg[:, (m + 1) % 2, :], wA_d[m + 1])
            em.wait_cnt("pe", "dpo", nd0 + 16 * (m + 1))
            wslab = wg[:, m % 2, :]
            if m < NCH:
                for t0 in (0, 512, 515):
                    bk = 4 + seq % 2
                    em.wait_cnt("pe", "ve", ve_done.get(seq - 2, 0))
                    for k in range(KC):
                        em.mm(em.bank(bk), wslab[:, k * 128:(k + 1) * 128],
                              ln0[:, k, t0:t0 + 512], start=(k == 0), stop=(k == KC - 1))
                    pm = em.mark("pe")
                    em.wait_cnt("ve", "pe", pm)
                    em.vcopy(xbcraw[:, m, t0:t0 + 512], em.bank(bk))
                    ve_done[seq] = em.mark("ve")
                    seq += 1
                pe_done[m] = pm
            else:
                for (ti, t0) in ((0, 3), (1, 515)):
                    bk = 4 + seq % 2
                    em.wait_cnt("pe", "ve", ve_done.get(seq - 2, 0))
                    em.barrier(["pe"], ["ac"])
                    for k in range(KC):
                        em.mm(em.bank(bk), wslab[:, k * 128:(k + 1) * 128],
                              ln0[:, k, t0:t0 + 512], start=(k == 0), stop=(k == KC - 1))
                    pm = em.mark("pe")
                    em.wait_cnt("ac", "pe", pm)
                    em.wait_cnt("ac", "ve", ve_done.get(seq - 1, 0))
                    em.act(conv_acc, em.bank(bk), AF.Relu, bias=dtb[:, 0:1])
                    em.act(conv_acc2, em.bank(bk), AF.Abs, bias=dtb[:, 0:1])
                    em.act(conv_acc2, conv_acc2, AF.Exp, scale=-1.0)
                    em.act(conv_acc2, conv_acc2, AF.Ln, bias=1.0)
                    em.barrier(["ve"], ["ac"])
                    em.tt(dtsp[:, ti * 512:(ti + 1) * 512], conv_acc, conv_acc2, ALU.add)
                    ve_done[seq] = em.mark("ve")
                    seq += 1
                pe_done[m] = pm
        em.barrier()
        sc_outer.close()
        if stage.startswith("proj"):
            em.dma("sy", xbc_dbg[:, 0:MO, :], xbcraw[:, 0:MO, :])
            em.fence_out()
            return nc

        for c in range(NCH):
            for half in range(2):
                o0 = half * 512
                acc = conv_acc if half == 0 else conv_acc2
                em.ts(acc[:, :], xbcraw[:, c, o0:o0 + 512], cw[:, c, 0:1], None, ALU.mult)
                for kk in range(1, 4):
                    em.stt(acc[:, :], xbcraw[:, c, o0 + kk:o0 + kk + 512], cw[:, c, kk:kk + 1],
                           acc[:, :], ALU.mult, ALU.add)
                em.barrier(["ac"], ["ve"])
                em.act(xbcraw[:, c, o0:o0 + 512], acc[:, :], AF.Silu, bias=cb[:, c:c + 1])
            em.barrier(["ve"], ["ac"])
        em.barrier()
        for c in range(MCH):
            em.vcopy(csave_sb[:, c, :], xbcraw[:, 33, c * 128:(c + 1) * 128])
        em.barrier(["sy"], ["ve"])
        em.dma("sy", csave_o, csave_sb[:, :, :])
        if stage == "conv":
            em.dma("sy", xbc_dbg, xbcraw[:, :, :])
            em.dma("sy", dtsp_dbg, dtsp[:, :])
            em.fence_out()
            return nc

        sc_ssd = ExitStack()

        def rsb(nm, shape, dt):
            return sc_ssd.enter_context(nc.sbuf_tensor(em.name(nm), shape, dt, side="right"))
        dA_t = rsb("dA_t", [128, NH], F32)
        dacs = rsb("dacs", [128, NH], F32)
        dacs_hm = rsb("dacs_hm", [64, 128], F32)
        dacs_row = rsb("dacs_row", [1, 8192], F32)
        q_hm = rsb("q_hm", [64, 128], BF16)
        dt_hm = rsb("dt_hm", [64, 128], BF16)
        ed2e_hm = rsb("ed2e_hm", [64, 128], BF16)
        expa_hm = rsb("expa_hm", [64, 128], BF16)
        cbt = rsb("cbt", [128, 128], BF16)
        b_t = rsb("b_tT", [128, 128], BF16)
        mt8 = rsb("mt8", [128, 8 * 128], BF16)
        et_b = rsb("et_b", [128, NH], F32)
        expEg = rsb("expEg", [128, 512], BF16)
        yg_s = rsb("yg_s", [128, 512], BF16)
        f32s = rsb("f32s", [128, 512], F32)
        f32b = rsb("f32b", [128, NH], F32)
        expg_sb = rsb("expg_sb", [128, NH], F32)
        offs = rowm[:, 1, :]

        nch_run = {"ssd1": 1, "ssd2": 2}.get(stage, MCH)
        for c in range(nch_run):
            ct = c * 128
            em.barrier()
            em.trans(em.bank(0, NH), dtsp[0:64, ct:ct + 128], cst["ident"][0:64, 0:64])
            em.barrier(["ve"], ["pe"])
            em.tt(dA_t[:, :], em.bank(0, NH), a_b[:, :], ALU.mult)
            em.barrier(["pe"], ["ve"])
            em.mm(em.bank(0, NH), cst["tril1"], dA_t[:, :])
            em.mm(em.bank(1, NH), cst["trilstrict"], dA_t[:, :])
            em.mm(em.bank(2, NH)[0:1, :], cst["ones_col"], dA_t[:, :])
            em.barrier(["ve"], ["pe"])
            em.vcopy(dacs[:, :], em.bank(0, NH))
            em.barrier(["pe"], ["ve"])
            em.mm(em.bank(3, NH), cst["ones_row"], offs)
            em.barrier(["ve"], ["pe"])
            em.tt(f32b[:, :], dacs[:, :], em.bank(3, NH), ALU.add)
            em.tt(offs, offs, em.bank(2, NH)[0:1, :], ALU.add)
            em.barrier(["ac"], ["ve"])
            em.act(expg_sb[:, :], f32b[:, :], AF.Exp)
            em.act(f32b[:, :], em.bank(1, NH), AF.Exp)
            em.act(et_b[:, :], dacs[:, :], AF.Exp)
            em.barrier(["sy", "pe", "ve"], ["ac"])
            em.dma("sy", expg_o[c], expg_sb[:, :])
            em.trans(em.pb[3][0:64, 0:128], dacs[:, :], cst["ident"])
            em.barrier(["ve"], ["pe"])
            em.vcopy(dacs_hm[:, :], em.pb[3][0:64, 0:128])
            em.barrier(["pe"], ["ve"])
            em.trans(em.pb[3][0:64, 0:128], f32b[:, :], cst["ident"])
            em.barrier(["ve"], ["pe"])
            em.vcopy(ed2e_hm[:, :], em.pb[3][0:64, 0:128])
            em.barrier(["pe"], ["ve"])
            em.trans(em.pb[3][0:64, 0:128], et_b[:, :], cst["ident"])
            em.barrier(["ve"], ["pe"])
            em.vcopy(expa_hm[:, :], em.pb[3][0:64, 0:128])
            em.vcopy(dt_hm[:, :], dtsp[0:64, ct:ct + 128])
            em.tt(q_hm[:, :], dt_hm[:, :], ed2e_hm[:, :], ALU.mult)
            em.barrier(["sy"], ["ve", "pe"])
            em.dma("sy", dacs_dram, dacs_hm[:, :])
            em.barrier(["sy"], ["sy"])
            em.dma("sy", dacs_row[:], dacs_dram.rearrange("a b -> (a b)")[None, :])
            for q in range(8):
                for j in range(4):
                    em.trans(em.bank_b(4 + (q % 2))[:, j * 128:(j + 1) * 128],
                             xbcraw[:, q * 4 + j, ct:ct + 128], cst["ident_b"])
                em.barrier(["ve"], ["pe"])
                em.vcopy(x_t[:, q * 512:(q + 1) * 512], em.bank_b(4 + (q % 2)))
                if q % 2 == 1:
                    em.barrier(["pe"], ["ve"])
            em.trans(em.bank_b(6)[:, 0:128], xbcraw[:, 32, ct:ct + 128], cst["ident_b"])
            em.barrier(["ve"], ["pe"])
            em.vcopy(b_t[:, :], em.bank_b(6)[:, 0:128])
            em.barrier(["pe"], ["ve"])
            for fb in range(8):
                em.mm(em.bank(4 + (fb % 2)), dt_hm[:, :], cst["eexp_b"][:, fb * 512:(fb + 1) * 512])
                em.barrier(["ve"], ["pe"])
                em.tt(dtx[:, fb * 512:(fb + 1) * 512], x_t[:, fb * 512:(fb + 1) * 512],
                      em.bank(4 + (fb % 2)), ALU.mult)
                if fb % 2 == 1:
                    em.barrier(["pe"], ["ve"])
            for fb in range(8):
                em.mm(em.bank(4 + (fb % 2)), q_hm[:, :], cst["eexp_b"][:, fb * 512:(fb + 1) * 512])
                em.barrier(["ve"], ["pe"])
                em.tt(ddtx[:, fb * 512:(fb + 1) * 512], x_t[:, fb * 512:(fb + 1) * 512],
                      em.bank(4 + (fb % 2)), ALU.mult)
                if fb % 2 == 1:
                    em.barrier(["pe"], ["ve"])
            em.mm(em.bank(6, 128), xbcraw[:, 32, ct:ct + 128], xbcraw[:, 33, ct:ct + 128])
            em.barrier(["ve"], ["pe"])
            em.vcopy(cbt[:, :], em.bank(6, 128))
            em.barrier(["pe"], ["sy", "ve"])
            for hg in range(8):
                for gi in range(2):
                    g = hg * 2 + gi
                    bk = 4 + gi
                    em.mm(em.bank(bk), cst["ones_row"], dacs_row[:, g * 512:(g + 1) * 512])
                    em.barrier(["ve"], ["pe"])
                    v3 = lambda ap: ap.rearrange("p (r i) -> p r i", r=4)
                    em.tt(v3(f32s[:, :]), v3(em.bank(bk)),
                          rep_ap(dacs[:, g * 4:(g + 1) * 4], 2, 128), ALU.subtract)
                    em.tt(v3(f32s[:, :]), v3(f32s[:, :]),
                          rep_ap(cst["trilneg"], 1, 4), ALU.min)
                    em.barrier(["ac"], ["ve"])
                    em.act(f32s[:, :], f32s[:, :], AF.Exp)
                    em.barrier(["ve"], ["ac"])
                    em.tt(v3(mt8[:, gi * 512:(gi + 1) * 512]), v3(f32s[:, :]),
                          rep_ap(cbt[:, :], 1, 4), ALU.mult)
                    em.barrier(["pe"], ["ve"])
                em.mm(em.bank(6), xbcraw[:, 33, ct:ct + 128],
                      S_b[:, hg * 512:(hg + 1) * 512], start=True, stop=False)
                em.mm(em.bank(7), expa_hm[:, :], cst["eexp_b"][:, hg * 512:(hg + 1) * 512])
                em.barrier(["ve"], ["pe"])
                em.vcopy(expEg[:, :], em.bank(7))
                em.tt(em.bank(6), em.bank(6), expEg[:, :], ALU.mult)
                em.barrier(["pe"], ["ve"])
                for hh in range(8):
                    em.mm(em.bank(6)[:, hh * 64:(hh + 1) * 64],
                          mt8[:, hh * 128:(hh + 1) * 128],
                          dtx[:, (hg * 8 + hh) * 64:(hg * 8 + hh + 1) * 64],
                          start=False, stop=(hh == 7))
                em.barrier(["ve"], ["pe"])
                yv = yg_s[:, :].rearrange("p (h d) -> p h d", h=8)
                em.tt(yv, x_t[:, hg * 512:(hg + 1) * 512].rearrange("p (h d) -> p h d", h=8),
                      rep_ap(d_b[:, hg * 8:(hg + 1) * 8], 2, 64), ALU.mult)
                em.tt(yg_s[:, :], yg_s[:, :], em.bank(6), ALU.add)
                em.barrier(["sy", "pe"], ["ve"])
                em.dma("sy", y_o[c][:, hg * 512:(hg + 1) * 512], yg_s[:, :])
                em.barrier(["ve"], ["sy"])
            em.barrier(["ac"], ["pe", "ve"])
            em.act(f32b[0:1, 0:NH], em.bank(2, NH)[0:1, :], AF.Exp)
            em.barrier(["pe"], ["ac"])
            em.mm(em.bank(7, NH), cst["ones_row"], f32b[0:1, 0:NH])
            em.barrier(["ve"], ["pe"])
            em.vcopy(et_b[:, :], em.bank(7, NH))
            em.barrier(["pe"], ["ve"])
            for fb in range(8):
                bk = 4 + (fb % 2)
                em.mm(em.bank(bk), b_t[:, :], ddtx[:, fb * 512:(fb + 1) * 512])
                em.barrier(["ve"], ["pe"])
                Sv = S[:, fb * 512:(fb + 1) * 512].rearrange("p (h d) -> p h d", h=8)
                em.tt(Sv, Sv, rep_ap(et_b[:, fb * 8:(fb + 1) * 8], 2, 64), ALU.mult)
                em.tt(S[:, fb * 512:(fb + 1) * 512], S[:, fb * 512:(fb + 1) * 512],
                      em.bank(bk), ALU.add)
                if fb % 2 == 1:
                    em.barrier(["pe"], ["ve"])
            em.vcopy(S_b[:, :], S[:, :])
            if DBG and stage == "ssd1" and c == 0:
                em.barrier(["sy"], ["ve"])
                em.dma("sy", ssd_dbg[0], x_t[:, :])
                em.dma("sy", ssd_dbg[1], dtx[:, :])
                em.dma("sy", ssd_dbg[2], ddtx[:, :])
                em.dma("sy", bt_dbg, b_t[:, :])
            em.barrier(["pe", "ac"], ["ve"])

        em.barrier(["sy"], ["ve", "ac", "pe"])
        if gin is None:
            em.dma("sy", S_o, S[:, :])
            em.dma("sy", offs_o, offs)
            em.fence_out()
        else:
            em.barrier(["pe"], ["ve"])
            em.mm(em.bank(0, NH), cst["ones_row"], offs)
            em.barrier(["ve"], ["pe"])
            em.vcopy(et_b[:, :], em.bank(0, NH))
            em.barrier(["sy"], ["ve"])
            em.dma("sy", gin[:, 0:4096], S[:, :])
            em.dma("sy", gin[:, 4096:4096 + NH], et_b[:, :])
        sc_ssd.close()


def prep_launchA(inputs):
    W_in = np.asarray(inputs["L0_attn_W_in"], np.float32)
    wxbc = prep_lhsT(W_in[:, DIN:DIN + 4352])
    wdt = prep_lhsT(W_in[:, DIN + 4352:], pad_to=128)
    wA = np.concatenate([wxbc, wdt], axis=0)
    convw = np.asarray(inputs["L0_attn_conv_w"], np.float32)
    cw = np.ascontiguousarray(convw.reshape(NCH, 128, 4).transpose(1, 0, 2))
    cb = np.ascontiguousarray(
        np.asarray(inputs["L0_attn_conv_b"], np.float32).reshape(NCH, 128).T)
    arow = np.asarray(inputs["L0_attn_A"], np.float32).reshape(1, NH)
    drow = np.asarray(inputs["L0_attn_D"], np.float32).reshape(1, NH)
    dtb = np.zeros((128, 1), np.float32)
    dtb[0:64, 0] = np.asarray(inputs["L0_attn_dt_bias"], np.float32)
    base = {"wA": wA, "cw": cw, "cb": cb, "arow": arow, "drow": drow, "dtb": dtb}
    base.update(make_consts())
    xs = shard_inputs(np.asarray(inputs["L0_attn_hidden_states"], np.float32))
    return [dict(base, xT=xs[k]) for k in range(8)]




CONSTS_B = ["c_ident", "c_ident_b", "c_eexp_b", "c_ones_col", "c_ones_col_b",
            "c_ones_row"]


def build_launchB():
    nc = bass.Bass("TRN2", target_bir_lowering=False)
    stack = ExitStack()
    with stack:
        em = Em(nc, stack)
        cst = declare_consts(em, nc, names=CONSTS_B)
        ios = {}
        TL = T + TPAD
        ios["xT_d"] = nc.dram_tensor("xT", [D, TL], F32, kind="ExternalInput").ap()
        ios["ln0_o"] = nc.dram_tensor("ln0_i", [D, TL], BF16, kind="ExternalInput").ap()
        ios["wz_d"] = nc.dram_tensor("wz", [32, 128, D], F32, kind="ExternalInput").ap()
        ios["wout_d"] = nc.dram_tensor("wout", [KC, 128, DIN], F32, kind="ExternalInput").ap()
        ios["nw_d"] = nc.dram_tensor("nw", [128, 32], F32, kind="ExternalInput").ap()
        ios["y_o"] = nc.dram_tensor("y_i", [MCH, 128, DIN], BF16, kind="ExternalInput").ap()
        ios["S_all_d"] = nc.dram_tensor("S_all", [4, 128, 4096], F32, kind="ExternalInput").ap()
        ios["offs_d"] = nc.dram_tensor("offs_all", [1, 4 * NH], F32, kind="ExternalInput").ap()
        ios["pmb_d"] = nc.dram_tensor("posmask", [1, 4], F32, kind="ExternalInput").ap()
        ios["csave_o"] = nc.dram_tensor("csave_i", [128, MCH, 128], BF16, kind="ExternalInput").ap()
        ios["expg_o"] = nc.dram_tensor("expg_i", [MCH, 128, NH], F32, kind="ExternalInput").ap()
        ios["x1_o"] = nc.dram_tensor("x1_o", [D, T], F32, kind="ExternalOutput").ap()
        emit_stageB(em, nc, cst, ios)
        em.fence_out()
    return nc


def emit_stageB(em, nc, cst, ios, gout=None, wait_cc=None):
    TL = T + TPAD
    xT_d = ios["xT_d"]; ln0_d = ios["ln0_o"]; wz_d = ios["wz_d"]
    wout_d = ios["wout_d"]; nw_d = ios["nw_d"]; y_d = ios["y_o"]
    pm_d = ios["pmb_d"]; csave_d = ios["csave_o"]; expg_d = ios["expg_o"]
    x1_o = ios["x1_o"]
    if True:

        S_in = em.sb("S_in", [128, 4096], F32)
        S_in_b = em.sb("S_in_b", [128, 4096], BF16)
        csave = em.sb("csaveB", [128, MCH, 128], BF16)
        expg = em.sb("expgB", [128, MCH, NH], F32)
        nw = em.sb("nwB", [128, 32], F32)
        rows = em.sb("rowsB", [1, 8, NH], F32)
        pm = em.sb("pmB", [1, 4], F32)
        z_s = em.sb("z_s", [128, 32, 512], BF16)
        yn_s = em.sb("yn_s", [128, 32, 512], BF16)
        y_sb = em.sb("y_sbB", [128, DIN], BF16)
        y_fm = em.sb("y_fmB", [128, 32, 128], BF16)
        eg_hm = em.sb("eg_hm", [64, 128], BF16)
        expE = em.sb("expEB", [128, 512], BF16)
        tscr = em.sb("tscrB", [128, 512], F32)
        wgran = em.sb("wgranB", [128, 2, D], BF16)
        wgout = em.sb("wgoutB", [128, 2, DIN], BF16)
        xstg = em.sb("xstgB", [128, 2, 512], F32)
        ostg = em.sb("ostgB", [128, 2, 512], F32)
        rrow = em.sb("rrowB", [1, 4, 128], F32)
        wb = em.sb("wbB", [128, NH], F32)
        ln0 = em.sb("ln0B", [128, KC, TL], BF16)

        for k in range(KC):
            em.dma("sy", ln0[:, k, :], ln0_d[k * 128:(k + 1) * 128, :])
        em.dma("sy", csave[:, :, :], csave_d)
        for c in range(MCH):
            em.dma("sy", expg[:, c, :], expg_d[c])
        em.dma("sy", nw[:, :], nw_d)
        if gout is None:
            em.dma("sy", rows[:, 0:4, :], ios["offs_d"].rearrange("a (j h) -> a j h", j=4))
        em.dma("sy", pm[:], pm_d)
        em.barrier()

        pe_done = {}
        z_done_tile = {}

        def zproj(ti):
            t0 = ti * 512
            em.barrier()
            em.dma("po", wgran[:, 0, :], wz_d[0])
            nd0 = em.cnt["dpo"] - 16
            ac_done = {}
            for m in range(32):
                if m + 1 < 32:
                    em.wait_cnt("po", "pe", pe_done.get(m - 1, 0))
                    em.dma("po", wgran[:, (m + 1) % 2, :], wz_d[m + 1])
                em.wait_cnt("pe", "dpo", nd0 + 16 * (m + 1))
                em.wait_cnt("pe", "ac", ac_done.get(m - 2, 0))
                for k in range(KC):
                    em.mm(em.bank(4 + m % 2), wgran[:, m % 2, k * 128:(k + 1) * 128],
                          ln0[:, k, TPAD + t0:TPAD + t0 + 512],
                          start=(k == 0), stop=(k == KC - 1))
                pe_done[m] = em.mark("pe")
                em.wait_cnt("ac", "pe", pe_done[m])
                em.act(z_s[:, m, :], em.bank(4 + m % 2), AF.Silu)
                ac_done[m] = em.mark("ac")
            em.barrier()

        zproj(0)
        if wait_cc is not None:
            wait_cc()
        if DBG and gout is not None and ios.get("gdbg") is not None:
            em.dma("sy", ios["gdbg"], gout)
        if gout is not None:
            for j in range(4):
                em.dma("sy", rows[:, j, :], gout[j * 128:j * 128 + 1, 4096:4096 + NH])
            em.barrier(["ve", "ac"], ["sy"])
        em.memset(rows[:, 7, :], 0.0)
        for j in (2, 1, 0):
            em.stt(rows[:, 4 + j, :], rows[:, j + 1, :], pm[:, j + 1:j + 2],
                   rows[:, 4 + j + 1, :], ALU.mult, ALU.add)
        em.barrier(["ac"], ["ve"])
        for j in range(4):
            em.act(rows[:, 4 + j, :], rows[:, 4 + j, :], AF.Exp)
        em.barrier(["ve"], ["ac"])
        for j in range(4):
            em.ts(rows[:, 4 + j, :], rows[:, 4 + j, :], pm[:, j:j + 1], None, ALU.mult)
        em.memset(S_in[:, :], 0.0)
        stage_S = yn_s[:, :, :].rearrange("p a b -> p (a b)").bitcast(F32)[:, 0:4096]
        for j in range(4):
            em.barrier(["sy"], ["ve"])
            if gout is None:
                em.dma("sy", stage_S[:, :], ios["S_all_d"][j])
            else:
                em.dma("sy", stage_S[:, :], gout[j * 128:(j + 1) * 128, 0:4096])
            em.barrier(["pe"], ["ve"])
            em.mm(em.bank(0, NH), cst["ones_row"], rows[:, 4 + j, :])
            em.barrier(["ve"], ["pe", "sy"])
            em.vcopy(wb[:, :], em.bank(0, NH))
            Sv = stage_S[:, :].rearrange("p (h d) -> p h d", h=NH)
            em.tt(Sv, Sv, rep_ap(wb[:, :], 2, HD), ALU.mult)
            em.tt(S_in[:, :], S_in[:, :], stage_S[:, :], ALU.add)
        em.vcopy(S_in_b[:, :], S_in[:, :])
        em.barrier()

        for ti in range(2):
            t0 = ti * 512
            if ti == 1:
                zproj(1)
            for ci in range(4):
                c = ti * 4 + ci
                em.barrier()
                em.dma("sy", y_sb[:, :], y_d[c])
                em.trans(em.pb[2][0:64, 0:128], expg[:, c, :], cst["ident"])
                em.barrier(["ve"], ["pe", "sy"])
                em.vcopy(eg_hm[:, :], em.pb[2][0:64, 0:128])
                em.barrier(["pe"], ["ve"])
                for fb in range(8):
                    em.mm(em.bank(4), csave[:, c, :], S_in_b[:, fb * 512:(fb + 1) * 512])
                    em.mm(em.bank(5), eg_hm[:, :], cst["eexp_b"][:, fb * 512:(fb + 1) * 512])
                    em.barrier(["ve"], ["pe"])
                    em.vcopy(expE[:, :], em.bank(5))
                    em.tt(tscr[:, :], em.bank(4), expE[:, :], ALU.mult)
                    em.tt(y_sb[:, fb * 512:(fb + 1) * 512], y_sb[:, fb * 512:(fb + 1) * 512],
                          tscr[:, :], ALU.add)
                    em.barrier(["pe"], ["ve"])
                for q in range(8):
                    for jj in range(4):
                        em.trans(em.bank_b(6 + (q % 2))[:, jj * 128:(jj + 1) * 128],
                                 y_sb[:, (q * 4 + jj) * 128:(q * 4 + jj + 1) * 128],
                                 cst["ident_b"])
                    em.barrier(["ve"], ["pe"])
                    em.vcopy(y_fm[:, q * 4:(q + 1) * 4, :].rearrange("p a b -> p (a b)"),
                             em.bank_b(6 + (q % 2)))
                    em.barrier(["pe"], ["ve"])
                em.tt(y_fm[:, :, :], y_fm[:, :, :],
                      z_s[:, :, ci * 128:(ci + 1) * 128], ALU.mult)
                sqf = y_sb[:, 0:4096].rearrange("p (a b) -> p a b", a=32)
                em.tt(sqf, y_fm[:, :, :], y_fm[:, :, :], ALU.mult)
                em.barrier(["pe"], ["ve"])
                for fc in range(32):
                    em.mm(em.bank(2, 128)[0:1, :], cst["ones_col_b"], sqf[:, fc, :],
                          start=(fc == 0), stop=(fc == 31))
                em.barrier(["ve"], ["pe"])
                emit_rsqrt(em, rrow[:, 1, :], em.bank(2, 128)[0:1, :],
                           rrow[:, 2, :], rrow[:, 3, :], scale=1.0 / DIN, eps=EPS)
                em.barrier(["pe"], ["ve"])
                em.mm(em.bank(3, 128), cst["ones_row"], rrow[:, 1, :])
                em.barrier(["ve"], ["pe"])
                em.tt(y_fm[:, :, :], y_fm[:, :, :], rep_ap(nw[:, :], 2, 128), ALU.mult)
                em.tt(yn_s[:, :, ci * 128:(ci + 1) * 128], y_fm[:, :, :],
                      rep_ap(em.bank(3, 128), 1, 32), ALU.mult)
                em.barrier(["pe"], ["ve"])
            em.barrier()
            em.dma("po", wgout[:, 0, :], wout_d[0])
            nd0 = em.cnt["dpo"] - 16
            ve_done, sy_done = {}, {}
            for m in range(KC):
                if m + 1 < KC:
                    em.wait_cnt("po", "pe", pe_done.get(100 + m - 1, 0))
                    em.dma("po", wgout[:, (m + 1) % 2, :], wout_d[m + 1])
                em.wait_cnt("sy", "ve", ve_done.get(m - 2, 0))
                em.dma("sy", xstg[:, m % 2, :], xT_d[m * 128:(m + 1) * 128, TPAD + t0:TPAD + t0 + 512])
                sy_done[m] = em.cnt["dsy"]
                em.wait_cnt("pe", "dpo", nd0 + 16 * (m + 1))
                em.wait_cnt("pe", "ve", ve_done.get(m - 2, 0))
                for k in range(32):
                    em.mm(em.bank(4 + m % 2), wgout[:, m % 2, k * 128:(k + 1) * 128],
                          yn_s[:, k, :], start=(k == 0), stop=(k == 31))
                pe_done[100 + m] = em.mark("pe")
                em.wait_cnt("ve", "pe", pe_done[100 + m])
                em.wait_cnt("ve", "dsy", sy_done[m])
                em.tt(ostg[:, m % 2, :], em.bank(4 + m % 2), xstg[:, m % 2, :], ALU.add)
                ve_done[m] = em.mark("ve")
                em.wait_cnt("sy", "ve", ve_done[m])
                em.dma("sy", x1_o[m * 128:(m + 1) * 128, t0:t0 + 512], ostg[:, m % 2, :])
            em.barrier()


def prep_launchB(inputs, resA):
    W_in = np.asarray(inputs["L0_attn_W_in"], np.float32)
    wz = prep_lhsT(W_in[:, 0:DIN])
    wout = prep_lhsT(np.asarray(inputs["L0_attn_W_out"], np.float32))
    nw = np.ascontiguousarray(
        np.asarray(inputs["L0_attn_norm_w"], np.float32).reshape(32, 128).T)
    base = {"wz": wz, "wout": wout, "nw": nw}
    base.update(make_consts())
    xs = shard_inputs(np.asarray(inputs["L0_attn_hidden_states"], np.float32))
    maps = []
    for k in range(8):
        g = k // 4
        grp = SEQ_GROUPS[g]
        S_all = np.stack([resA[j]["S_o"] for j in grp])
        offs_all = np.concatenate([resA[j]["offs_o"][0] for j in grp]).reshape(1, 4 * NH)
        pmv = np.zeros((1, 4), np.float32)
        pmv[0, :k % 4] = 1.0
        maps.append(dict(base, xT=xs[k], ln0_i=resA[k]["ln0_o"], y_i=resA[k]["y_o"],
                         S_all=S_all, offs_all=offs_all, posmask=pmv,
                         csave_i=resA[k]["csave_o"], expg_i=resA[k]["expg_o"]))
    return maps




def declare_io_ffn(nc, kind, pref=""):
    FHC = FH // 128
    ios = {}
    ios["xT_d"] = nc.dram_tensor(pref + "xT", [D, T], F32, kind="ExternalInput").ap()
    ios["xo_d"] = nc.dram_tensor(pref + "xo", [D, T], F32, kind="ExternalOutput").ap()
    ios["wdn_d"] = nc.dram_tensor(pref + "wdn", [KC, 128, FH], F32, kind="ExternalInput").ap()
    if kind == "sign":
        ios["wup_d"] = nc.dram_tensor(pref + "wup", [FHC, 128, D], F32, kind="ExternalInput").ap()
        ios["upsc_d"] = nc.dram_tensor(pref + "upsc", [128, FHC], F32, kind="ExternalInput").ap()
        ios["dnsc_d"] = nc.dram_tensor(pref + "dnsc", [128, KC], F32, kind="ExternalInput").ap()
        ios["osc_d"] = nc.dram_tensor(pref + "osc", [1, 1], F32, kind="ExternalInput").ap()
    else:
        ios["wga_d"] = nc.dram_tensor(pref + "wga", [FHC, 128, D], F32, kind="ExternalInput").ap()
        ios["wup_d"] = nc.dram_tensor(pref + "wup", [FHC, 128, D], F32, kind="ExternalInput").ap()
    return ios


def build_ffn(kind):
    nc = bass.Bass("TRN2", target_bir_lowering=False)
    stack = ExitStack()
    with stack:
        em = Em(nc, stack)
        cst = declare_consts(em, nc, names=[
            "c_ones_col", "c_ones_col_b", "c_ones_row"])
        ios = declare_io_ffn(nc, kind)
        emit_ffn(em, nc, cst, ios, kind)
        em.fence_out()
    return nc


def emit_ffn(em, nc, cst, ios, kind):
    FHC = FH // 128
    xT_d = ios["xT_d"]; xo_d = ios["xo_d"]; wdn_d = ios["wdn_d"]
    wup_d = ios["wup_d"]
    if kind == "sign":
        upsc_d = ios["upsc_d"]; dnsc_d = ios["dnsc_d"]; osc_d = ios["osc_d"]
    else:
        wga_d = ios["wga_d"]
    TW = 512
    ln1 = em.sb("ln1", [128, KC, T], BF16)
    if kind == "sign":
        upsc = em.sb("upscC", [128, FHC], F32)
        dsos = em.sb("dsosC", [128, KC], F32)
        osc = em.sb("oscC", [1, 1], F32)
        em.dma("sy", upsc[:, :], upsc_d)
        em.dma("sy", dsos[:, :], dnsc_d)
        em.dma("sy", osc[:], osc_d)

    sc_x = ExitStack()
    x_sb = sc_x.enter_context(nc.sbuf_tensor(em.name("x_sbC"), [128, KC, T], F32, side="right"))
    scr = {
        "sqf": sc_x.enter_context(nc.sbuf_tensor(em.name("sqfC"), [128, KC, 512], BF16, side="right")),
        "row": sc_x.enter_context(nc.sbuf_tensor(em.name("rowC"), [1, 4, 512], F32, side="right")),
        "sc512": sc_x.enter_context(nc.sbuf_tensor(em.name("sc512C"), [128, 512], F32, side="right")),
    }
    for k in range(KC):
        em.dma("sy", x_sb[:, k, :], xT_d[k * 128:(k + 1) * 128, :])
    em.barrier()
    if kind == "sign":
        em.mm(em.bank(0, 1), cst["ones_row"], osc[:])
        em.barrier(["ve"], ["pe", "sy"])
        em.tt(dsos[:, :], dsos[:, :], em.bank(0, 1).broadcast_to([128, KC]), ALU.mult)
        em.barrier(["pe"], ["ve"])
    emit_ln(em, x_sb, ln1, cst, scr, T)
    sc_x.close()
    h_s = em.sb("h_s", [128, FHC, TW], BF16)
    NPSL = 3 if kind == "sign" else 2
    wgran = em.sb("wgranC", [128, NPSL, 2, D], BF16)
    wgran2 = em.sb("wgran2C", [128, NPSL, 2, D], BF16) if kind == "swiglu" else wgran
    wdng3 = em.sb("wdngC", [128, 3, FH], BF16)
    xstg = em.sb("xstgC", [128, 2, TW], F32)
    ostg = xstg
    sgs = em.sb("sgsC", [128, 2, TW], BF16)

    nper = 32 if kind == "swiglu" else 16
    NP = FHC // 2
    for ti in range(T // TW):
        t0 = ti * TW
        em.barrier()
        for pp in range(min(NPSL, NP)):
            em.dma("po", wgran[:, pp, :, :],
                   wup_d[2 * pp:2 * pp + 2].rearrange("s p j -> p s j"))
            if kind == "swiglu":
                em.dma("po", wgran2[:, pp, :, :],
                       wga_d[2 * pp:2 * pp + 2].rearrange("s p j -> p s j"))
        nd0 = em.cnt["dpo"] - NPSL * (nper // 16) * 16
        pe_done, ve_done, ac_done = {}, {}, {}
        def up_prefetch(p):
            if p + NPSL < NP:
                em.wait_cnt("po", "pe", pe_done[2 * p + 1])
                em.dma("po", wgran[:, p % NPSL, :, :],
                       wup_d[2 * (p + NPSL):2 * (p + NPSL) + 2].rearrange("s p j -> p s j"))
                if kind == "swiglu":
                    em.dma("po", wgran2[:, p % NPSL, :, :],
                           wga_d[2 * (p + NPSL):2 * (p + NPSL) + 2].rearrange("s p j -> p s j"))

        for m in range(FHC):
            p, sl = m // 2, (m // 2) % NPSL
            dready = nd0 + (nper // 16) * 16 * (p + 1)
            wsl = wgran[:, p % NPSL, m % 2, :]
            wsl2 = wgran2[:, p % NPSL, m % 2, :]
            bk = 4 + m % 2
            if kind == "sign":
                em.wait_cnt("ac", "dpo", dready)
                em.wait_cnt("ac", "pe", pe_done.get(m - 4, 0))
                em.act(wsl, wsl, AF.Sign)
                ac_done[m] = em.mark("ac")
                em.wait_cnt("pe", "ac", ac_done[m])
                em.wait_cnt("pe", "ve", ve_done.get(m - 2, 0))
                for k in range(KC):
                    em.mm(em.bank(bk, TW), wsl[:, k * 128:(k + 1) * 128],
                          ln1[:, k, t0:t0 + TW], start=(k == 0), stop=(k == KC - 1))
                pe_done[m] = em.mark("pe")
                if m % 2 == 1:
                    up_prefetch(p)
                em.wait_cnt("ve", "pe", pe_done[m])
                em.ts(h_s[:, m, :], em.bank(bk, TW), upsc[:, m:m + 1], 0.0,
                      ALU.mult, ALU.max)
                ve_done[m] = em.mark("ve")
            else:
                bk2 = 4 + (m % 2) * 2
                em.wait_cnt("pe", "dpo", dready)
                em.wait_cnt("pe", "ve", ve_done.get(m - 2, 0))
                em.wait_cnt("pe", "ac", ac_done.get(m - 2, 0))
                for k in range(KC):
                    em.mm(em.bank(bk2, TW), wsl2[:, k * 128:(k + 1) * 128],
                          ln1[:, k, t0:t0 + TW], start=(k == 0), stop=(k == KC - 1))
                gm = em.mark("pe")
                for k in range(KC):
                    em.mm(em.bank(bk2 + 1, TW), wsl[:, k * 128:(k + 1) * 128],
                          ln1[:, k, t0:t0 + TW], start=(k == 0), stop=(k == KC - 1))
                pe_done[m] = em.mark("pe")
                if m % 2 == 1:
                    up_prefetch(p)
                em.wait_cnt("ac", "pe", gm)
                em.wait_cnt("ac", "ve", ve_done.get(m - 2, 0))
                em.act(sgs[:, m % 2, :], em.bank(bk2, TW), AF.Silu)
                ac_done[m] = em.mark("ac")
                em.wait_cnt("ve", "pe", pe_done[m])
                em.wait_cnt("ve", "ac", ac_done[m])
                em.tt(h_s[:, m, :], sgs[:, m % 2, :], em.bank(bk2 + 1, TW), ALU.mult)
                ve_done[m] = em.mark("ve")
        em.barrier()
        for sl in range(3):
            em.dma("po", wdng3[:, sl, :], wdn_d[sl])
        nd0 = em.cnt["dpo"] - 3 * 16
        pe_done, ve_done, ac_done, sy_done = {}, {}, {}, {}
        for m in range(KC):
            em.wait_cnt("sy", "ve", ve_done.get(m - 2, 0))
            em.dma("sy", xstg[:, m % 2, :], xT_d[m * 128:(m + 1) * 128, t0:t0 + TW])
            sy_done[m] = em.cnt["dsy"]
            if kind == "sign":
                em.wait_cnt("ac", "dpo", nd0 + 16 * (m + 1))
                em.wait_cnt("ac", "pe", pe_done.get(m - 3, 0))
                em.act(wdng3[:, m % 3, :], wdng3[:, m % 3, :], AF.Sign)
                ac_done[m] = em.mark("ac")
                em.wait_cnt("pe", "ac", ac_done[m])
            else:
                em.wait_cnt("pe", "dpo", nd0 + 16 * (m + 1))
            em.wait_cnt("pe", "ve", ve_done.get(m - 2, 0))
            bk = 4 + m % 2
            for k in range(FHC):
                em.mm(em.bank(bk, TW), wdng3[:, m % 3, k * 128:(k + 1) * 128],
                      h_s[:, k, :], start=(k == 0), stop=(k == FHC - 1))
            pe_done[m] = em.mark("pe")
            if m + 3 < KC:
                em.wait_cnt("po", "pe", pe_done[m])
                em.dma("po", wdng3[:, (m + 3) % 3, :], wdn_d[m + 3])
            em.wait_cnt("ve", "pe", pe_done[m])
            em.wait_cnt("ve", "dsy", sy_done[m])
            if kind == "sign":
                em.stt(ostg[:, m % 2, :], em.bank(bk, TW), dsos[:, m:m + 1],
                       xstg[:, m % 2, :], ALU.mult, ALU.add)
            else:
                em.tt(ostg[:, m % 2, :], em.bank(bk, TW), xstg[:, m % 2, :], ALU.add)
            ve_done[m] = em.mark("ve")
            em.wait_cnt("sy", "ve", ve_done[m])
            em.dma("sy", xo_d[m * 128:(m + 1) * 128, t0:t0 + TW], ostg[:, m % 2, :])
        em.barrier()


def prep_ffn0(inputs, x1s):
    wup = prep_lhsT(np.asarray(inputs["L0_ffn_up_weight"], np.float32).T)
    wdn = prep_lhsT(np.asarray(inputs["L0_ffn_down_weight"], np.float32).T)
    upsc = np.ascontiguousarray(
        np.asarray(inputs["L0_ffn_up_scale"], np.float32).reshape(FH // 128, 128).T)
    dnsc = np.ascontiguousarray(
        np.asarray(inputs["L0_ffn_down_scale"], np.float32).reshape(KC, 128).T)
    osc = np.asarray(inputs["L0_ffn_output_scale"], np.float32).reshape(1, 1)
    base = {"wup": wup, "wdn": wdn, "upsc": upsc, "dnsc": dnsc, "osc": osc}
    base.update(make_consts())
    return [dict(base, xT=x1s[k]) for k in range(8)]


def prep_ffn1(inputs, x3s):
    wga = prep_lhsT(np.asarray(inputs["L1_ffn_w_gate"], np.float32).T)
    wup = prep_lhsT(np.asarray(inputs["L1_ffn_w_up"], np.float32).T)
    wdn = prep_lhsT(np.asarray(inputs["L1_ffn_w_down"], np.float32).T)
    base = {"wga": wga, "wup": wup, "wdn": wdn}
    base.update(make_consts())
    return [dict(base, xT=x3s[k]) for k in range(8)]




CONSTS_D = ["c_ident", "c_ident_b", "c_tri16", "c_triu01_b",
            "c_ones_col", "c_ones_col_b", "c_ones_row"]


def declare_ioD(nc, ext=True):
    ki = dict(kind="ExternalInput")
    ko = dict(kind="ExternalOutput") if ext else dict(kind="Internal")
    ios = {}
    if ext:
        ios["xT_d"] = nc.dram_tensor("xT", [D, T], F32, **ki).ap()
    ios["wq_d"] = nc.dram_tensor("wq", [KC, 128, D], F32, **ki).ap()
    ios["wk_d"] = nc.dram_tensor("wk", [KC, 128, D], F32, **ki).ap()
    ios["wv_d"] = nc.dram_tensor("wv", [8, 128, D], F32, **ki).ap()
    ios["wgk1_d"] = nc.dram_tensor("wgk1", [1, 128, D], F32, **ki).ap()
    ios["wgk2_d"] = nc.dram_tensor("wgk2", [16, 128, 128], F32, **ki).ap()
    ios["bg_d"] = nc.dram_tensor("bg", [128, 16], F32, **ki).ap()
    ios["ln2_o"] = nc.dram_tensor("ln2_o", [D, T], BF16, **ko).ap()
    ios["o_o"] = nc.dram_tensor("o_o", [GCH, 128, 1024], BF16, **ko).ap()
    ios["qe_om"] = nc.dram_tensor("qe_o", [128, 16, T], BF16, **ko).ap()
    ios["Sg_o"] = nc.dram_tensor("Sg_o", [128, 4096], F32, **ko).ap()
    ios["L_o"] = nc.dram_tensor("L_o", [128, 16], F32, **ko).ap()
    if ext:
        ios["bts_dbg"] = nc.dram_tensor("bts_dbg", [128, 2048], F32, **ko).ap()
        ios["spt_dbg"] = nc.dram_tensor("spt_dbg", [128, 2048], F32, **ko).ap()
        ios["low_dbg"] = nc.dram_tensor("low_dbg", [128, 512], BF16, **ko).ap()
        ios["gkp_dbg"] = nc.dram_tensor("gkp_dbg", [128, 128], F32, **ko).ap()
        ios["qec_dbg"] = nc.dram_tensor("qec_dbg", [128, 16, 128], BF16, **ko).ap()
    return ios


def build_launchD():
    nc = bass.Bass("TRN2", target_bir_lowering=False)
    stack = ExitStack()
    with stack:
        em = Em(nc, stack)
        cst = declare_consts(em, nc, names=CONSTS_D)
        ios = declare_ioD(nc, True)
        emit_stageD(em, nc, cst, ios)
        em.fence_out()
    return nc


def emit_stageD(em, nc, cst, ios, gin=None):
    xT_d = ios["xT_d"]; wq_d = ios["wq_d"]; wk_d = ios["wk_d"]
    wv_d = ios["wv_d"]; wgk1_d = ios["wgk1_d"]; wgk2_d = ios["wgk2_d"]
    bg_d = ios["bg_d"]; ln2_o = ios["ln2_o"]; o_o = ios["o_o"]
    qe_om = ios["qe_om"]; Sg_o = ios["Sg_o"]; L_o = ios["L_o"]
    bts_dbg = ios.get("bts_dbg"); spt_dbg = ios.get("spt_dbg")
    low_dbg = ios.get("low_dbg"); gkp_dbg = ios.get("gkp_dbg")
    qec_dbg = ios.get("qec_dbg")
    if True:

        ln2 = em.sb("ln2", [128, KC, T], BF16)
        Lcum = em.sb("LcumD", [128, 16], F32)
        bl = em.sb("blD", [128, 16], F32)
        ebl = em.sb("eblD", [128, 16], F32)
        eL = em.sb("eLD", [128, 16], F32)
        wgk2 = em.sb("wgk2D", [128, 16, 128], BF16)
        bg = em.sb("bgD", [128, 16], F32)
        nbg = em.sb("nbgD", [128, 16], F32)

        sc_x = ExitStack()
        x_sb = sc_x.enter_context(nc.sbuf_tensor(em.name("x_sbD"), [128, KC, T], F32, side="right"))
        scr = {
            "sqf": sc_x.enter_context(nc.sbuf_tensor(em.name("sqfD"), [128, KC, 512], BF16, side="right")),
            "row": sc_x.enter_context(nc.sbuf_tensor(em.name("rowD"), [1, 4, 512], F32, side="right")),
            "sc512": sc_x.enter_context(nc.sbuf_tensor(em.name("sc512D"), [128, 512], F32, side="right")),
        }
        for k in range(KC):
            em.dma("sy", x_sb[:, k, :], xT_d[k * 128:(k + 1) * 128, :])
        em.dma("sy", bg[:, :], bg_d)
        em.barrier()
        for mq in range(16):
            em.dma("po", wgk2[:, mq, :], wgk2_d[mq])
        em.ts(nbg[:, :], bg[:, :], -1.0, None, ALU.mult)
        em.memset(Lcum[:, :], 0.0)
        em.barrier()
        emit_ln(em, x_sb, ln2, cst, scr, T)
        sc_x.close()
        for k in range(KC):
            em.dma("sy", ln2_o[k * 128:(k + 1) * 128, :], ln2[:, k, :])
        qf = em.sb("qf", [128, 16, 512], BF16)
        kf = em.sb("kf", [128, 16, 512], BF16)
        vf = em.sb("vf", [128, 8, 512], BF16)
        qe_all = em.sb("qe_all", [128, 16, T], BF16)
        o_sb = em.sb("o_sbD", [128, GCH, 1024], BF16)
        S = em.sb("SgD", [128, 4096], F32)
        S_b = em.sb("SgD_b", [128, 4096], BF16)
        low_b = em.sb("lowbD", [128, 512], BF16)
        sp_t = em.sb("sptD", [128, 2048], F32)
        b_ts = em.sb("btsD", [128, 2048], F32)
        expb_c = em.sb("expbD", [128, 16, 128], BF16)
        kd_c = em.sb("kdD", [128, 16, 128], BF16)
        qe_c = em.sb("qeD", [128, 16, 128], BF16)
        k2_tm = em.sb("k2tmD", [128, 16, 128], BF16)
        v_tm = em.sb("vtmD", [128, 8, 128], BF16)
        at_m = em.sb("atmD", [128, 4, 128], BF16)
        f32s = em.sb("f32sD", [128, 512], F32)
        sq_q = em.sb("sqqD", [128, 128], F32)
        wgran = em.sb("wgranD", [128, 2, D], BF16)
        em.memset(S[:, :], 0.0)
        em.memset(S_b[:, :], 0.0)

        pe_done = {}
        for ti in range(2):
            t0 = ti * 512
            for (tag, wd, nmo, dest) in (("q", wq_d, 16, qf), ("k", wk_d, 16, kf),
                                         ("v", wv_d, 8, vf)):
                em.barrier(["po"], ["pe", "ve"])
                em.dma("po", wgran[:, 0, :], wd[0])
                nd0 = em.cnt["dpo"] - 16
                ve_done = {}
                for m in range(nmo):
                    if m + 1 < nmo:
                        em.wait_cnt("po", "pe", pe_done.get(m - 1, 0))
                        em.dma("po", wgran[:, (m + 1) % 2, :], wd[m + 1])
                    em.wait_cnt("pe", "dpo", nd0 + 16 * (m + 1))
                    em.wait_cnt("pe", "ve", ve_done.get(m - 2, 0))
                    for k in range(KC):
                        em.mm(em.bank(4 + m % 2), wgran[:, m % 2, k * 128:(k + 1) * 128],
                              ln2[:, k, t0:t0 + 512], start=(k == 0), stop=(k == KC - 1))
                    pe_done[m] = em.mark("pe")
                    em.wait_cnt("ve", "pe", pe_done[m])
                    em.vcopy(dest[:, m, :], em.bank(4 + m % 2))
                    ve_done[m] = em.mark("ve")
            em.barrier(["po"], ["pe"])
            em.dma("po", wgran[:, 0, :], wgk1_d[0])
            em.wait_val("pe", "dpo", em.cnt["dpo"])
            em.barrier(["pe"], ["ve"])
            for k in range(KC):
                em.mm(em.bank(6), wgran[:, 0, k * 128:(k + 1) * 128],
                      ln2[:, k, t0:t0 + 512], start=(k == 0), stop=(k == KC - 1))
            em.barrier(["ve"], ["pe"])
            em.vcopy(low_b[:, :], em.bank(6))
            em.barrier(["pe"], ["ve"])

            for tc in range(4):
                c = ti * 4 + tc
                em.barrier()
                for f in range(16):
                    em.mm(em.bank(0, 128), wgk2[:, f, :], low_b[:, tc * 128:(tc + 1) * 128])
                    if DBG and c == 0 and f == 0 and gkp_dbg is not None:
                        em.barrier(["ve"], ["pe"])
                        em.vcopy(f32s[:, 0:128], em.bank(0, 128))
                        em.barrier(["sy"], ["ve"])
                        em.dma("sy", gkp_dbg, f32s[:, 0:128])
                        em.barrier(["ve", "ac"], ["sy"])
                    em.barrier(["ac"], ["pe"])
                    em.act(f32s[:, 0:128], em.bank(0, 128), AF.Relu, scale=-1.0, bias=nbg[:, f:f + 1])
                    em.act(sq_q[:, :], em.bank(0, 128), AF.Abs, bias=bg[:, f:f + 1])
                    em.act(sq_q[:, :], sq_q[:, :], AF.Exp, scale=-1.0)
                    em.act(sq_q[:, :], sq_q[:, :], AF.Ln, bias=1.0)
                    em.barrier(["ve"], ["ac"])
                    em.tt(f32s[:, 0:128], f32s[:, 0:128], sq_q[:, :], ALU.add)
                    em.barrier(["pe"], ["ve"])
                    em.trans(em.bank(1, 128), f32s[:, 0:128], cst["ident"])
                    em.barrier(["ve"], ["pe"])
                    em.vcopy(sp_t[:, f * 128:(f + 1) * 128], em.bank(1, 128))
                    em.barrier(["pe", "ac"], ["ve"])
                for s4 in range(4):
                    em.mm(em.bank(2), cst["tri16"], sp_t[:, s4 * 512:(s4 + 1) * 512])
                    em.barrier(["ve"], ["pe"])
                    em.vcopy(b_ts[:, s4 * 512:(s4 + 1) * 512], em.bank(2))
                    em.barrier(["pe"], ["ve"])
                for f in range(16):
                    em.trans(em.bank(3, 128), b_ts[:, f * 128:(f + 1) * 128], cst["ident"])
                    em.barrier(["ve", "ac"], ["pe"])
                    em.acopy(bl[:, f:f + 1], em.bank(3, 128)[:, 127:128])
                    em.act(expb_c[:, f, :], em.bank(3, 128), AF.Exp)
                    em.act(sq_q[:, :].bitcast(BF16)[:, 0:128], em.bank(3, 128), AF.Exp, scale=-1.0)
                    em.barrier(["ve"], ["ac"])
                    em.tt(kd_c[:, f, :], kf[:, f, tc * 128:(tc + 1) * 128],
                          sq_q[:, :].bitcast(BF16)[:, 0:128], ALU.mult)
                    em.stt(qe_c[:, f, :], qf[:, f, tc * 128:(tc + 1) * 128],
                           float(1.0 / np.sqrt(512.0)), expb_c[:, f, :], ALU.mult, ALU.mult)
                    em.barrier(["pe", "ac"], ["ve"])
                if DBG and c == 0 and bts_dbg is not None:
                    em.barrier(["sy"], ["ve", "ac"])
                    em.dma("sy", bts_dbg, b_ts[:, :])
                    em.dma("sy", spt_dbg, sp_t[:, :])
                    em.dma("sy", low_dbg, low_b[:, :])
                em.barrier(["ac"], ["ve", "pe", "ac"])
                em.act(ebl[:, :], bl[:, :], AF.Exp)
                em.act(eL[:, :], Lcum[:, :], AF.Exp)
                em.barrier(["ve"], ["ac"])
                em.tt(k2_tm[:, :, :], kd_c[:, :, :], rep_ap(ebl[:, :], 2, 128), ALU.mult)
                em.tt(qe_all[:, :, c * 128:(c + 1) * 128].rearrange("p a b -> p a b"),
                      qe_c[:, :, :], rep_ap(eL[:, :], 2, 128), ALU.mult)
                em.tt(Lcum[:, :], Lcum[:, :], bl[:, :], ALU.add)
                if DBG and c == 0 and qec_dbg is not None:
                    em.barrier(["sy"], ["ve"])
                    em.dma("sy", qec_dbg, qe_c[:, :, :])
                em.barrier(["pe"], ["ve"])
                for f in range(16):
                    em.trans(em.bank_b(0 + (f % 2))[:, 0:128], k2_tm[:, f, :], cst["ident_b"])
                    em.barrier(["ve"], ["pe"])
                    em.vcopy(k2_tm[:, f, :], em.bank_b(0 + (f % 2))[:, 0:128])
                    em.barrier(["pe"], ["ve"])
                for vc in range(8):
                    em.trans(em.bank_b(0 + (vc % 2))[:, 0:128],
                             vf[:, vc, tc * 128:(tc + 1) * 128], cst["ident_b"])
                    em.barrier(["ve"], ["pe"])
                    em.vcopy(v_tm[:, vc, :], em.bank_b(0 + (vc % 2))[:, 0:128])
                    em.barrier(["pe"], ["ve"])
                for h in range(GH):
                    for dkc in range(4):
                        em.mm(em.bank(2)[:, h * 128:(h + 1) * 128],
                              kd_c[:, h * 4 + dkc, :], qe_c[:, h * 4 + dkc, :],
                              start=(dkc == 0), stop=(dkc == 3))
                em.barrier(["ve"], ["pe"])
                em.ts(f32s[:, :], em.bank(2), 1e30, -1e30, ALU.min, ALU.max)
                em.tt(at_m[:, :, :], f32s[:, :].rearrange("p (a b) -> p a b", a=4),
                      rep_ap(cst["triu01_b"], 1, 4), ALU.mult)
                em.barrier(["pe"], ["ve"])
                for h in range(GH):
                    bk = 4 + h // 2
                    off = (h % 2) * 256
                    for dkc in range(4):
                        em.mm(em.bank(bk)[:, off:off + 256],
                              qe_c[:, h * 4 + dkc, :],
                              S_b[:, (h * 4 + dkc) * 256:(h * 4 + dkc + 1) * 256],
                              start=(dkc == 0), stop=False)
                    em.mm(em.bank(bk)[:, off:off + 256], at_m[:, h, :],
                          v_tm[:, 2 * h:2 * h + 2, :], start=False, stop=True)
                em.barrier(["ve"], ["pe"])
                em.vcopy(o_sb[:, c, 0:512], em.bank(4))
                em.vcopy(o_sb[:, c, 512:1024], em.bank(5))
                em.barrier(["pe"], ["ve"])
                for h in range(GH):
                    for dkc in range(4):
                        i2 = (h * 4 + dkc)
                        em.mm(em.bank(6)[:, (i2 % 2) * 256:(i2 % 2) * 256 + 256],
                              k2_tm[:, i2, :], v_tm[:, 2 * h:2 * h + 2, :])
                        em.barrier(["ve"], ["pe"])
                        em.ts(f32s[:, 0:256], S[:, i2 * 256:(i2 + 1) * 256],
                              ebl[:, i2:i2 + 1], None, ALU.mult)
                        em.tt(S[:, i2 * 256:(i2 + 1) * 256], f32s[:, 0:256],
                              em.bank(6)[:, (i2 % 2) * 256:(i2 % 2) * 256 + 256], ALU.add)
                        em.barrier(["pe"], ["ve"])
                em.vcopy(S_b[:, :], S[:, :])
                em.barrier(["pe", "ac", "sy"], ["ve"])

        em.barrier(["sy"], ["ve", "ac", "pe"])
        for c in range(GCH):
            em.dma("sy", o_o[c], o_sb[:, c, :])
        em.dma("sy", qe_om, qe_all[:, :, :])
        if gin is None:
            em.dma("sy", Sg_o, S[:, :])
            em.dma("sy", L_o, Lcum[:, :])
        else:
            em.dma("sy", gin[:, 0:4096], S[:, :])
            em.dma("sy", gin[:, 4096:4096 + 16], Lcum[:, :])


def prep_launchD(inputs, x2s):
    wq = prep_lhsT(np.asarray(inputs["L1_attn_Wq"], np.float32))
    wk = prep_lhsT(np.asarray(inputs["L1_attn_Wk"], np.float32))
    wv = prep_lhsT(np.asarray(inputs["L1_attn_Wv"], np.float32))
    wgk1 = prep_lhsT(np.asarray(inputs["L1_attn_Wgk1"], np.float32), pad_to=128)
    Wgk2 = np.asarray(inputs["L1_attn_Wgk2"], np.float32)
    wgk2 = np.zeros((16, 128, 128), np.float32)
    wgk2[:, 0:16, :] = Wgk2.reshape(16, 16, 128).transpose(1, 0, 2)
    bg = np.ascontiguousarray(
        np.asarray(inputs["L1_attn_bgk2"], np.float32).reshape(16, 128).T)
    base = {"wq": wq, "wk": wk, "wv": wv, "wgk1": wgk1, "wgk2": wgk2, "bg": bg}
    base.update(make_consts())
    return [dict(base, xT=x2s[k]) for k in range(8)]


CONSTS_E = ["c_ident", "c_ident_b", "c_ones_col", "c_ones_col_b", "c_ones_row"]


def build_launchE():
    nc = bass.Bass("TRN2", target_bir_lowering=False)
    stack = ExitStack()
    with stack:
        em = Em(nc, stack)
        cst = declare_consts(em, nc, names=CONSTS_E)
        ios = {}
        ios["xT_d"] = nc.dram_tensor("xT", [D, T], F32, kind="ExternalInput").ap()
        ios["ln2_o"] = nc.dram_tensor("ln2_i", [D, T], BF16, kind="ExternalInput").ap()
        ios["o_o"] = nc.dram_tensor("o_i", [GCH, 128, 1024], BF16, kind="ExternalInput").ap()
        ios["qe_om"] = nc.dram_tensor("qe_i", [128, 16, T], BF16, kind="ExternalInput").ap()
        ios["Sall_d"] = nc.dram_tensor("Sg_all", [4, 128, 4096], F32, kind="ExternalInput").ap()
        ios["Lall_d"] = nc.dram_tensor("L_all", [4, 128, 16], F32, kind="ExternalInput").ap()
        ios["pme_d"] = nc.dram_tensor("posmask", [128, 4], F32, kind="ExternalInput").ap()
        ios["wg_d"] = nc.dram_tensor("wg", [8, 128, D], F32, kind="ExternalInput").ap()
        ios["wn_d"] = nc.dram_tensor("wn", [128, 2], F32, kind="ExternalInput").ap()
        ios["wo_d"] = nc.dram_tensor("wo", [KC, 128, 1024], F32, kind="ExternalInput").ap()
        ios["x3_o"] = nc.dram_tensor("x3_o", [D, T], F32, kind="ExternalOutput").ap()
        if DBG:
            ios["sin_dbg"] = nc.dram_tensor("sin_dbg", [128, 4096], F32, kind="ExternalOutput").ap()
            ios["ob_dbg"] = nc.dram_tensor("ob_dbg", [128, 1024], BF16, kind="ExternalOutput").ap()
            ios["ons_dbg"] = nc.dram_tensor("ons_dbg", [128, 8, 512], BF16, kind="ExternalOutput").ap()
            ios["ob0_dbg"] = nc.dram_tensor("ob0_dbg", [128, 1024], BF16, kind="ExternalOutput").ap()
            ios["ms_dbg"] = nc.dram_tensor("ms_dbg", [128, 4, 64], F32, kind="ExternalOutput").ap()
        emit_stageE(em, nc, cst, ios)
        em.fence_out()
    return nc


def emit_stageE(em, nc, cst, ios, gout=None, wait_cc=None):
    xT_d = ios["xT_d"]; ln2_d = ios["ln2_o"]; o_d = ios["o_o"]
    qe_d = ios["qe_om"]; pm_d = ios["pme_d"]; wg_d = ios["wg_d"]
    wn_d = ios["wn_d"]; wo_d = ios["wo_d"]; x3_o = ios["x3_o"]
    sin_dbg = ios.get("sin_dbg"); ob_dbg = ios.get("ob_dbg")
    ons_dbg = ios.get("ons_dbg"); ob0_dbg = ios.get("ob0_dbg")
    ms_dbg = ios.get("ms_dbg")
    if True:

        ln2 = em.sb("ln2E", [128, KC, T], BF16)
        o_sb = em.sb("o_sbE", [128, GCH, 1024], BF16)
        qe_all = em.sb("qe_allE", [128, 16, T], BF16)
        S_in = em.sb("S_inE", [128, 4096], F32)
        S_in_b = em.sb("S_inE_b", [128, 4096], BF16)
        Lst = em.sb("LstE", [128, 4, 16], F32)
        aft = em.sb("aftE", [128, 5, 16], F32)
        pm = em.sb("pmE", [128, 4], F32)
        wn = em.sb("wnE", [128, 2], F32)
        sg_s = em.sb("sg_sE", [128, 2, 8, 512], BF16)
        on_s = em.sb("on_sE", [128, 8, 512], BF16)
        ob = em.sb("obE", [128, 1024], BF16)
        ofm = em.sb("ofmE", [128, 8, 128], BF16)
        ms = em.sb("msE", [128, 4, 64], F32)
        stage_S = em.sb("stageSE", [128, 4096], F32)
        wb16 = em.sb("wb16E", [128, 16], F32)
        wgran = em.sb("wgranE", [128, 2, D], BF16)
        wogr = em.sb("wogrE", [128, 2, 1024], BF16)
        xstg = em.sb("xstgE", [128, 2, 512], F32)
        ostg = em.sb("ostgE", [128, 2, 512], F32)

        for k in range(KC):
            em.dma("sy", ln2[:, k, :], ln2_d[k * 128:(k + 1) * 128, :])
        for c in range(GCH):
            em.dma("sy", o_sb[:, c, :], o_d[c])
        em.dma("sy", qe_all[:, :, :], qe_d)
        if gout is None:
            for j in range(4):
                em.dma("sy", Lst[:, j, :], ios["Lall_d"][j])
        em.dma("sy", pm[:, :], pm_d)
        em.dma("sy", wn[:, :], wn_d)
        pe_done = {}

        def gproj(ti):
            t0 = ti * 512
            em.barrier()
            em.dma("po", wgran[:, 0, :], wg_d[0])
            nd0 = em.cnt["dpo"] - 16
            ac_done = {}
            for m in range(8):
                if m + 1 < 8:
                    em.wait_cnt("po", "pe", pe_done.get(m - 1, 0))
                    em.dma("po", wgran[:, (m + 1) % 2, :], wg_d[m + 1])
                em.wait_cnt("pe", "dpo", nd0 + 16 * (m + 1))
                em.wait_cnt("pe", "ac", ac_done.get(m - 2, 0))
                for k in range(KC):
                    em.mm(em.bank(4 + m % 2), wgran[:, m % 2, k * 128:(k + 1) * 128],
                          ln2[:, k, t0:t0 + 512], start=(k == 0), stop=(k == KC - 1))
                pe_done[m] = em.mark("pe")
                em.wait_cnt("ac", "pe", pe_done[m])
                em.act(sg_s[:, ti, m, :], em.bank(4 + m % 2), AF.Silu)
                ac_done[m] = em.mark("ac")
            em.barrier()

        gproj(0)
        gproj(1)
        if wait_cc is not None:
            wait_cc()
        if gout is not None:
            for j in range(4):
                em.dma("sy", Lst[:, j, :], gout[j * 128:(j + 1) * 128, 4096:4096 + 16])
        em.barrier()
        em.memset(aft[:, 3, :], 0.0)
        for j in (2, 1, 0):
            em.stt(aft[:, j, :], Lst[:, j + 1, :], pm[:, j + 1:j + 2],
                   aft[:, j + 1, :], ALU.mult, ALU.add)
        em.barrier(["ac"], ["ve"])
        for j in range(4):
            em.act(aft[:, j, :], aft[:, j, :], AF.Exp)
        em.barrier(["ve"], ["ac"])
        em.memset(S_in[:, :], 0.0)
        for j in range(4):
            em.ts(wb16[:, :], aft[:, j, :], pm[:, j:j + 1], None, ALU.mult)
            em.barrier(["sy"], ["ve"])
            if gout is None:
                em.dma("sy", stage_S[:, :], ios["Sall_d"][j])
            else:
                em.dma("sy", stage_S[:, :], gout[j * 128:(j + 1) * 128, 0:4096])
            em.barrier(["ve"], ["sy"])
            Sv = stage_S[:, :].rearrange("p (a b) -> p a b", a=16)
            em.tt(Sv, Sv, rep_ap(wb16[:, :], 2, 256), ALU.mult)
            em.tt(S_in[:, :], S_in[:, :], stage_S[:, :], ALU.add)
        em.vcopy(S_in_b[:, :], S_in[:, :])
        if DBG and sin_dbg is not None:
            em.barrier(["sy"], ["ve"])
            em.dma("sy", sin_dbg, S_in[:, :])
        em.barrier()

        for ti in range(2):
            t0 = ti * 512
            for tc in range(4):
                c = ti * 4 + tc
                em.barrier()
                for h in range(GH):
                    bk = 4 + h // 2
                    off = (h % 2) * 256
                    for dkc in range(4):
                        em.mm(em.bank(bk)[:, off:off + 256],
                              qe_all[:, h * 4 + dkc, c * 128:(c + 1) * 128],
                              S_in_b[:, (h * 4 + dkc) * 256:(h * 4 + dkc + 1) * 256],
                              start=(dkc == 0), stop=(dkc == 3))
                em.barrier(["ve"], ["pe"])
                em.tt(ob[:, 0:512], o_sb[:, c, 0:512], em.bank(4), ALU.add)
                em.tt(ob[:, 512:1024], o_sb[:, c, 512:1024], em.bank(5), ALU.add)
                if DBG and c == 0 and ob0_dbg is not None:
                    em.barrier(["sy"], ["ve"])
                    em.dma("sy", ob0_dbg, ob[:, :])
                    em.barrier(["ve"], ["sy"])
                obv = ob[:, :].rearrange("p (h v) -> p h v", h=4)
                sqv = o_sb[:, c, :].rearrange("p (h v) -> p h v", h=4)
                em.tt(sqv, obv, obv, ALU.mult)
                em.memset(ms[:, 0, :], 1.0)
                for hh in range(GH):
                    em.reduce(ms[:, 0, hh:hh + 1], sqv[:, hh, :], ALU.add)
                em.barrier(["ve"], ["ve"])
                emit_rsqrt(em, ms[:, 1, :], ms[:, 0, :], ms[:, 2, :], ms[:, 3, :],
                           scale=1.0 / GDV, eps=EPS)
                em.tt(obv, obv, rep_ap(ms[:, 1, 0:4], 2, GDV), ALU.mult)
                em.barrier(["pe"], ["ve"])
                for vc in range(8):
                    em.trans(em.bank_b(0 + (vc % 2))[:, 0:128], ob[:, vc * 128:(vc + 1) * 128],
                             cst["ident_b"])
                    em.barrier(["ve"], ["pe"])
                    em.vcopy(ofm[:, vc, :], em.bank_b(0 + (vc % 2))[:, 0:128])
                    em.barrier(["pe"], ["ve"])
                if DBG and c == 0 and ob_dbg is not None:
                    em.barrier(["sy"], ["ve"])
                    em.dma("sy", ob_dbg, ob[:, :])
                    em.dma("sy", ms_dbg, ms[:, :, :])
                for vc in range(8):
                    em.stt(on_s[:, vc, tc * 128:(tc + 1) * 128], ofm[:, vc, :],
                           wn[:, vc % 2:vc % 2 + 1], sg_s[:, ti, vc, tc * 128:(tc + 1) * 128],
                           ALU.mult, ALU.mult)
                em.barrier(["pe", "sy"], ["ve"])
            if DBG and ti == 0 and ons_dbg is not None:
                em.barrier(["sy"], ["ve"])
                em.dma("sy", ons_dbg, on_s[:, :, :])
            em.barrier()
            em.dma("po", wogr[:, 0, :], wo_d[0])
            nd0 = em.cnt["dpo"] - 16
            ve_done, sy_done = {}, {}
            for m in range(KC):
                if m + 1 < KC:
                    em.wait_cnt("po", "pe", pe_done.get(300 + m - 1, 0))
                    em.dma("po", wogr[:, (m + 1) % 2, :], wo_d[m + 1])
                em.wait_cnt("sy", "ve", ve_done.get(m - 2, 0))
                em.dma("sy", xstg[:, m % 2, :], xT_d[m * 128:(m + 1) * 128, t0:t0 + 512])
                sy_done[m] = em.cnt["dsy"]
                em.wait_cnt("pe", "dpo", nd0 + 16 * (m + 1))
                em.wait_cnt("pe", "ve", ve_done.get(m - 2, 0))
                for k in range(8):
                    em.mm(em.bank(4 + m % 2), wogr[:, m % 2, k * 128:(k + 1) * 128],
                          on_s[:, k, :], start=(k == 0), stop=(k == 7))
                pe_done[300 + m] = em.mark("pe")
                em.wait_cnt("ve", "pe", pe_done[300 + m])
                em.wait_cnt("ve", "dsy", sy_done[m])
                em.tt(ostg[:, m % 2, :], em.bank(4 + m % 2), xstg[:, m % 2, :], ALU.add)
                ve_done[m] = em.mark("ve")
                em.wait_cnt("sy", "ve", ve_done[m])
                em.dma("sy", x3_o[m * 128:(m + 1) * 128, t0:t0 + 512], ostg[:, m % 2, :])
            em.barrier()


def prep_launchE(inputs, x2s, resD):
    wg = prep_lhsT(np.asarray(inputs["L1_attn_Wg"], np.float32))
    wo = prep_lhsT(np.asarray(inputs["L1_attn_Wo"], np.float32))
    wn = np.ascontiguousarray(
        np.asarray(inputs["L1_attn_Wnorm"], np.float32).reshape(2, 128).T)
    base = {"wg": wg, "wo": wo, "wn": wn}
    base.update(make_consts())
    maps = []
    for k in range(8):
        grp = SEQ_GROUPS[k // 4]
        S_all = np.stack([resD[j]["Sg_o"] for j in grp])
        L_all = np.stack([resD[j]["L_o"] for j in grp])
        pmv = np.zeros((128, 4), np.float32)
        pmv[:, :k % 4] = 1.0
        maps.append(dict(base, xT=x2s[k], ln2_i=resD[k]["ln2_o"], o_i=resD[k]["o_o"],
                         qe_i=resD[k]["qe_o"], Sg_all=S_all, L_all=L_all, posmask=pmv))
    return maps


def kernel(**inputs):
    return kernel_fused(**inputs)


def kernel_unfused(**inputs):
    inputs = {k: np.asarray(v) for k, v in inputs.items()}
    resA = run_bass_kernel_spmd(build_launchA(), prep_launchA(inputs),
                                core_ids=list(range(8))).results
    resB = run_bass_kernel_spmd(build_launchB(), prep_launchB(inputs, resA),
                                core_ids=list(range(8))).results
    x1s = [resB[k]["x1_o"] for k in range(8)]
    resC = run_bass_kernel_spmd(build_ffn("sign"), prep_ffn0(inputs, x1s),
                                core_ids=list(range(8))).results
    x2s = [resC[k]["xo"] for k in range(8)]
    resD = run_bass_kernel_spmd(build_launchD(), prep_launchD(inputs, x2s),
                                core_ids=list(range(8))).results
    resE = run_bass_kernel_spmd(build_launchE(), prep_launchE(inputs, x2s, resD),
                                core_ids=list(range(8))).results
    x3s = [resE[k]["x3_o"] for k in range(8)]
    resF = run_bass_kernel_spmd(build_ffn("swiglu"), prep_ffn1(inputs, x3s),
                                core_ids=list(range(8))).results
    out = np.empty((2, 4096, 2048), np.float32)
    for k in range(8):
        b, pos = k // 4, k % 4
        out[b, pos * T:(pos + 1) * T, :] = np.asarray(resF[k]["xo"]).T
    return out


def build_fused():
    nc = bass.Bass("TRN2", target_bir_lowering=False)
    stack = ExitStack()
    with stack:
        em = Em(nc, stack)
        cnames = sorted(set(CONSTS_A + CONSTS_B + CONSTS_D + CONSTS_E +
                            ["c_ones_col", "c_ones_col_b", "c_ones_row"]))
        cst = declare_consts(em, nc, names=cnames)
        TL = T + TPAD

        iosA = declare_ioA(nc, ext=False)
        kdbg = dict(kind="ExternalOutput") if DBG else {}
        x1_d = nc.dram_tensor("x1_int", [D, T], F32, **kdbg).ap()
        x2_d = nc.dram_tensor("x2_int", [D, T], F32, **kdbg).ap()
        x3_d = nc.dram_tensor("x3_int", [D, T], F32, **kdbg).ap()
        out_d = nc.dram_tensor("xo", [D, T], F32, kind="ExternalOutput").ap()
        gin_m = nc.dram_tensor("gin_m", [128, 4096 + NH], F32).ap()
        gout_m = nc.dram_tensor("gout_m", [512, 4096 + NH], F32).ap()
        if DBG:
            iosA["gdbg"] = nc.dram_tensor("gdbg", [512, 4096 + NH], F32, kind="ExternalOutput").ap()
            iosA["gindbg"] = nc.dram_tensor("gindbg", [128, 4096 + NH], F32, kind="ExternalOutput").ap()
        gin_g = nc.dram_tensor("gin_g", [128, 4096 + 16], F32).ap()
        gout_g = nc.dram_tensor("gout_g", [512, 4096 + 16], F32).ap()

        sc = em.push_scope()
        emit_stageA(em, nc, cst, iosA, "full", gin=gin_m)
        em.barrier()
        em.pop_scope()
        inst = nc.gpsimd.collective_compute(
            "AllGather", ALU.bypass, replica_groups=SEQ_GROUPS,
            ins=[gin_m], outs=[gout_m])
        inst.then_inc(em.sem["cc"])
        em.cnt["cc"] += 1
        ccv = em.cnt["cc"]
        if DBG and iosA.get("gindbg") is not None:
            em.dma("sy", iosA["gindbg"], gin_m)

        def wait_cc1():
            for e in Em.ENGS:
                em.eng[e].wait_ge(em.sem["cc"], ccv)
        iosB = {"xT_d": iosA["xT_d"], "ln0_o": iosA["ln0_o"], "y_o": iosA["y_o"],
                "csave_o": iosA["csave_o"], "expg_o": iosA["expg_o"],
                "gdbg": iosA.get("gdbg"),
                "wz_d": nc.dram_tensor("wz", [32, 128, D], F32, kind="ExternalInput").ap(),
                "wout_d": nc.dram_tensor("wout", [KC, 128, DIN], F32, kind="ExternalInput").ap(),
                "nw_d": nc.dram_tensor("nw", [128, 32], F32, kind="ExternalInput").ap(),
                "pmb_d": nc.dram_tensor("pmb", [1, 4], F32, kind="ExternalInput").ap(),
                "x1_o": x1_d}
        wait_cc1()
        sc = em.push_scope()
        emit_stageB(em, nc, cst, iosB, gout=gout_m, wait_cc=None)
        em.barrier()
        em.pop_scope()
        iosC = {"xT_d": x1_d, "xo_d": x2_d,
                "wup_d": nc.dram_tensor("wup0", [FH // 128, 128, D], F32, kind="ExternalInput").ap(),
                "wdn_d": nc.dram_tensor("wdn0", [KC, 128, FH], F32, kind="ExternalInput").ap(),
                "upsc_d": nc.dram_tensor("upsc", [128, FH // 128], F32, kind="ExternalInput").ap(),
                "dnsc_d": nc.dram_tensor("dnsc", [128, KC], F32, kind="ExternalInput").ap(),
                "osc_d": nc.dram_tensor("osc", [1, 1], F32, kind="ExternalInput").ap()}
        sc = em.push_scope()
        emit_ffn(em, nc, cst, iosC, "sign")
        em.barrier()
        em.pop_scope()
        iosD = declare_ioD(nc, ext=False)
        iosD["xT_d"] = x2_d
        sc = em.push_scope()
        emit_stageD(em, nc, cst, iosD, gin=gin_g)
        em.barrier()
        em.pop_scope()
        inst = nc.gpsimd.collective_compute(
            "AllGather", ALU.bypass, replica_groups=SEQ_GROUPS,
            ins=[gin_g], outs=[gout_g])
        inst.then_inc(em.sem["cc"])
        em.cnt["cc"] += 1
        ccv2 = em.cnt["cc"]

        def wait_cc2():
            for e in Em.ENGS:
                em.eng[e].wait_ge(em.sem["cc"], ccv2)
        iosE = {"xT_d": x2_d, "ln2_o": iosD["ln2_o"], "o_o": iosD["o_o"],
                "qe_om": iosD["qe_om"],
                "pme_d": nc.dram_tensor("pme", [128, 4], F32, kind="ExternalInput").ap(),
                "wg_d": nc.dram_tensor("wg", [8, 128, D], F32, kind="ExternalInput").ap(),
                "wn_d": nc.dram_tensor("wn", [128, 2], F32, kind="ExternalInput").ap(),
                "wo_d": nc.dram_tensor("wo", [KC, 128, 1024], F32, kind="ExternalInput").ap(),
                "x3_o": x3_d}
        wait_cc2()
        sc = em.push_scope()
        emit_stageE(em, nc, cst, iosE, gout=gout_g, wait_cc=None)
        em.barrier()
        em.pop_scope()
        iosF = {"xT_d": x3_d, "xo_d": out_d,
                "wga_d": nc.dram_tensor("wga1", [FH // 128, 128, D], F32, kind="ExternalInput").ap(),
                "wup_d": nc.dram_tensor("wup1", [FH // 128, 128, D], F32, kind="ExternalInput").ap(),
                "wdn_d": nc.dram_tensor("wdn1", [KC, 128, FH], F32, kind="ExternalInput").ap()}
        sc = em.push_scope()
        emit_ffn(em, nc, cst, iosF, "swiglu")
        em.pop_scope()
        em.fence_out()
    return nc


def prep_fused(inputs):
    mapsA = prep_launchA(inputs)
    W_in = np.asarray(inputs["L0_attn_W_in"], np.float32)
    base = {
        "wz": prep_lhsT(W_in[:, 0:DIN]),
        "wout": prep_lhsT(np.asarray(inputs["L0_attn_W_out"], np.float32)),
        "nw": np.ascontiguousarray(
            np.asarray(inputs["L0_attn_norm_w"], np.float32).reshape(32, 128).T),
        "wup0": prep_lhsT(np.asarray(inputs["L0_ffn_up_weight"], np.float32).T),
        "wdn0": prep_lhsT(np.asarray(inputs["L0_ffn_down_weight"], np.float32).T),
        "upsc": np.ascontiguousarray(
            np.asarray(inputs["L0_ffn_up_scale"], np.float32).reshape(FH // 128, 128).T),
        "dnsc": np.ascontiguousarray(
            np.asarray(inputs["L0_ffn_down_scale"], np.float32).reshape(KC, 128).T),
        "osc": np.asarray(inputs["L0_ffn_output_scale"], np.float32).reshape(1, 1),
        "wq": prep_lhsT(np.asarray(inputs["L1_attn_Wq"], np.float32)),
        "wk": prep_lhsT(np.asarray(inputs["L1_attn_Wk"], np.float32)),
        "wv": prep_lhsT(np.asarray(inputs["L1_attn_Wv"], np.float32)),
        "wgk1": prep_lhsT(np.asarray(inputs["L1_attn_Wgk1"], np.float32), pad_to=128),
        "bg": np.ascontiguousarray(
            np.asarray(inputs["L1_attn_bgk2"], np.float32).reshape(16, 128).T),
        "wg": prep_lhsT(np.asarray(inputs["L1_attn_Wg"], np.float32)),
        "wn": np.ascontiguousarray(
            np.asarray(inputs["L1_attn_Wnorm"], np.float32).reshape(2, 128).T),
        "wo": prep_lhsT(np.asarray(inputs["L1_attn_Wo"], np.float32)),
        "wga1": prep_lhsT(np.asarray(inputs["L1_ffn_w_gate"], np.float32).T),
        "wup1": prep_lhsT(np.asarray(inputs["L1_ffn_w_up"], np.float32).T),
        "wdn1": prep_lhsT(np.asarray(inputs["L1_ffn_w_down"], np.float32).T),
    }
    wgk2 = np.zeros((16, 128, 128), np.float32)
    wgk2[:, 0:16, :] = np.asarray(
        inputs["L1_attn_Wgk2"], np.float32).reshape(16, 16, 128).transpose(1, 0, 2)
    base["wgk2"] = wgk2
    maps = []
    for k in range(8):
        pmb = np.zeros((1, 4), np.float32)
        pmb[0, :k % 4] = 1.0
        pme = np.zeros((128, 4), np.float32)
        pme[:, :k % 4] = 1.0
        maps.append(dict(mapsA[k], **base, pmb=pmb, pme=pme))
    return maps


def kernel_fused(**inputs):
    inputs = {k: np.asarray(v) for k, v in inputs.items()}
    res = run_bass_kernel_spmd(build_fused(), prep_fused(inputs),
                               core_ids=list(range(8))).results
    out = np.empty((2, 4096, 2048), np.float32)
    for k in range(8):
        b, pos = k // 4, k % 4
        out[b, pos * T:(pos + 1) * T, :] = np.asarray(res[k]["xo"]).T
    return out


# version 12 (final)
# speedup vs baseline: 1.0244x; 1.0244x over previous
import numpy as np
from contextlib import ExitStack

import concourse.bass as bass
import concourse.mybir as mybir
from concourse.bass_utils import run_bass_kernel_spmd

F32 = mybir.dt.float32
BF16 = mybir.dt.bfloat16
I32 = mybir.dt.int32
AF = mybir.ActivationFunctionType
ALU = mybir.AluOpType
AX = mybir.AxisListType

DBG = False
D = 2048
KC = 16
T = 1024
TPAD = 3
NH = 64
HD = 64
NS = 128
DIN = 4096
NCH = 34
MCH = 8
GH = 4
GDK = 512
GDV = 256
GCH = 8
FH = 8192
EPS = 1e-5
MAGIC = 0x5F3759DF
LNS512 = float(np.log(1.0 / np.sqrt(512.0)))


class Em:

    ENGS = ("pe", "ve", "ac", "po", "sy")

    def __init__(self, nc, stack):
        self.nc = nc
        self.stack = stack
        self.eng = {"pe": nc.tensor, "ve": nc.vector, "ac": nc.scalar,
                    "po": nc.gpsimd, "sy": nc.sync}
        self.sem = {}
        for e in self.ENGS:
            self.sem[e] = stack.enter_context(nc.semaphore(f"sem_{e}"))
        self.sem["dsy"] = stack.enter_context(nc.semaphore("sem_dsy"))
        self.sem["dpo"] = stack.enter_context(nc.semaphore("sem_dpo"))
        self.sem["cc"] = stack.enter_context(nc.semaphore("sem_cc"))
        self.cnt = {k: 0 for k in self.sem}
        self.last = {}
        self.dirty = set()
        self._names = {}
        self.pb = [stack.enter_context(nc.psum_tensor(f"pbank{i}", [128, 512], F32))
                   for i in range(8)]

    def bank(self, i, w=512):
        return self.pb[i][:, :w]

    def bank_b(self, i, w=512):
        return self.pb[i][:].bitcast(BF16)[:, :w]

    def name(self, prefix):
        n = self._names.get(prefix, 0)
        self._names[prefix] = n + 1
        return f"{prefix}{n}"

    def sb(self, prefix, shape, dtype):
        st = getattr(self, "_scope", None) or self.stack
        return st.enter_context(
            self.nc.sbuf_tensor(self.name(prefix), shape, dtype))

    def push_scope(self):
        self._scope = ExitStack()
        return self._scope

    def pop_scope(self):
        self._scope.close()
        self._scope = None

    def _t(self, e, inst):
        self.last[e] = inst
        self.dirty.add(e)
        return inst

    def mm(self, out, lhsT, rhs, start=True, stop=True):
        return self._t("pe", self.nc.tensor.matmul(_ap(out), _ap(lhsT), _ap(rhs), start=start, stop=stop))

    def trans(self, out, in_, ident):
        return self._t("pe", self.nc.tensor.transpose(_ap(out), _ap(in_), _ap(ident)))

    def tt(self, out, a, b, op):
        return self._t("ve", self.nc.vector.tensor_tensor(_ap(out), _ap(a), _ap(b), op))

    def ts(self, out, a, s1, s2=None, op0=ALU.mult, op1=None):
        if op1 is None:
            return self._t("ve", self.nc.vector.tensor_scalar(_ap(out), _ap(a), _ap(s1), None, op0))
        return self._t("ve", self.nc.vector.tensor_scalar(_ap(out), _ap(a), _ap(s1), _ap(s2), op0, op1))

    def stt(self, out, in0, scalar, in1, op0, op1):
        return self._t("ve", self.nc.vector.scalar_tensor_tensor(_ap(out), _ap(in0), _ap(scalar), _ap(in1), op0, op1))

    def vcopy(self, out, in_):
        return self._t("ve", self.nc.vector.tensor_copy(_ap(out), _ap(in_)))

    def reduce(self, out, in_, op):
        return self._t("ve", self.nc.vector.tensor_reduce(_ap(out), _ap(in_), AX.X, op))

    def memset(self, ap, val, eng="ve"):
        return self._t(eng, self.eng[eng].memset(_ap(ap), val))

    def act(self, out, in_, func, bias=0.0, scale=1.0, accum_out=None):
        return self._t("ac", self.nc.scalar.activation(
            _ap(out), _ap(in_), func, bias=_ap(bias) if bias is not None else bias,
            scale=_ap(scale) if scale is not None else scale, accum_out=_ap(accum_out)))

    def acopy(self, out, in_):
        return self._t("ac", self.nc.scalar.copy(_ap(out), _ap(in_)))

    def pott(self, out, a, b, op):
        return self._t("po", self.nc.gpsimd.tensor_tensor(_ap(out), _ap(a), _ap(b), op))

    def postt(self, out, in0, scalar, in1, op0, op1):
        return self._t("po", self.nc.gpsimd.scalar_tensor_tensor(_ap(out), _ap(in0), _ap(scalar), _ap(in1), op0, op1))

    def dma(self, q, out, in_):
        inst = self.eng[q].dma_start(out=_ap(out), in_=_ap(in_))
        ds = "d" + q
        inst.then_inc(self.sem[ds], 16)
        self.cnt[ds] += 16
        if self.last.get(q) is not None:
            self.last[q] = None
        self.dirty.add(q)
        return inst

    def _flush(self, producers):
        for e in producers:
            if e in self.dirty and self.last.get(e) is not None:
                self.last[e].then_inc(self.sem[e], 1)
                self.cnt[e] += 1
                self.last[e] = None
            self.dirty.discard(e)

    def barrier(self, waiters=None, producers=None):
        producers = list(producers) if producers is not None else list(self.ENGS)
        waiters = list(waiters) if waiters is not None else list(self.ENGS)
        self._flush(producers)
        sems = []
        for p in producers:
            if p in ("sy", "po") and self.cnt["d" + p]:
                sems.append("d" + p)
            if self.cnt.get(p):
                sems.append(p)
        for w in waiters:
            for s in sems:
                self.eng[w].wait_ge(self.sem[s], self.cnt[s])
        return self

    def fence_out(self):
        self._flush(list(self.ENGS))
        for s in ("dsy", "dpo"):
            if self.cnt[s]:
                self.eng["sy"].wait_ge(self.sem[s], self.cnt[s])

    def wait_val(self, w, sem_name, value):
        self.eng[w].wait_ge(self.sem[sem_name], value)

    def mark(self, e):
        self._flush([e])
        return self.cnt[e]

    def wait_cnt(self, w, sem_name, value):
        if value > 0:
            self.eng[w].wait_ge(self.sem[sem_name], value)


def _ap(x):
    from concourse.bass_types import AP as _AP
    if x is None or isinstance(x, (_AP, int, float)):
        return x
    return x[:]


def rep_ap(ap, axis, n):
    sh = list(ap.shape)
    sh.insert(axis, n)
    return ap.unsqueeze(axis).broadcast_to(sh)


def emit_rsqrt(em, out, in_, sa, sb_, scale=1.0, eps=0.0, iters=4):
    em.ts(sb_, in_, scale, eps, ALU.mult, ALU.add)
    g = out
    em.ts(g.bitcast(I32), sb_.bitcast(I32), 1, None, ALU.arith_shift_right)
    em.ts(g.bitcast(I32), g.bitcast(I32), MAGIC, None, ALU.subtract)
    em.ts(g.bitcast(I32), g.bitcast(I32), -1, None, ALU.mult)
    for _ in range(iters):
        em.tt(sa, g, g, ALU.mult)
        em.tt(sa, sa, sb_, ALU.mult)
        em.ts(sa, sa, -0.5, 1.5, ALU.mult, ALU.add)
        em.tt(g, g, sa, ALU.mult)


def emit_ln(em, x_sb, ln_sb, cst, scr, ncols, b0=0, taps=None):
    tiles = []
    t0 = 0
    while t0 < ncols:
        tw = min(512, ncols - t0)
        if tw < 128 and tiles:
            t0 = ncols - 512
            tw = 512
            tiles = [t for t in tiles if t[0] < t0] + [(t0, tw)]
            break
        tiles.append((t0, tw))
        t0 += tw
    ps_s, ps_q, ps_b, ps_c = (em.bank(b0), em.bank(b0 + 1), em.bank(b0 + 2), em.bank(b0 + 3))
    sqf = scr["sqf"]
    r = [scr["row"][:, i, :] for i in range(4)]
    sc = scr["sc512"]
    for (t0, tw) in tiles:
        for k in range(KC):
            em.mm(ps_s[0:1, :tw], cst["ones_col"], x_sb[:, k, t0:t0 + tw],
                  start=(k == 0), stop=(k == KC - 1))
        for k in range(KC):
            em.act(sqf[:, k, :tw], x_sb[:, k, t0:t0 + tw], AF.Square)
        em.barrier(["pe"], ["ac"])
        for k in range(KC):
            em.mm(ps_q[0:1, :tw], cst["ones_col_b"], sqf[:, k, :tw],
                  start=(k == 0), stop=(k == KC - 1))
        em.barrier(["ve"], ["pe"])
        if taps is not None and t0 == ncols - 512:
            em.vcopy(scr["sc512"][0:1, :], ps_s[0:1, :tw])
            em.barrier(["sy"], ["ve"])
            em.dma("sy", taps[0][:, 0, :], scr["sc512"][0:1, :])
            em.vcopy(scr["sc512"][0:1, :], ps_q[0:1, :tw])
            em.barrier(["sy"], ["ve"])
            em.dma("sy", taps[0][:, 1, :], scr["sc512"][0:1, :])
            em.vcopy(scr["sc512"][:, :], sqf[:, 0, :tw])
            em.barrier(["sy"], ["ve"])
            em.dma("sy", taps[1], scr["sc512"][:, :])
            em.barrier(["ve"], ["sy"])
        em.ts(r[0][:, :tw], ps_s[0:1, :tw], 1.0 / D)
        em.ts(r[1][:, :tw], ps_q[0:1, :tw], 1.0 / D)
        em.tt(r[2][:, :tw], r[0][:, :tw], r[0][:, :tw], ALU.mult)
        em.tt(r[1][:, :tw], r[1][:, :tw], r[2][:, :tw], ALU.subtract)
        emit_rsqrt(em, r[2][:, :tw], r[1][:, :tw], r[3][:, :tw], r[1][:, :tw], eps=EPS)
        em.tt(r[3][:, :tw], r[0][:, :tw], r[2][:, :tw], ALU.mult)
        em.barrier(["pe"], ["ve"])
        em.mm(ps_b[:, :tw], cst["ones_row"], r[2][:, :tw])
        em.mm(ps_c[:, :tw], cst["ones_row"], r[3][:, :tw])
        em.barrier(["ve"], ["pe"])
        for k in range(KC):
            em.tt(sc[:, :tw], x_sb[:, k, t0:t0 + tw], ps_b[:, :tw], ALU.mult)
            em.tt(ln_sb[:, k, t0:t0 + tw], sc[:, :tw], ps_c[:, :tw], ALU.subtract)
        em.barrier()


def emit_proj(em, w_sb, ln_sb, mo, ko, t0, tw, out_cb, banks=(4, 5), kc_base=0):
    for m in range(mo):
        b = banks[m % len(banks)]
        for k in range(ko):
            em.mm(em.bank(b, tw), w_sb[:, (m * ko + k) * 128:(m * ko + k + 1) * 128],
                  ln_sb[:, kc_base + k, t0:t0 + tw],
                  start=(k == 0), stop=(k == ko - 1))
        out_cb(m, em.bank(b, tw), b)


def prep_lhsT(W, pad_to=None):
    K, M = W.shape
    if pad_to is not None and M < pad_to:
        W = np.concatenate([W, np.zeros((K, pad_to - M), W.dtype)], axis=1)
        M = pad_to
    Ko, Mo = K // 128, M // 128
    Wr = np.asarray(W, np.float32).reshape(Ko, 128, Mo, 128)
    return np.ascontiguousarray(Wr.transpose(2, 1, 0, 3).reshape(Mo, 128, Ko * 128))


def make_consts():
    i = np.arange(128)
    tril1 = (i[:, None] <= i[None, :]).astype(np.float32)
    trilstrict = (i[:, None] > i[None, :]).astype(np.float32)
    trilneg = np.where(i[:, None] <= i[None, :], 0.0, -1e30).astype(np.float32)
    ident = np.eye(128, dtype=np.float32)
    esel = np.zeros((64, 64 * 128), np.float32)
    for h in range(64):
        esel[h, h * 128:(h + 1) * 128] = -1.0
    eexp = np.zeros((64, 64 * 64), np.float32)
    for h in range(64):
        eexp[h, h * 64:(h + 1) * 64] = 1.0
    return {
        "c_tril1": tril1, "c_trilstrict": trilstrict, "c_trilneg": trilneg,
        "c_triu01_b": tril1, "c_ident": ident, "c_ident_b": ident,
        "c_tri16": (-tril1 / 16.0).astype(np.float32),
        "c_esel": esel, "c_eexp_b": eexp,
        "c_ones_col": np.ones((128, 1), np.float32),
        "c_ones_col_b": np.ones((128, 1), np.float32),
        "c_ones_row": np.ones((1, 128), np.float32),
    }


CONST_SPECS = {
    "c_tril1": ([128, 128], F32), "c_trilstrict": ([128, 128], F32),
    "c_trilneg": ([128, 128], F32), "c_triu01_b": ([128, 128], BF16),
    "c_ident": ([128, 128], F32), "c_ident_b": ([128, 128], BF16),
    "c_tri16": ([128, 128], F32), "c_esel": ([64, 8192], F32),
    "c_eexp_b": ([64, 4096], BF16),
    "c_ones_col": ([128, 1], F32), "c_ones_col_b": ([128, 1], BF16),
    "c_ones_row": ([1, 128], F32),
}


def declare_consts(em, nc, names=None):
    cst = {}
    stage = []
    sc = ExitStack()
    for name, (shape, dt) in CONST_SPECS.items():
        if names is not None and name not in names:
            continue
        d = nc.dram_tensor(name, shape, F32, kind="ExternalInput").ap()
        if dt == BF16:
            f = sc.enter_context(nc.sbuf_tensor(name + "_f", shape, F32, side="right"))
            t = em.sb(name + "_sb", shape, BF16)
            em.dma("sy", f[:], d)
            stage.append((t, f))
        else:
            t = em.sb(name + "_sb", shape, dt)
            em.dma("sy", t[:], d)
        cst[name[2:]] = t[:]
    if stage:
        em.barrier(["ve"], ["sy"])
        for t, f in stage:
            em.vcopy(t[:], f[:])
        em.barrier()
    sc.close()
    return cst


def shard_inputs(hs):
    outs = []
    for core in range(8):
        b, pos = core // 4, core % 4
        t0 = pos * T
        sl = hs[b, max(0, t0 - TPAD):t0 + T, :]
        x = np.asarray(sl, np.float32).T
        if x.shape[1] < T + TPAD:
            x = np.concatenate([np.zeros((D, T + TPAD - x.shape[1]), np.float32), x], axis=1)
        outs.append(np.ascontiguousarray(x))
    return outs


def load_x(em, nc, name, ncols, kind="ExternalInput"):
    d = nc.dram_tensor(name, [D, ncols], F32, kind=kind).ap()
    t = em.sb(name + "_sb", [128, KC, ncols], F32)
    for k in range(KC):
        em.dma("sy", t[:, k, :], d[k * 128:(k + 1) * 128, :])
    return d, t


def load_weight_slab(em, nc, name, mo, kcols, dt_in=F32):
    d = nc.dram_tensor(name, [mo, 128, kcols], dt_in, kind="ExternalInput").ap()
    t = em.sb(name + "_sb", [128, mo, kcols], BF16)
    for m in range(mo):
        em.dma("po", t[:, m, :], d[m])
    return t




SEQ_GROUPS = [[0, 1, 2, 3], [4, 5, 6, 7]]


def shard_inputs(hs):
    outs = []
    for core in range(8):
        b, pos = core // 4, core % 4
        t0 = pos * T
        sl = hs[b, max(0, t0 - TPAD):t0 + T, :]
        x = np.asarray(sl, np.float32).T
        if x.shape[1] < T + TPAD:
            x = np.concatenate([np.zeros((D, T + TPAD - x.shape[1]), np.float32), x], axis=1)
        outs.append(np.ascontiguousarray(x))
    return outs


def load_x(em, nc, name, ncols, sbuf=None, side=None):
    d = nc.dram_tensor(name, [D, ncols], F32, kind="ExternalInput").ap()
    t = sbuf if sbuf is not None else em.stack.enter_context(
        nc.sbuf_tensor(em.name(name + "_sb"), [128, KC, ncols], F32, side=side))
    for k in range(KC):
        em.dma("sy", t[:, k, :], d[k * 128:(k + 1) * 128, :])
    return d, t


CONSTS_A = ["c_tril1", "c_trilstrict", "c_trilneg", "c_ident", "c_ident_b",
            "c_eexp_b", "c_ones_col", "c_ones_col_b", "c_ones_row"]


def declare_ioA(nc, ext=True):
    ki = dict(kind="ExternalInput") if ext else dict(kind="ExternalInput")
    ko = dict(kind="ExternalOutput") if ext else dict(kind="Internal")
    TL = T + TPAD
    ios = {}
    ios["wA_d"] = nc.dram_tensor("wA", [NCH + 1, 128, D], F32, **ki).ap()
    ios["cw_d"] = nc.dram_tensor("cw", [128, NCH, 4], F32, **ki).ap()
    ios["cb_d"] = nc.dram_tensor("cb", [128, NCH], F32, **ki).ap()
    ios["arow_d"] = nc.dram_tensor("arow", [1, NH], F32, **ki).ap()
    ios["drow_d"] = nc.dram_tensor("drow", [1, NH], F32, **ki).ap()
    ios["dtb_d"] = nc.dram_tensor("dtb", [128, 1], F32, **ki).ap()
    ios["xT_d"] = nc.dram_tensor("xT", [D, TL], F32, **ki).ap()
    ios["ln0_o"] = nc.dram_tensor("ln0_o", [D, TL], BF16, **ko).ap()
    ios["y_o"] = nc.dram_tensor("y_o", [MCH, 128, DIN], BF16, **ko).ap()
    ios["S_o"] = nc.dram_tensor("S_o", [128, 4096], F32, **ko).ap()
    ios["offs_o"] = nc.dram_tensor("offs_o", [1, NH], F32, **ko).ap()
    ios["expg_o"] = nc.dram_tensor("expg_o", [MCH, 128, NH], F32, **ko).ap()
    ios["csave_o"] = nc.dram_tensor("csave_o", [128, MCH, 128], BF16, **ko).ap()
    if ext:
        ios["xbc_dbg"] = nc.dram_tensor("xbc_dbg", [128, NCH, TL], BF16, **ko).ap()
        ios["ssd_dbg"] = nc.dram_tensor("ssd_dbg", [3, 128, DIN], BF16, **ko).ap()
        ios["bt_dbg"] = nc.dram_tensor("bt_dbg", [128, 128], BF16, **ko).ap()
        ios["dtsp_dbg"] = nc.dram_tensor("dtsp_dbg", [128, T], F32, **ko).ap()
    ios["dacs_dram"] = nc.dram_tensor("dacs_scr", [64, 128], F32).ap()
    return ios


def build_launchA(stage="full"):
    nc = bass.Bass("TRN2", target_bir_lowering=False)
    stack = ExitStack()
    with stack:
        em = Em(nc, stack)
        cst = declare_consts(em, nc, names=CONSTS_A)
        ios = declare_ioA(nc, True)
        emit_stageA(em, nc, cst, ios, stage)
    return nc


def emit_stageA(em, nc, cst, ios, stage="full", gin=None):
    TL = T + TPAD
    wA_d = ios["wA_d"]; cw_d = ios["cw_d"]; cb_d = ios["cb_d"]
    arow_d = ios["arow_d"]; drow_d = ios["drow_d"]; dtb_d = ios["dtb_d"]
    ln0_o = ios["ln0_o"]; y_o = ios["y_o"]; S_o = ios["S_o"]
    offs_o = ios["offs_o"]; expg_o = ios["expg_o"]; csave_o = ios["csave_o"]
    xbc_dbg = ios.get("xbc_dbg"); ssd_dbg = ios.get("ssd_dbg")
    bt_dbg = ios.get("bt_dbg"); dtsp_dbg = ios.get("dtsp_dbg")
    dacs_dram = ios["dacs_dram"]
    if True:

        cw = em.sb("cw", [128, NCH, 4], F32)
        cb = em.sb("cb", [128, NCH], F32)
        dtb = em.sb("dtb", [128, 1], F32)
        a_b = em.sb("a_b", [128, NH], F32)
        d_b = em.sb("d_b", [128, NH], F32)
        dtsp = em.sb("dtsp", [128, T], F32)
        rowm = em.sb("rowm", [1, 2, NH], F32)
        em.dma("sy", cw[:, :, :], cw_d)
        em.dma("sy", cb[:, :], cb_d)
        em.dma("sy", dtb[:], dtb_d)
        em.dma("sy", rowm[:, 0, :], arow_d)

        sc_outer = ExitStack()
        ln0 = sc_outer.enter_context(
            nc.sbuf_tensor(em.name("ln0"), [128, KC, TL], BF16, side="right"))
        sc_inner = ExitStack()
        x_sb = sc_inner.enter_context(
            nc.sbuf_tensor(em.name("x_sb"), [128, KC, TL], F32, side="right"))
        scr = {
            "sqf": sc_inner.enter_context(nc.sbuf_tensor(em.name("sqf"), [128, KC, 512], BF16, side="right")),
            "row": sc_inner.enter_context(nc.sbuf_tensor(em.name("rowscr"), [1, 4, 512], F32, side="right")),
            "sc512": sc_inner.enter_context(nc.sbuf_tensor(em.name("sc512"), [128, 512], F32, side="right")),
        }
        xT_d = ios["xT_d"]
        for k in range(KC):
            em.dma("sy", x_sb[:, k, :], xT_d[k * 128:(k + 1) * 128, :])

        drow_sb = em.sb("drow_sb", [1, NH], F32)
        em.dma("sy", drow_sb[:], drow_d)
        em.barrier()
        em.mm(em.bank(0, NH), cst["ones_row"], rowm[:, 0, :])
        em.mm(em.bank(1, NH), cst["ones_row"], drow_sb[:])
        em.barrier(["ve"], ["pe"])
        em.vcopy(a_b[:, :], em.bank(0, NH))
        em.vcopy(d_b[:, :], em.bank(1, NH))
        em.memset(rowm[:, 1, :], 0.0)
        em.barrier()

        emit_ln(em, x_sb, ln0, cst, scr, TL)
        for k in range(KC):
            em.dma("sy", ln0_o[k * 128:(k + 1) * 128, :], ln0[:, k, :])
        em.barrier()
        sc_inner.close()
        if stage == "ln":
            em.fence_out()
            return nc

        xbcraw = em.sb("xbcraw", [128, NCH, TL], BF16)
        wg = em.sb("wg", [128, 2, D], BF16)
        S = em.sb("S", [128, 4096], F32)
        S_b = em.sb("S_bf", [128, 4096], BF16)
        x_t = em.sb("x_t", [128, DIN], BF16)
        dtx = em.sb("dtx", [128, DIN], BF16)
        ddtx = em.sb("ddtx", [128, DIN], BF16)
        csave_sb = em.sb("csave_sb", [128, MCH, 128], BF16)
        conv_acc = em.sb("conv_acc", [128, 512], F32)
        conv_acc2 = em.sb("conv_acc2", [128, 512], F32)
        em.memset(S[:, :], 0.0)
        em.memset(S_b[:, :], 0.0)

        MO = NCH + 1
        if stage.startswith("proj"):
            MO = int(stage[4:])
        em.dma("po", wg[:, 0, :], wA_d[0])
        nd0 = em.cnt["dpo"] - 16
        pe_done, ve_done, seq = {}, {}, 0
        for m in range(MO):
            if m + 1 < MO:
                em.wait_cnt("po", "pe", pe_done.get(m - 1, 0))
                em.dma("po", wg[:, (m + 1) % 2, :], wA_d[m + 1])
            em.wait_cnt("pe", "dpo", nd0 + 16 * (m + 1))
            wslab = wg[:, m % 2, :]
            if m < NCH:
                for t0 in (0, 512, 515):
                    bk = 4 + seq % 2
                    em.wait_cnt("pe", "ve", ve_done.get(seq - 2, 0))
                    for k in range(KC):
                        em.mm(em.bank(bk), wslab[:, k * 128:(k + 1) * 128],
                              ln0[:, k, t0:t0 + 512], start=(k == 0), stop=(k == KC - 1))
                    pm = em.mark("pe")
                    em.wait_cnt("ve", "pe", pm)
                    em.vcopy(xbcraw[:, m, t0:t0 + 512], em.bank(bk))
                    ve_done[seq] = em.mark("ve")
                    seq += 1
                pe_done[m] = pm
            else:
                for (ti, t0) in ((0, 3), (1, 515)):
                    bk = 4 + seq % 2
                    em.wait_cnt("pe", "ve", ve_done.get(seq - 2, 0))
                    em.barrier(["pe"], ["ac"])
                    for k in range(KC):
                        em.mm(em.bank(bk), wslab[:, k * 128:(k + 1) * 128],
                              ln0[:, k, t0:t0 + 512], start=(k == 0), stop=(k == KC - 1))
                    pm = em.mark("pe")
                    em.wait_cnt("ac", "pe", pm)
                    em.wait_cnt("ac", "ve", ve_done.get(seq - 1, 0))
                    em.act(conv_acc, em.bank(bk), AF.Relu, bias=dtb[:, 0:1])
                    em.act(conv_acc2, em.bank(bk), AF.Abs, bias=dtb[:, 0:1])
                    em.act(conv_acc2, conv_acc2, AF.Exp, scale=-1.0)
                    em.act(conv_acc2, conv_acc2, AF.Ln, bias=1.0)
                    em.barrier(["ve"], ["ac"])
                    em.tt(dtsp[:, ti * 512:(ti + 1) * 512], conv_acc, conv_acc2, ALU.add)
                    ve_done[seq] = em.mark("ve")
                    seq += 1
                pe_done[m] = pm
        em.barrier()
        sc_outer.close()
        if stage.startswith("proj"):
            em.dma("sy", xbc_dbg[:, 0:MO, :], xbcraw[:, 0:MO, :])
            em.fence_out()
            return nc

        for c in range(NCH):
            for half in range(2):
                o0 = half * 512
                acc = conv_acc if half == 0 else conv_acc2
                em.ts(acc[:, :], xbcraw[:, c, o0:o0 + 512], cw[:, c, 0:1], None, ALU.mult)
                for kk in range(1, 4):
                    em.stt(acc[:, :], xbcraw[:, c, o0 + kk:o0 + kk + 512], cw[:, c, kk:kk + 1],
                           acc[:, :], ALU.mult, ALU.add)
                em.barrier(["ac"], ["ve"])
                em.act(xbcraw[:, c, o0:o0 + 512], acc[:, :], AF.Silu, bias=cb[:, c:c + 1])
            em.barrier(["ve"], ["ac"])
        em.barrier()
        for c in range(MCH):
            em.vcopy(csave_sb[:, c, :], xbcraw[:, 33, c * 128:(c + 1) * 128])
        em.barrier(["sy"], ["ve"])
        em.dma("sy", csave_o, csave_sb[:, :, :])
        if stage == "conv":
            em.dma("sy", xbc_dbg, xbcraw[:, :, :])
            em.dma("sy", dtsp_dbg, dtsp[:, :])
            em.fence_out()
            return nc

        sc_ssd = ExitStack()

        def rsb(nm, shape, dt):
            return sc_ssd.enter_context(nc.sbuf_tensor(em.name(nm), shape, dt, side="right"))
        dA_t = rsb("dA_t", [128, NH], F32)
        dacs = rsb("dacs", [128, NH], F32)
        dacs_hm = rsb("dacs_hm", [64, 128], F32)
        dacs_row = rsb("dacs_row", [1, 8192], F32)
        q_hm = rsb("q_hm", [64, 128], BF16)
        dt_hm = rsb("dt_hm", [64, 128], BF16)
        ed2e_hm = rsb("ed2e_hm", [64, 128], BF16)
        expa_hm = rsb("expa_hm", [64, 128], BF16)
        cbt = rsb("cbt", [128, 128], BF16)
        b_t = rsb("b_tT", [128, 128], BF16)
        mt8 = rsb("mt8", [128, 8 * 128], BF16)
        et_b = rsb("et_b", [128, NH], F32)
        expEg = rsb("expEg", [128, 512], BF16)
        yg_s = rsb("yg_s", [128, 512], BF16)
        f32s = rsb("f32s", [128, 512], F32)
        f32b = rsb("f32b", [128, NH], F32)
        expg_sb = rsb("expg_sb", [128, NH], F32)
        offs = rowm[:, 1, :]

        nch_run = {"ssd1": 1, "ssd2": 2}.get(stage, MCH)
        for c in range(nch_run):
            ct = c * 128
            em.barrier()
            em.trans(em.bank(0, NH), dtsp[0:64, ct:ct + 128], cst["ident"][0:64, 0:64])
            em.barrier(["ve"], ["pe"])
            em.tt(dA_t[:, :], em.bank(0, NH), a_b[:, :], ALU.mult)
            em.barrier(["pe"], ["ve"])
            em.mm(em.bank(0, NH), cst["tril1"], dA_t[:, :])
            em.mm(em.bank(1, NH), cst["trilstrict"], dA_t[:, :])
            em.mm(em.bank(2, NH)[0:1, :], cst["ones_col"], dA_t[:, :])
            em.barrier(["ve"], ["pe"])
            em.vcopy(dacs[:, :], em.bank(0, NH))
            em.barrier(["pe"], ["ve"])
            em.mm(em.bank(3, NH), cst["ones_row"], offs)
            em.barrier(["ve"], ["pe"])
            em.tt(f32b[:, :], dacs[:, :], em.bank(3, NH), ALU.add)
            em.tt(offs, offs, em.bank(2, NH)[0:1, :], ALU.add)
            em.barrier(["ac"], ["ve"])
            em.act(expg_sb[:, :], f32b[:, :], AF.Exp)
            em.act(f32b[:, :], em.bank(1, NH), AF.Exp)
            em.act(et_b[:, :], dacs[:, :], AF.Exp)
            em.barrier(["sy", "pe", "ve"], ["ac"])
            em.dma("sy", expg_o[c], expg_sb[:, :])
            em.trans(em.pb[3][0:64, 0:128], dacs[:, :], cst["ident"])
            em.barrier(["ve"], ["pe"])
            em.vcopy(dacs_hm[:, :], em.pb[3][0:64, 0:128])
            em.barrier(["pe"], ["ve"])
            em.trans(em.pb[3][0:64, 0:128], f32b[:, :], cst["ident"])
            em.barrier(["ve"], ["pe"])
            em.vcopy(ed2e_hm[:, :], em.pb[3][0:64, 0:128])
            em.barrier(["pe"], ["ve"])
            em.trans(em.pb[3][0:64, 0:128], et_b[:, :], cst["ident"])
            em.barrier(["ve"], ["pe"])
            em.vcopy(expa_hm[:, :], em.pb[3][0:64, 0:128])
            em.vcopy(dt_hm[:, :], dtsp[0:64, ct:ct + 128])
            em.tt(q_hm[:, :], dt_hm[:, :], ed2e_hm[:, :], ALU.mult)
            em.barrier(["sy"], ["ve", "pe"])
            em.dma("sy", dacs_dram, dacs_hm[:, :])
            em.barrier(["sy"], ["sy"])
            em.dma("sy", dacs_row[:], dacs_dram.rearrange("a b -> (a b)")[None, :])
            for q in range(8):
                for j in range(4):
                    em.trans(em.bank_b(4 + (q % 2))[:, j * 128:(j + 1) * 128],
                             xbcraw[:, q * 4 + j, ct:ct + 128], cst["ident_b"])
                em.barrier(["ve"], ["pe"])
                em.vcopy(x_t[:, q * 512:(q + 1) * 512], em.bank_b(4 + (q % 2)))
                if q % 2 == 1:
                    em.barrier(["pe"], ["ve"])
            em.trans(em.bank_b(6)[:, 0:128], xbcraw[:, 32, ct:ct + 128], cst["ident_b"])
            em.barrier(["ve"], ["pe"])
            em.vcopy(b_t[:, :], em.bank_b(6)[:, 0:128])
            em.barrier(["pe"], ["ve"])
            for fb in range(8):
                em.mm(em.bank(4 + (fb % 2)), dt_hm[:, :], cst["eexp_b"][:, fb * 512:(fb + 1) * 512])
                em.barrier(["ve"], ["pe"])
                em.tt(dtx[:, fb * 512:(fb + 1) * 512], x_t[:, fb * 512:(fb + 1) * 512],
                      em.bank(4 + (fb % 2)), ALU.mult)
                if fb % 2 == 1:
                    em.barrier(["pe"], ["ve"])
            for fb in range(8):
                em.mm(em.bank(4 + (fb % 2)), q_hm[:, :], cst["eexp_b"][:, fb * 512:(fb + 1) * 512])
                em.barrier(["ve"], ["pe"])
                em.tt(ddtx[:, fb * 512:(fb + 1) * 512], x_t[:, fb * 512:(fb + 1) * 512],
                      em.bank(4 + (fb % 2)), ALU.mult)
                if fb % 2 == 1:
                    em.barrier(["pe"], ["ve"])
            em.mm(em.bank(6, 128), xbcraw[:, 32, ct:ct + 128], xbcraw[:, 33, ct:ct + 128])
            em.barrier(["ve"], ["pe"])
            em.vcopy(cbt[:, :], em.bank(6, 128))
            em.barrier(["pe"], ["sy", "ve"])
            for hg in range(8):
                for gi in range(2):
                    g = hg * 2 + gi
                    bk = 4 + gi
                    em.mm(em.bank(bk), cst["ones_row"], dacs_row[:, g * 512:(g + 1) * 512])
                    em.barrier(["ve"], ["pe"])
                    v3 = lambda ap: ap.rearrange("p (r i) -> p r i", r=4)
                    em.tt(v3(f32s[:, :]), v3(em.bank(bk)),
                          rep_ap(dacs[:, g * 4:(g + 1) * 4], 2, 128), ALU.subtract)
                    em.tt(v3(f32s[:, :]), v3(f32s[:, :]),
                          rep_ap(cst["trilneg"], 1, 4), ALU.min)
                    em.barrier(["ac"], ["ve"])
                    em.act(f32s[:, :], f32s[:, :], AF.Exp)
                    em.barrier(["ve"], ["ac"])
                    em.tt(v3(mt8[:, gi * 512:(gi + 1) * 512]), v3(f32s[:, :]),
                          rep_ap(cbt[:, :], 1, 4), ALU.mult)
                    em.barrier(["pe"], ["ve"])
                em.mm(em.bank(6), xbcraw[:, 33, ct:ct + 128],
                      S_b[:, hg * 512:(hg + 1) * 512], start=True, stop=False)
                em.mm(em.bank(7), expa_hm[:, :], cst["eexp_b"][:, hg * 512:(hg + 1) * 512])
                em.barrier(["ve"], ["pe"])
                em.vcopy(expEg[:, :], em.bank(7))
                em.tt(em.bank(6), em.bank(6), expEg[:, :], ALU.mult)
                em.barrier(["pe"], ["ve"])
                for hh in range(8):
                    em.mm(em.bank(6)[:, hh * 64:(hh + 1) * 64],
                          mt8[:, hh * 128:(hh + 1) * 128],
                          dtx[:, (hg * 8 + hh) * 64:(hg * 8 + hh + 1) * 64],
                          start=False, stop=(hh == 7))
                em.barrier(["ve"], ["pe"])
                yv = yg_s[:, :].rearrange("p (h d) -> p h d", h=8)
                em.tt(yv, x_t[:, hg * 512:(hg + 1) * 512].rearrange("p (h d) -> p h d", h=8),
                      rep_ap(d_b[:, hg * 8:(hg + 1) * 8], 2, 64), ALU.mult)
                em.tt(yg_s[:, :], yg_s[:, :], em.bank(6), ALU.add)
                em.barrier(["sy", "pe"], ["ve"])
                em.dma("sy", y_o[c][:, hg * 512:(hg + 1) * 512], yg_s[:, :])
                em.barrier(["ve"], ["sy"])
            em.barrier(["ac"], ["pe", "ve"])
            em.act(f32b[0:1, 0:NH], em.bank(2, NH)[0:1, :], AF.Exp)
            em.barrier(["pe"], ["ac"])
            em.mm(em.bank(7, NH), cst["ones_row"], f32b[0:1, 0:NH])
            em.barrier(["ve"], ["pe"])
            em.vcopy(et_b[:, :], em.bank(7, NH))
            em.barrier(["pe"], ["ve"])
            for fb in range(8):
                bk = 4 + (fb % 2)
                em.mm(em.bank(bk), b_t[:, :], ddtx[:, fb * 512:(fb + 1) * 512])
                em.barrier(["ve"], ["pe"])
                Sv = S[:, fb * 512:(fb + 1) * 512].rearrange("p (h d) -> p h d", h=8)
                em.tt(Sv, Sv, rep_ap(et_b[:, fb * 8:(fb + 1) * 8], 2, 64), ALU.mult)
                em.tt(S[:, fb * 512:(fb + 1) * 512], S[:, fb * 512:(fb + 1) * 512],
                      em.bank(bk), ALU.add)
                if fb % 2 == 1:
                    em.barrier(["pe"], ["ve"])
            em.vcopy(S_b[:, :], S[:, :])
            if DBG and stage == "ssd1" and c == 0:
                em.barrier(["sy"], ["ve"])
                em.dma("sy", ssd_dbg[0], x_t[:, :])
                em.dma("sy", ssd_dbg[1], dtx[:, :])
                em.dma("sy", ssd_dbg[2], ddtx[:, :])
                em.dma("sy", bt_dbg, b_t[:, :])
            em.barrier(["pe", "ac"], ["ve"])

        em.barrier(["sy"], ["ve", "ac", "pe"])
        if gin is None:
            em.dma("sy", S_o, S[:, :])
            em.dma("sy", offs_o, offs)
            em.fence_out()
        else:
            em.barrier(["pe"], ["ve"])
            em.mm(em.bank(0, NH), cst["ones_row"], offs)
            em.barrier(["ve"], ["pe"])
            em.vcopy(et_b[:, :], em.bank(0, NH))
            em.barrier(["sy"], ["ve"])
            em.dma("sy", gin[:, 0:4096], S[:, :])
            em.dma("sy", gin[:, 4096:4096 + NH], et_b[:, :])
        sc_ssd.close()


def prep_launchA(inputs):
    W_in = np.asarray(inputs["L0_attn_W_in"], np.float32)
    wxbc = prep_lhsT(W_in[:, DIN:DIN + 4352])
    wdt = prep_lhsT(W_in[:, DIN + 4352:], pad_to=128)
    wA = np.concatenate([wxbc, wdt], axis=0)
    convw = np.asarray(inputs["L0_attn_conv_w"], np.float32)
    cw = np.ascontiguousarray(convw.reshape(NCH, 128, 4).transpose(1, 0, 2))
    cb = np.ascontiguousarray(
        np.asarray(inputs["L0_attn_conv_b"], np.float32).reshape(NCH, 128).T)
    arow = np.asarray(inputs["L0_attn_A"], np.float32).reshape(1, NH)
    drow = np.asarray(inputs["L0_attn_D"], np.float32).reshape(1, NH)
    dtb = np.zeros((128, 1), np.float32)
    dtb[0:64, 0] = np.asarray(inputs["L0_attn_dt_bias"], np.float32)
    base = {"wA": wA, "cw": cw, "cb": cb, "arow": arow, "drow": drow, "dtb": dtb}
    base.update(make_consts())
    xs = shard_inputs(np.asarray(inputs["L0_attn_hidden_states"], np.float32))
    return [dict(base, xT=xs[k]) for k in range(8)]




CONSTS_B = ["c_ident", "c_ident_b", "c_eexp_b", "c_ones_col", "c_ones_col_b",
            "c_ones_row"]


def build_launchB():
    nc = bass.Bass("TRN2", target_bir_lowering=False)
    stack = ExitStack()
    with stack:
        em = Em(nc, stack)
        cst = declare_consts(em, nc, names=CONSTS_B)
        ios = {}
        TL = T + TPAD
        ios["xT_d"] = nc.dram_tensor("xT", [D, TL], F32, kind="ExternalInput").ap()
        ios["ln0_o"] = nc.dram_tensor("ln0_i", [D, TL], BF16, kind="ExternalInput").ap()
        ios["wz_d"] = nc.dram_tensor("wz", [32, 128, D], F32, kind="ExternalInput").ap()
        ios["wout_d"] = nc.dram_tensor("wout", [KC, 128, DIN], F32, kind="ExternalInput").ap()
        ios["nw_d"] = nc.dram_tensor("nw", [128, 32], F32, kind="ExternalInput").ap()
        ios["y_o"] = nc.dram_tensor("y_i", [MCH, 128, DIN], BF16, kind="ExternalInput").ap()
        ios["S_all_d"] = nc.dram_tensor("S_all", [4, 128, 4096], F32, kind="ExternalInput").ap()
        ios["offs_d"] = nc.dram_tensor("offs_all", [1, 4 * NH], F32, kind="ExternalInput").ap()
        ios["pmb_d"] = nc.dram_tensor("posmask", [1, 4], F32, kind="ExternalInput").ap()
        ios["csave_o"] = nc.dram_tensor("csave_i", [128, MCH, 128], BF16, kind="ExternalInput").ap()
        ios["expg_o"] = nc.dram_tensor("expg_i", [MCH, 128, NH], F32, kind="ExternalInput").ap()
        ios["x1_o"] = nc.dram_tensor("x1_o", [D, T], F32, kind="ExternalOutput").ap()
        emit_stageB(em, nc, cst, ios)
        em.fence_out()
    return nc


def emit_stageB(em, nc, cst, ios, gout=None, wait_cc=None):
    TL = T + TPAD
    xT_d = ios["xT_d"]; ln0_d = ios["ln0_o"]; wz_d = ios["wz_d"]
    wout_d = ios["wout_d"]; nw_d = ios["nw_d"]; y_d = ios["y_o"]
    pm_d = ios["pmb_d"]; csave_d = ios["csave_o"]; expg_d = ios["expg_o"]
    x1_o = ios["x1_o"]
    if True:

        S_in = em.sb("S_in", [128, 4096], F32)
        S_in_b = em.sb("S_in_b", [128, 4096], BF16)
        csave = em.sb("csaveB", [128, MCH, 128], BF16)
        expg = em.sb("expgB", [128, MCH, NH], F32)
        nw = em.sb("nwB", [128, 32], F32)
        rows = em.sb("rowsB", [1, 8, NH], F32)
        pm = em.sb("pmB", [1, 4], F32)
        z_s = em.sb("z_s", [128, 32, 512], BF16)
        yn_s = em.sb("yn_s", [128, 32, 512], BF16)
        y_sb = em.sb("y_sbB", [128, DIN], BF16)
        y_fm = em.sb("y_fmB", [128, 32, 128], BF16)
        eg_hm = em.sb("eg_hm", [64, 128], BF16)
        expE = em.sb("expEB", [128, 512], BF16)
        tscr = em.sb("tscrB", [128, 512], F32)
        wgran = em.sb("wgranB", [128, 2, D], BF16)
        wgout = em.sb("wgoutB", [128, 2, DIN], BF16)
        xstg = em.sb("xstgB", [128, 2, 512], F32)
        ostg = em.sb("ostgB", [128, 2, 512], F32)
        rrow = em.sb("rrowB", [1, 4, 128], F32)
        wb = em.sb("wbB", [128, NH], F32)
        ln0 = em.sb("ln0B", [128, KC, TL], BF16)

        for k in range(KC):
            em.dma("sy", ln0[:, k, :], ln0_d[k * 128:(k + 1) * 128, :])
        em.dma("sy", csave[:, :, :], csave_d)
        for c in range(MCH):
            em.dma("sy", expg[:, c, :], expg_d[c])
        em.dma("sy", nw[:, :], nw_d)
        if gout is None:
            em.dma("sy", rows[:, 0:4, :], ios["offs_d"].rearrange("a (j h) -> a j h", j=4))
        em.dma("sy", pm[:], pm_d)
        em.barrier()

        pe_done = {}
        z_done_tile = {}

        def zproj(ti):
            t0 = ti * 512
            em.barrier()
            em.dma("po", wgran[:, 0, :], wz_d[0])
            nd0 = em.cnt["dpo"] - 16
            ac_done = {}
            for m in range(32):
                if m + 1 < 32:
                    em.wait_cnt("po", "pe", pe_done.get(m - 1, 0))
                    em.dma("po", wgran[:, (m + 1) % 2, :], wz_d[m + 1])
                em.wait_cnt("pe", "dpo", nd0 + 16 * (m + 1))
                em.wait_cnt("pe", "ac", ac_done.get(m - 2, 0))
                for k in range(KC):
                    em.mm(em.bank(4 + m % 2), wgran[:, m % 2, k * 128:(k + 1) * 128],
                          ln0[:, k, TPAD + t0:TPAD + t0 + 512],
                          start=(k == 0), stop=(k == KC - 1))
                pe_done[m] = em.mark("pe")
                em.wait_cnt("ac", "pe", pe_done[m])
                em.act(z_s[:, m, :], em.bank(4 + m % 2), AF.Silu)
                ac_done[m] = em.mark("ac")
            em.barrier()

        zproj(0)
        if wait_cc is not None:
            wait_cc()
        if DBG and gout is not None and ios.get("gdbg") is not None:
            em.dma("sy", ios["gdbg"], gout)
        if gout is not None:
            for j in range(4):
                em.dma("sy", rows[:, j, :], gout[j * 128:j * 128 + 1, 4096:4096 + NH])
            em.barrier(["ve", "ac"], ["sy"])
        em.memset(rows[:, 7, :], 0.0)
        for j in (2, 1, 0):
            em.stt(rows[:, 4 + j, :], rows[:, j + 1, :], pm[:, j + 1:j + 2],
                   rows[:, 4 + j + 1, :], ALU.mult, ALU.add)
        em.barrier(["ac"], ["ve"])
        for j in range(4):
            em.act(rows[:, 4 + j, :], rows[:, 4 + j, :], AF.Exp)
        em.barrier(["ve"], ["ac"])
        for j in range(4):
            em.ts(rows[:, 4 + j, :], rows[:, 4 + j, :], pm[:, j:j + 1], None, ALU.mult)
        em.memset(S_in[:, :], 0.0)
        stage_S = yn_s[:, :, :].rearrange("p a b -> p (a b)").bitcast(F32)[:, 0:4096]
        for j in range(4):
            em.barrier(["sy"], ["ve"])
            if gout is None:
                em.dma("sy", stage_S[:, :], ios["S_all_d"][j])
            else:
                em.dma("sy", stage_S[:, :], gout[j * 128:(j + 1) * 128, 0:4096])
            em.barrier(["pe"], ["ve"])
            em.mm(em.bank(0, NH), cst["ones_row"], rows[:, 4 + j, :])
            em.barrier(["ve"], ["pe", "sy"])
            em.vcopy(wb[:, :], em.bank(0, NH))
            Sv = stage_S[:, :].rearrange("p (h d) -> p h d", h=NH)
            em.tt(Sv, Sv, rep_ap(wb[:, :], 2, HD), ALU.mult)
            em.tt(S_in[:, :], S_in[:, :], stage_S[:, :], ALU.add)
        em.vcopy(S_in_b[:, :], S_in[:, :])
        em.barrier()

        for ti in range(2):
            t0 = ti * 512
            if ti == 1:
                zproj(1)
            for ci in range(4):
                c = ti * 4 + ci
                em.barrier()
                em.dma("sy", y_sb[:, :], y_d[c])
                em.trans(em.pb[2][0:64, 0:128], expg[:, c, :], cst["ident"])
                em.barrier(["ve"], ["pe", "sy"])
                em.vcopy(eg_hm[:, :], em.pb[2][0:64, 0:128])
                em.barrier(["pe"], ["ve"])
                for fb in range(8):
                    em.mm(em.bank(4), csave[:, c, :], S_in_b[:, fb * 512:(fb + 1) * 512])
                    em.mm(em.bank(5), eg_hm[:, :], cst["eexp_b"][:, fb * 512:(fb + 1) * 512])
                    em.barrier(["ve"], ["pe"])
                    em.vcopy(expE[:, :], em.bank(5))
                    em.tt(tscr[:, :], em.bank(4), expE[:, :], ALU.mult)
                    em.tt(y_sb[:, fb * 512:(fb + 1) * 512], y_sb[:, fb * 512:(fb + 1) * 512],
                          tscr[:, :], ALU.add)
                    em.barrier(["pe"], ["ve"])
                for q in range(8):
                    for jj in range(4):
                        em.trans(em.bank_b(6 + (q % 2))[:, jj * 128:(jj + 1) * 128],
                                 y_sb[:, (q * 4 + jj) * 128:(q * 4 + jj + 1) * 128],
                                 cst["ident_b"])
                    em.barrier(["ve"], ["pe"])
                    em.vcopy(y_fm[:, q * 4:(q + 1) * 4, :].rearrange("p a b -> p (a b)"),
                             em.bank_b(6 + (q % 2)))
                    em.barrier(["pe"], ["ve"])
                em.tt(y_fm[:, :, :], y_fm[:, :, :],
                      z_s[:, :, ci * 128:(ci + 1) * 128], ALU.mult)
                sqf = y_sb[:, 0:4096].rearrange("p (a b) -> p a b", a=32)
                em.tt(sqf, y_fm[:, :, :], y_fm[:, :, :], ALU.mult)
                em.barrier(["pe"], ["ve"])
                for fc in range(32):
                    em.mm(em.bank(2, 128)[0:1, :], cst["ones_col_b"], sqf[:, fc, :],
                          start=(fc == 0), stop=(fc == 31))
                em.barrier(["ve"], ["pe"])
                emit_rsqrt(em, rrow[:, 1, :], em.bank(2, 128)[0:1, :],
                           rrow[:, 2, :], rrow[:, 3, :], scale=1.0 / DIN, eps=EPS)
                em.barrier(["pe"], ["ve"])
                em.mm(em.bank(3, 128), cst["ones_row"], rrow[:, 1, :])
                em.barrier(["ve"], ["pe"])
                em.tt(y_fm[:, :, :], y_fm[:, :, :], rep_ap(nw[:, :], 2, 128), ALU.mult)
                em.tt(yn_s[:, :, ci * 128:(ci + 1) * 128], y_fm[:, :, :],
                      rep_ap(em.bank(3, 128), 1, 32), ALU.mult)
                em.barrier(["pe"], ["ve"])
            em.barrier()
            em.dma("po", wgout[:, 0, :], wout_d[0])
            nd0 = em.cnt["dpo"] - 16
            ve_done, sy_done = {}, {}
            for m in range(KC):
                if m + 1 < KC:
                    em.wait_cnt("po", "pe", pe_done.get(100 + m - 1, 0))
                    em.dma("po", wgout[:, (m + 1) % 2, :], wout_d[m + 1])
                em.wait_cnt("sy", "ve", ve_done.get(m - 2, 0))
                em.dma("sy", xstg[:, m % 2, :], xT_d[m * 128:(m + 1) * 128, TPAD + t0:TPAD + t0 + 512])
                sy_done[m] = em.cnt["dsy"]
                em.wait_cnt("pe", "dpo", nd0 + 16 * (m + 1))
                em.wait_cnt("pe", "ve", ve_done.get(m - 2, 0))
                for k in range(32):
                    em.mm(em.bank(4 + m % 2), wgout[:, m % 2, k * 128:(k + 1) * 128],
                          yn_s[:, k, :], start=(k == 0), stop=(k == 31))
                pe_done[100 + m] = em.mark("pe")
                em.wait_cnt("ve", "pe", pe_done[100 + m])
                em.wait_cnt("ve", "dsy", sy_done[m])
                em.tt(ostg[:, m % 2, :], em.bank(4 + m % 2), xstg[:, m % 2, :], ALU.add)
                ve_done[m] = em.mark("ve")
                em.wait_cnt("sy", "ve", ve_done[m])
                em.dma("sy", x1_o[m * 128:(m + 1) * 128, t0:t0 + 512], ostg[:, m % 2, :])
            em.barrier()


def prep_launchB(inputs, resA):
    W_in = np.asarray(inputs["L0_attn_W_in"], np.float32)
    wz = prep_lhsT(W_in[:, 0:DIN])
    wout = prep_lhsT(np.asarray(inputs["L0_attn_W_out"], np.float32))
    nw = np.ascontiguousarray(
        np.asarray(inputs["L0_attn_norm_w"], np.float32).reshape(32, 128).T)
    base = {"wz": wz, "wout": wout, "nw": nw}
    base.update(make_consts())
    xs = shard_inputs(np.asarray(inputs["L0_attn_hidden_states"], np.float32))
    maps = []
    for k in range(8):
        g = k // 4
        grp = SEQ_GROUPS[g]
        S_all = np.stack([resA[j]["S_o"] for j in grp])
        offs_all = np.concatenate([resA[j]["offs_o"][0] for j in grp]).reshape(1, 4 * NH)
        pmv = np.zeros((1, 4), np.float32)
        pmv[0, :k % 4] = 1.0
        maps.append(dict(base, xT=xs[k], ln0_i=resA[k]["ln0_o"], y_i=resA[k]["y_o"],
                         S_all=S_all, offs_all=offs_all, posmask=pmv,
                         csave_i=resA[k]["csave_o"], expg_i=resA[k]["expg_o"]))
    return maps




def declare_io_ffn(nc, kind, pref=""):
    FHC = FH // 128
    ios = {}
    ios["xT_d"] = nc.dram_tensor(pref + "xT", [D, T], F32, kind="ExternalInput").ap()
    ios["xo_d"] = nc.dram_tensor(pref + "xo", [D, T], F32, kind="ExternalOutput").ap()
    ios["wdn_d"] = nc.dram_tensor(pref + "wdn", [KC, 128, FH], F32, kind="ExternalInput").ap()
    if kind == "sign":
        ios["wup_d"] = nc.dram_tensor(pref + "wup", [FHC, 128, D], F32, kind="ExternalInput").ap()
        ios["upsc_d"] = nc.dram_tensor(pref + "upsc", [128, FHC], F32, kind="ExternalInput").ap()
        ios["dnsc_d"] = nc.dram_tensor(pref + "dnsc", [128, KC], F32, kind="ExternalInput").ap()
        ios["osc_d"] = nc.dram_tensor(pref + "osc", [1, 1], F32, kind="ExternalInput").ap()
    else:
        ios["wga_d"] = nc.dram_tensor(pref + "wga", [FHC, 128, D], F32, kind="ExternalInput").ap()
        ios["wup_d"] = nc.dram_tensor(pref + "wup", [FHC, 128, D], F32, kind="ExternalInput").ap()
    return ios


def build_ffn(kind):
    nc = bass.Bass("TRN2", target_bir_lowering=False)
    stack = ExitStack()
    with stack:
        em = Em(nc, stack)
        cst = declare_consts(em, nc, names=[
            "c_ones_col", "c_ones_col_b", "c_ones_row"])
        ios = declare_io_ffn(nc, kind)
        emit_ffn(em, nc, cst, ios, kind)
        em.fence_out()
    return nc


def emit_ffn(em, nc, cst, ios, kind):
    FHC = FH // 128
    xT_d = ios["xT_d"]; xo_d = ios["xo_d"]; wdn_d = ios["wdn_d"]
    wup_d = ios["wup_d"]
    if kind == "sign":
        upsc_d = ios["upsc_d"]; dnsc_d = ios["dnsc_d"]; osc_d = ios["osc_d"]
    else:
        wga_d = ios["wga_d"]
    TW = 512
    ln1 = em.sb("ln1", [128, KC, T], BF16)
    if kind == "sign":
        upsc = em.sb("upscC", [128, FHC], F32)
        dsos = em.sb("dsosC", [128, KC], F32)
        osc = em.sb("oscC", [1, 1], F32)
        em.dma("sy", upsc[:, :], upsc_d)
        em.dma("sy", dsos[:, :], dnsc_d)
        em.dma("sy", osc[:], osc_d)

    sc_x = ExitStack()
    x_sb = sc_x.enter_context(nc.sbuf_tensor(em.name("x_sbC"), [128, KC, T], F32, side="right"))
    scr = {
        "sqf": sc_x.enter_context(nc.sbuf_tensor(em.name("sqfC"), [128, KC, 512], BF16, side="right")),
        "row": sc_x.enter_context(nc.sbuf_tensor(em.name("rowC"), [1, 4, 512], F32, side="right")),
        "sc512": sc_x.enter_context(nc.sbuf_tensor(em.name("sc512C"), [128, 512], F32, side="right")),
    }
    for k in range(KC):
        em.dma("sy", x_sb[:, k, :], xT_d[k * 128:(k + 1) * 128, :])
    em.barrier()
    if kind == "sign":
        em.mm(em.bank(0, 1), cst["ones_row"], osc[:])
        em.barrier(["ve"], ["pe", "sy"])
        em.tt(dsos[:, :], dsos[:, :], em.bank(0, 1).broadcast_to([128, KC]), ALU.mult)
        em.barrier(["pe"], ["ve"])
    emit_ln(em, x_sb, ln1, cst, scr, T)
    sc_x.close()
    h_s = em.sb("h_s", [128, FHC, TW], BF16)
    NPSL = 3 if kind == "sign" else 2
    wgran = em.sb("wgranC", [128, NPSL, 2, D], BF16)
    wgran2 = em.sb("wgran2C", [128, NPSL, 2, D], BF16) if kind == "swiglu" else wgran
    wdng3 = em.sb("wdngC", [128, 3, FH], BF16)
    xstg = em.sb("xstgC", [128, 2, TW], F32)
    ostg = xstg
    sgs = em.sb("sgsC", [128, 2, TW], BF16)

    nper = 32 if kind == "swiglu" else 16
    NP = FHC // 2
    for ti in range(T // TW):
        t0 = ti * TW
        em.barrier()
        for pp in range(min(NPSL, NP)):
            em.dma("po", wgran[:, pp, :, :],
                   wup_d[2 * pp:2 * pp + 2].rearrange("s p j -> p s j"))
            if kind == "swiglu":
                em.dma("po", wgran2[:, pp, :, :],
                       wga_d[2 * pp:2 * pp + 2].rearrange("s p j -> p s j"))
        nd0 = em.cnt["dpo"] - NPSL * (nper // 16) * 16
        pe_done, ve_done, ac_done = {}, {}, {}
        def up_prefetch(p):
            if p + NPSL < NP:
                em.wait_cnt("po", "pe", pe_done[2 * p + 1])
                em.dma("po", wgran[:, p % NPSL, :, :],
                       wup_d[2 * (p + NPSL):2 * (p + NPSL) + 2].rearrange("s p j -> p s j"))
                if kind == "swiglu":
                    em.dma("po", wgran2[:, p % NPSL, :, :],
                           wga_d[2 * (p + NPSL):2 * (p + NPSL) + 2].rearrange("s p j -> p s j"))

        for m in range(FHC):
            p, sl = m // 2, (m // 2) % NPSL
            dready = nd0 + (nper // 16) * 16 * (p + 1)
            wsl = wgran[:, p % NPSL, m % 2, :]
            wsl2 = wgran2[:, p % NPSL, m % 2, :]
            bk = 4 + m % 2
            if kind == "sign":
                em.wait_cnt("ac", "dpo", dready)
                em.wait_cnt("ac", "pe", pe_done.get(m - 4, 0))
                em.act(wsl, wsl, AF.Sign)
                ac_done[m] = em.mark("ac")
                em.wait_cnt("pe", "ac", ac_done[m])
                em.wait_cnt("pe", "ve", ve_done.get(m - 2, 0))
                for k in range(KC):
                    em.mm(em.bank(bk, TW), wsl[:, k * 128:(k + 1) * 128],
                          ln1[:, k, t0:t0 + TW], start=(k == 0), stop=(k == KC - 1))
                pe_done[m] = em.mark("pe")
                if m % 2 == 1:
                    up_prefetch(p)
                em.wait_cnt("ve", "pe", pe_done[m])
                em.ts(h_s[:, m, :], em.bank(bk, TW), upsc[:, m:m + 1], 0.0,
                      ALU.mult, ALU.max)
                ve_done[m] = em.mark("ve")
            else:
                bk2 = 4 + (m % 2) * 2
                em.wait_cnt("pe", "dpo", dready)
                em.wait_cnt("pe", "ve", ve_done.get(m - 2, 0))
                em.wait_cnt("pe", "ac", ac_done.get(m - 2, 0))
                for k in range(KC):
                    em.mm(em.bank(bk2, TW), wsl2[:, k * 128:(k + 1) * 128],
                          ln1[:, k, t0:t0 + TW], start=(k == 0), stop=(k == KC - 1))
                gm = em.mark("pe")
                for k in range(KC):
                    em.mm(em.bank(bk2 + 1, TW), wsl[:, k * 128:(k + 1) * 128],
                          ln1[:, k, t0:t0 + TW], start=(k == 0), stop=(k == KC - 1))
                pe_done[m] = em.mark("pe")
                if m % 2 == 1:
                    up_prefetch(p)
                em.wait_cnt("ac", "pe", gm)
                em.wait_cnt("ac", "ve", ve_done.get(m - 2, 0))
                em.act(sgs[:, m % 2, :], em.bank(bk2, TW), AF.Silu)
                ac_done[m] = em.mark("ac")
                em.wait_cnt("ve", "pe", pe_done[m])
                em.wait_cnt("ve", "ac", ac_done[m])
                em.tt(h_s[:, m, :], sgs[:, m % 2, :], em.bank(bk2 + 1, TW), ALU.mult)
                ve_done[m] = em.mark("ve")
        em.barrier()
        for sl in range(3):
            em.dma("po", wdng3[:, sl, :], wdn_d[sl])
        nd0 = em.cnt["dpo"] - 3 * 16
        pe_done, ve_done, ac_done, sy_done = {}, {}, {}, {}
        for m in range(KC):
            em.wait_cnt("sy", "ve", ve_done.get(m - 2, 0))
            em.dma("sy", xstg[:, m % 2, :], xT_d[m * 128:(m + 1) * 128, t0:t0 + TW])
            sy_done[m] = em.cnt["dsy"]
            if kind == "sign":
                em.wait_cnt("ac", "dpo", nd0 + 16 * (m + 1))
                em.wait_cnt("ac", "pe", pe_done.get(m - 3, 0))
                em.act(wdng3[:, m % 3, :], wdng3[:, m % 3, :], AF.Sign)
                ac_done[m] = em.mark("ac")
                em.wait_cnt("pe", "ac", ac_done[m])
            else:
                em.wait_cnt("pe", "dpo", nd0 + 16 * (m + 1))
            em.wait_cnt("pe", "ve", ve_done.get(m - 2, 0))
            bk = 4 + m % 2
            for k in range(FHC):
                em.mm(em.bank(bk, TW), wdng3[:, m % 3, k * 128:(k + 1) * 128],
                      h_s[:, k, :], start=(k == 0), stop=(k == FHC - 1))
            pe_done[m] = em.mark("pe")
            if m + 3 < KC:
                em.wait_cnt("po", "pe", pe_done[m])
                em.dma("po", wdng3[:, (m + 3) % 3, :], wdn_d[m + 3])
            em.wait_cnt("ve", "pe", pe_done[m])
            em.wait_cnt("ve", "dsy", sy_done[m])
            if kind == "sign":
                em.stt(ostg[:, m % 2, :], em.bank(bk, TW), dsos[:, m:m + 1],
                       xstg[:, m % 2, :], ALU.mult, ALU.add)
            else:
                em.tt(ostg[:, m % 2, :], em.bank(bk, TW), xstg[:, m % 2, :], ALU.add)
            ve_done[m] = em.mark("ve")
            em.wait_cnt("sy", "ve", ve_done[m])
            em.dma("sy", xo_d[m * 128:(m + 1) * 128, t0:t0 + TW], ostg[:, m % 2, :])
        em.barrier()


def prep_ffn0(inputs, x1s):
    wup = prep_lhsT(np.asarray(inputs["L0_ffn_up_weight"], np.float32).T)
    wdn = prep_lhsT(np.asarray(inputs["L0_ffn_down_weight"], np.float32).T)
    upsc = np.ascontiguousarray(
        np.asarray(inputs["L0_ffn_up_scale"], np.float32).reshape(FH // 128, 128).T)
    dnsc = np.ascontiguousarray(
        np.asarray(inputs["L0_ffn_down_scale"], np.float32).reshape(KC, 128).T)
    osc = np.asarray(inputs["L0_ffn_output_scale"], np.float32).reshape(1, 1)
    base = {"wup": wup, "wdn": wdn, "upsc": upsc, "dnsc": dnsc, "osc": osc}
    base.update(make_consts())
    return [dict(base, xT=x1s[k]) for k in range(8)]


def prep_ffn1(inputs, x3s):
    wga = prep_lhsT(np.asarray(inputs["L1_ffn_w_gate"], np.float32).T)
    wup = prep_lhsT(np.asarray(inputs["L1_ffn_w_up"], np.float32).T)
    wdn = prep_lhsT(np.asarray(inputs["L1_ffn_w_down"], np.float32).T)
    base = {"wga": wga, "wup": wup, "wdn": wdn}
    base.update(make_consts())
    return [dict(base, xT=x3s[k]) for k in range(8)]




CONSTS_D = ["c_ident", "c_ident_b", "c_tri16", "c_triu01_b",
            "c_ones_col", "c_ones_col_b", "c_ones_row"]


def declare_ioD(nc, ext=True):
    ki = dict(kind="ExternalInput")
    ko = dict(kind="ExternalOutput") if ext else dict(kind="Internal")
    ios = {}
    if ext:
        ios["xT_d"] = nc.dram_tensor("xT", [D, T], F32, **ki).ap()
    ios["wq_d"] = nc.dram_tensor("wq", [KC, 128, D], F32, **ki).ap()
    ios["wk_d"] = nc.dram_tensor("wk", [KC, 128, D], F32, **ki).ap()
    ios["wv_d"] = nc.dram_tensor("wv", [8, 128, D], F32, **ki).ap()
    ios["wgk1_d"] = nc.dram_tensor("wgk1", [1, 128, D], F32, **ki).ap()
    ios["wgk2_d"] = nc.dram_tensor("wgk2", [16, 128, 128], F32, **ki).ap()
    ios["bg_d"] = nc.dram_tensor("bg", [128, 16], F32, **ki).ap()
    ios["ln2_o"] = nc.dram_tensor("ln2_o", [D, T], BF16, **ko).ap()
    ios["o_o"] = nc.dram_tensor("o_o", [GCH, 128, 1024], BF16, **ko).ap()
    ios["qe_om"] = nc.dram_tensor("qe_o", [MCH, 128, 16 * 128], BF16, **ko).ap()
    ios["Sg_o"] = nc.dram_tensor("Sg_o", [128, 4096], F32, **ko).ap()
    ios["L_o"] = nc.dram_tensor("L_o", [128, 16], F32, **ko).ap()
    if ext:
        ios["bts_dbg"] = nc.dram_tensor("bts_dbg", [128, 2048], F32, **ko).ap()
        ios["spt_dbg"] = nc.dram_tensor("spt_dbg", [128, 2048], F32, **ko).ap()
        ios["low_dbg"] = nc.dram_tensor("low_dbg", [128, 512], BF16, **ko).ap()
        ios["gkp_dbg"] = nc.dram_tensor("gkp_dbg", [128, 128], F32, **ko).ap()
        ios["qec_dbg"] = nc.dram_tensor("qec_dbg", [128, 16, 128], BF16, **ko).ap()
    return ios


def build_launchD():
    nc = bass.Bass("TRN2", target_bir_lowering=False)
    stack = ExitStack()
    with stack:
        em = Em(nc, stack)
        cst = declare_consts(em, nc, names=CONSTS_D)
        ios = declare_ioD(nc, True)
        emit_stageD(em, nc, cst, ios)
        em.fence_out()
    return nc


def emit_stageD(em, nc, cst, ios, gin=None):
    xT_d = ios["xT_d"]; wq_d = ios["wq_d"]; wk_d = ios["wk_d"]
    wv_d = ios["wv_d"]; wgk1_d = ios["wgk1_d"]; wgk2_d = ios["wgk2_d"]
    bg_d = ios["bg_d"]; ln2_o = ios["ln2_o"]; o_o = ios["o_o"]
    qe_om = ios["qe_om"]; Sg_o = ios["Sg_o"]; L_o = ios["L_o"]
    bts_dbg = ios.get("bts_dbg"); spt_dbg = ios.get("spt_dbg")
    low_dbg = ios.get("low_dbg"); gkp_dbg = ios.get("gkp_dbg")
    qec_dbg = ios.get("qec_dbg")
    if True:

        ln2 = em.sb("ln2", [128, KC, T], BF16)
        Lcum = em.sb("LcumD", [128, 16], F32)
        bl = em.sb("blD", [128, 16], F32)
        ebl = em.sb("eblD", [128, 16], F32)
        eL = em.sb("eLD", [128, 16], F32)
        wgk2 = em.sb("wgk2D", [128, 16, 128], BF16)
        bg = em.sb("bgD", [128, 16], F32)
        nbg = em.sb("nbgD", [128, 16], F32)

        sc_x = ExitStack()
        x_sb = sc_x.enter_context(nc.sbuf_tensor(em.name("x_sbD"), [128, KC, T], F32, side="right"))
        scr = {
            "sqf": sc_x.enter_context(nc.sbuf_tensor(em.name("sqfD"), [128, KC, 512], BF16, side="right")),
            "row": sc_x.enter_context(nc.sbuf_tensor(em.name("rowD"), [1, 4, 512], F32, side="right")),
            "sc512": sc_x.enter_context(nc.sbuf_tensor(em.name("sc512D"), [128, 512], F32, side="right")),
        }
        for k in range(KC):
            em.dma("sy", x_sb[:, k, :], xT_d[k * 128:(k + 1) * 128, :])
        em.dma("sy", bg[:, :], bg_d)
        em.barrier()
        for mq in range(16):
            em.dma("po", wgk2[:, mq, :], wgk2_d[mq])
        em.ts(nbg[:, :], bg[:, :], -1.0, None, ALU.mult)
        em.memset(Lcum[:, :], 0.0)
        em.barrier()
        emit_ln(em, x_sb, ln2, cst, scr, T)
        sc_x.close()
        for k in range(KC):
            em.dma("sy", ln2_o[k * 128:(k + 1) * 128, :], ln2[:, k, :])
        qf = em.sb("qf", [128, 16, 512], BF16)
        kf = em.sb("kf", [128, 16, 512], BF16)
        vf = em.sb("vf", [128, 8, 512], BF16)
        qep_c = em.sb("qep_c", [128, 16, 128], BF16)
        o_sb = em.sb("o_sbD", [128, GCH, 1024], BF16)
        S = em.sb("SgD", [128, 4096], F32)
        S_b = em.sb("SgD_b", [128, 4096], BF16)
        low_b = em.sb("lowbD", [128, 512], BF16)
        sp_t = em.sb("sptD", [128, 2048], F32)
        b_ts = em.sb("btsD", [128, 2048], F32)
        expb_c = em.sb("expbD", [128, 16, 128], BF16)
        spfm = em.sb("spfmD", [128, 16, 128], F32)
        kd_c = em.sb("kdD", [128, 16, 128], BF16)
        qe_c = em.sb("qeD", [128, 16, 128], BF16)
        k2_tm = em.sb("k2tmD", [128, 16, 128], BF16)
        v_tm = em.sb("vtmD", [128, 8, 128], BF16)
        at_m = em.sb("atmD", [128, 4, 128], BF16)
        enb_c = k2_tm

        f32s = em.sb("f32sD", [128, 512], F32)
        sq_q = em.sb("sqqD", [128, 128], F32)
        wgran = em.sb("wgranD", [128, 2, D], BF16)
        em.memset(S[:, :], 0.0)
        em.memset(S_b[:, :], 0.0)

        pe_done = {}
        for ti in range(2):
            t0 = ti * 512
            for (tag, wd, nmo, dest) in (("q", wq_d, 16, qf), ("k", wk_d, 16, kf),
                                         ("v", wv_d, 8, vf)):
                em.barrier(["po"], ["pe", "ve"])
                em.dma("po", wgran[:, 0, :], wd[0])
                nd0 = em.cnt["dpo"] - 16
                ve_done = {}
                for m in range(nmo):
                    if m + 1 < nmo:
                        em.wait_cnt("po", "pe", pe_done.get(m - 1, 0))
                        em.dma("po", wgran[:, (m + 1) % 2, :], wd[m + 1])
                    em.wait_cnt("pe", "dpo", nd0 + 16 * (m + 1))
                    em.wait_cnt("pe", "ve", ve_done.get(m - 2, 0))
                    for k in range(KC):
                        em.mm(em.bank(4 + m % 2), wgran[:, m % 2, k * 128:(k + 1) * 128],
                              ln2[:, k, t0:t0 + 512], start=(k == 0), stop=(k == KC - 1))
                    pe_done[m] = em.mark("pe")
                    em.wait_cnt("ve", "pe", pe_done[m])
                    em.vcopy(dest[:, m, :], em.bank(4 + m % 2))
                    ve_done[m] = em.mark("ve")
            em.barrier(["po"], ["pe"])
            em.dma("po", wgran[:, 0, :], wgk1_d[0])
            em.wait_val("pe", "dpo", em.cnt["dpo"])
            em.barrier(["pe"], ["ve"])
            for k in range(KC):
                em.mm(em.bank(6), wgran[:, 0, k * 128:(k + 1) * 128],
                      ln2[:, k, t0:t0 + 512], start=(k == 0), stop=(k == KC - 1))
            em.barrier(["ve"], ["pe"])
            em.vcopy(low_b[:, :], em.bank(6))
            em.barrier(["pe"], ["ve"])

            for tc in range(4):
                c = ti * 4 + tc
                em.barrier()
                for f in range(16):
                    em.mm(em.bank(0 + f % 2, 128), wgk2[:, f, :],
                          low_b[:, tc * 128:(tc + 1) * 128],)
                    pmf = em.mark("pe")
                    em.wait_cnt("ve", "pe", pmf)
                    em.ts(spfm[:, f, :], em.bank(0 + f % 2, 128), bg[:, f:f + 1],
                          None, ALU.add)
                    vmf = em.mark("ve")
                    em.wait_cnt("pe", "ve", vmf)
                aw = b_ts[:, :]
                spw = spfm[:, :, :].rearrange("p a b -> p (a b)")
                em.barrier(["ac"], ["ve"])
                em.act(aw, spw, AF.Abs)
                em.act(aw, aw, AF.Exp, scale=-1.0)
                em.act(aw, aw, AF.Ln, bias=1.0)
                em.barrier(["ve"], ["ac"])
                f32w = sp_t[:, :]
                em.ts(f32w, spw, 0.0, None, ALU.min)
                em.stt(spw, f32w, -1.0, aw, ALU.mult, ALU.add)
                em.barrier(["pe"], ["ve"])
                for f in range(16):
                    em.trans(em.bank(0 + f % 2, 128), spfm[:, f, :], cst["ident"])
                    pmf = em.mark("pe")
                    em.wait_cnt("ve", "pe", pmf)
                    em.vcopy(sp_t[:, f * 128:(f + 1) * 128], em.bank(0 + f % 2, 128))
                    vmf = em.mark("ve")
                    em.wait_cnt("pe", "ve", vmf)
                for s4 in range(4):
                    em.mm(em.bank(2), cst["tri16"], sp_t[:, s4 * 512:(s4 + 1) * 512])
                    em.barrier(["ve"], ["pe"])
                    em.vcopy(b_ts[:, s4 * 512:(s4 + 1) * 512], em.bank(2))
                    em.barrier(["pe"], ["ve"])
                for f in range(16):
                    em.trans(em.bank(2 + f % 2, 128), b_ts[:, f * 128:(f + 1) * 128],
                             cst["ident"])
                    pmf = em.mark("pe")
                    em.wait_cnt("ve", "pe", pmf)
                    em.vcopy(spfm[:, f, :], em.bank(2 + f % 2, 128))
                    vmf = em.mark("ve")
                    em.wait_cnt("pe", "ve", vmf)
                em.vcopy(bl[:, :], spfm[:, :, 127])
                em.barrier(["ac"], ["ve"])
                em.act(expb_c[:, :, :].rearrange("p a b -> p (a b)"),
                       spfm[:, :, :].rearrange("p a b -> p (a b)"), AF.Exp)
                em.act(enb_c[:, :, :].rearrange("p a b -> p (a b)"),
                       spfm[:, :, :].rearrange("p a b -> p (a b)"), AF.Exp, scale=-1.0)
                em.barrier(["ve"], ["ac"])
                em.tt(kd_c[:, :, :], kf[:, :, tc * 128:(tc + 1) * 128], enb_c[:, :, :],
                      ALU.mult)
                em.stt(qe_c[:, :, :], qf[:, :, tc * 128:(tc + 1) * 128],
                       float(1.0 / np.sqrt(512.0)), expb_c[:, :, :], ALU.mult, ALU.mult)
                em.barrier(["pe", "ac"], ["ve"])
                if DBG and c == 0 and bts_dbg is not None:
                    em.barrier(["sy"], ["ve", "ac"])
                    em.dma("sy", bts_dbg, b_ts[:, :])
                    em.dma("sy", spt_dbg, sp_t[:, :])
                    em.dma("sy", low_dbg, low_b[:, :])
                em.barrier(["ac"], ["ve", "pe", "ac"])
                em.act(ebl[:, :], bl[:, :], AF.Exp)
                em.act(eL[:, :], Lcum[:, :], AF.Exp)
                em.barrier(["ve"], ["ac"])
                em.tt(k2_tm[:, :, :], kd_c[:, :, :], rep_ap(ebl[:, :], 2, 128), ALU.mult)
                em.tt(qep_c[:, :, :], qe_c[:, :, :], rep_ap(eL[:, :], 2, 128), ALU.mult)
                em.tt(Lcum[:, :], Lcum[:, :], bl[:, :], ALU.add)
                em.barrier(["sy"], ["ve"])
                em.dma("sy", qe_om[c], qep_c[:, :, :].rearrange("p a b -> p (a b)"))
                if DBG and c == 0 and qec_dbg is not None:
                    em.barrier(["sy"], ["ve"])
                    em.dma("sy", qec_dbg, qe_c[:, :, :])
                em.barrier(["pe"], ["ve"])
                for f in range(16):
                    em.trans(em.bank_b(0 + (f % 2))[:, 0:128], k2_tm[:, f, :], cst["ident_b"])
                    em.barrier(["ve"], ["pe"])
                    em.vcopy(k2_tm[:, f, :], em.bank_b(0 + (f % 2))[:, 0:128])
                    em.barrier(["pe"], ["ve"])
                for vc in range(8):
                    em.trans(em.bank_b(0 + (vc % 2))[:, 0:128],
                             vf[:, vc, tc * 128:(tc + 1) * 128], cst["ident_b"])
                    em.barrier(["ve"], ["pe"])
                    em.vcopy(v_tm[:, vc, :], em.bank_b(0 + (vc % 2))[:, 0:128])
                    em.barrier(["pe"], ["ve"])
                for h in range(GH):
                    for dkc in range(4):
                        em.mm(em.bank(2)[:, h * 128:(h + 1) * 128],
                              kd_c[:, h * 4 + dkc, :], qe_c[:, h * 4 + dkc, :],
                              start=(dkc == 0), stop=(dkc == 3))
                em.barrier(["ve"], ["pe"])
                em.ts(f32s[:, :], em.bank(2), 1e30, -1e30, ALU.min, ALU.max)
                em.tt(at_m[:, :, :], f32s[:, :].rearrange("p (a b) -> p a b", a=4),
                      rep_ap(cst["triu01_b"], 1, 4), ALU.mult)
                em.barrier(["pe"], ["ve"])
                for h in range(GH):
                    bk = 4 + h // 2
                    off = (h % 2) * 256
                    for dkc in range(4):
                        em.mm(em.bank(bk)[:, off:off + 256],
                              qe_c[:, h * 4 + dkc, :],
                              S_b[:, (h * 4 + dkc) * 256:(h * 4 + dkc + 1) * 256],
                              start=(dkc == 0), stop=False)
                    em.mm(em.bank(bk)[:, off:off + 256], at_m[:, h, :],
                          v_tm[:, 2 * h:2 * h + 2, :], start=False, stop=True)
                em.barrier(["ve"], ["pe"])
                em.vcopy(o_sb[:, c, 0:512], em.bank(4))
                em.vcopy(o_sb[:, c, 512:1024], em.bank(5))
                em.barrier(["pe"], ["ve"])
                for h in range(GH):
                    for dkc in range(4):
                        i2 = (h * 4 + dkc)
                        em.mm(em.bank(6)[:, (i2 % 2) * 256:(i2 % 2) * 256 + 256],
                              k2_tm[:, i2, :], v_tm[:, 2 * h:2 * h + 2, :])
                        em.barrier(["ve"], ["pe"])
                        em.ts(f32s[:, 0:256], S[:, i2 * 256:(i2 + 1) * 256],
                              ebl[:, i2:i2 + 1], None, ALU.mult)
                        em.tt(S[:, i2 * 256:(i2 + 1) * 256], f32s[:, 0:256],
                              em.bank(6)[:, (i2 % 2) * 256:(i2 % 2) * 256 + 256], ALU.add)
                        em.barrier(["pe"], ["ve"])
                em.vcopy(S_b[:, :], S[:, :])
                em.barrier(["pe", "ac", "sy"], ["ve"])

        em.barrier(["sy"], ["ve", "ac", "pe"])
        for c in range(GCH):
            em.dma("sy", o_o[c], o_sb[:, c, :])
        if gin is None:
            em.dma("sy", Sg_o, S[:, :])
            em.dma("sy", L_o, Lcum[:, :])
        else:
            em.dma("sy", gin[:, 0:4096], S[:, :])
            em.dma("sy", gin[:, 4096:4096 + 16], Lcum[:, :])


def prep_launchD(inputs, x2s):
    wq = prep_lhsT(np.asarray(inputs["L1_attn_Wq"], np.float32))
    wk = prep_lhsT(np.asarray(inputs["L1_attn_Wk"], np.float32))
    wv = prep_lhsT(np.asarray(inputs["L1_attn_Wv"], np.float32))
    wgk1 = prep_lhsT(np.asarray(inputs["L1_attn_Wgk1"], np.float32), pad_to=128)
    Wgk2 = np.asarray(inputs["L1_attn_Wgk2"], np.float32)
    wgk2 = np.zeros((16, 128, 128), np.float32)
    wgk2[:, 0:16, :] = Wgk2.reshape(16, 16, 128).transpose(1, 0, 2)
    bg = np.ascontiguousarray(
        np.asarray(inputs["L1_attn_bgk2"], np.float32).reshape(16, 128).T)
    base = {"wq": wq, "wk": wk, "wv": wv, "wgk1": wgk1, "wgk2": wgk2, "bg": bg}
    base.update(make_consts())
    return [dict(base, xT=x2s[k]) for k in range(8)]


CONSTS_E = ["c_ident", "c_ident_b", "c_ones_col", "c_ones_col_b", "c_ones_row"]


def build_launchE():
    nc = bass.Bass("TRN2", target_bir_lowering=False)
    stack = ExitStack()
    with stack:
        em = Em(nc, stack)
        cst = declare_consts(em, nc, names=CONSTS_E)
        ios = {}
        ios["xT_d"] = nc.dram_tensor("xT", [D, T], F32, kind="ExternalInput").ap()
        ios["ln2_o"] = nc.dram_tensor("ln2_i", [D, T], BF16, kind="ExternalInput").ap()
        ios["o_o"] = nc.dram_tensor("o_i", [GCH, 128, 1024], BF16, kind="ExternalInput").ap()
        ios["qe_om"] = nc.dram_tensor("qe_i", [MCH, 128, 16 * 128], BF16, kind="ExternalInput").ap()
        ios["Sall_d"] = nc.dram_tensor("Sg_all", [4, 128, 4096], F32, kind="ExternalInput").ap()
        ios["Lall_d"] = nc.dram_tensor("L_all", [4, 128, 16], F32, kind="ExternalInput").ap()
        ios["pme_d"] = nc.dram_tensor("posmask", [128, 4], F32, kind="ExternalInput").ap()
        ios["wg_d"] = nc.dram_tensor("wg", [8, 128, D], F32, kind="ExternalInput").ap()
        ios["wn_d"] = nc.dram_tensor("wn", [128, 2], F32, kind="ExternalInput").ap()
        ios["wo_d"] = nc.dram_tensor("wo", [KC, 128, 1024], F32, kind="ExternalInput").ap()
        ios["x3_o"] = nc.dram_tensor("x3_o", [D, T], F32, kind="ExternalOutput").ap()
        if DBG:
            ios["sin_dbg"] = nc.dram_tensor("sin_dbg", [128, 4096], F32, kind="ExternalOutput").ap()
            ios["ob_dbg"] = nc.dram_tensor("ob_dbg", [128, 1024], BF16, kind="ExternalOutput").ap()
            ios["ons_dbg"] = nc.dram_tensor("ons_dbg", [128, 8, 512], BF16, kind="ExternalOutput").ap()
            ios["ob0_dbg"] = nc.dram_tensor("ob0_dbg", [128, 1024], BF16, kind="ExternalOutput").ap()
            ios["ms_dbg"] = nc.dram_tensor("ms_dbg", [128, 4, 64], F32, kind="ExternalOutput").ap()
        emit_stageE(em, nc, cst, ios)
        em.fence_out()
    return nc


def emit_stageE(em, nc, cst, ios, gout=None, wait_cc=None):
    xT_d = ios["xT_d"]; ln2_d = ios["ln2_o"]; o_d = ios["o_o"]
    qe_d = ios["qe_om"]; pm_d = ios["pme_d"]; wg_d = ios["wg_d"]
    wn_d = ios["wn_d"]; wo_d = ios["wo_d"]; x3_o = ios["x3_o"]
    sin_dbg = ios.get("sin_dbg"); ob_dbg = ios.get("ob_dbg")
    ons_dbg = ios.get("ons_dbg"); ob0_dbg = ios.get("ob0_dbg")
    ms_dbg = ios.get("ms_dbg")
    if True:

        ln2 = em.sb("ln2E", [128, KC, T], BF16)
        o_sb = em.sb("o_sbE", [128, GCH, 1024], BF16)
        qe_2s = em.sb("qe2sE", [128, 2, 16, 128], BF16)
        S_in = em.sb("S_inE", [128, 4096], F32)
        S_in_b = em.sb("S_inE_b", [128, 4096], BF16)
        Lst = em.sb("LstE", [128, 4, 16], F32)
        aft = em.sb("aftE", [128, 5, 16], F32)
        pm = em.sb("pmE", [128, 4], F32)
        wn = em.sb("wnE", [128, 2], F32)
        sg_s = em.sb("sg_sE", [128, 2, 8, 512], BF16)
        on_s = em.sb("on_sE", [128, 8, 512], BF16)
        ob = em.sb("obE", [128, 1024], BF16)
        ofm = em.sb("ofmE", [128, 8, 128], BF16)
        ms = em.sb("msE", [128, 4, 64], F32)
        stage_S = em.sb("stageSE", [128, 4096], F32)
        wb16 = em.sb("wb16E", [128, 16], F32)
        wgran = em.sb("wgranE", [128, 2, D], BF16)
        wogr = em.sb("wogrE", [128, 2, 1024], BF16)
        xstg = em.sb("xstgE", [128, 2, 512], F32)
        ostg = em.sb("ostgE", [128, 2, 512], F32)

        for k in range(KC):
            em.dma("sy", ln2[:, k, :], ln2_d[k * 128:(k + 1) * 128, :])
        for c in range(GCH):
            em.dma("sy", o_sb[:, c, :], o_d[c])
        em.dma("sy", qe_2s[:, 0, :, :].rearrange("p a b -> p (a b)"), qe_d[0])
        if gout is None:
            for j in range(4):
                em.dma("sy", Lst[:, j, :], ios["Lall_d"][j])
        em.dma("sy", pm[:, :], pm_d)
        em.dma("sy", wn[:, :], wn_d)
        pe_done = {}

        def gproj(ti):
            t0 = ti * 512
            em.barrier()
            em.dma("po", wgran[:, 0, :], wg_d[0])
            nd0 = em.cnt["dpo"] - 16
            ac_done = {}
            for m in range(8):
                if m + 1 < 8:
                    em.wait_cnt("po", "pe", pe_done.get(m - 1, 0))
                    em.dma("po", wgran[:, (m + 1) % 2, :], wg_d[m + 1])
                em.wait_cnt("pe", "dpo", nd0 + 16 * (m + 1))
                em.wait_cnt("pe", "ac", ac_done.get(m - 2, 0))
                for k in range(KC):
                    em.mm(em.bank(4 + m % 2), wgran[:, m % 2, k * 128:(k + 1) * 128],
                          ln2[:, k, t0:t0 + 512], start=(k == 0), stop=(k == KC - 1))
                pe_done[m] = em.mark("pe")
                em.wait_cnt("ac", "pe", pe_done[m])
                em.act(sg_s[:, ti, m, :], em.bank(4 + m % 2), AF.Silu)
                ac_done[m] = em.mark("ac")
            em.barrier()

        gproj(0)
        gproj(1)
        if wait_cc is not None:
            wait_cc()
        if gout is not None:
            for j in range(4):
                em.dma("sy", Lst[:, j, :], gout[j * 128:(j + 1) * 128, 4096:4096 + 16])
        em.barrier()
        em.memset(aft[:, 3, :], 0.0)
        for j in (2, 1, 0):
            em.stt(aft[:, j, :], Lst[:, j + 1, :], pm[:, j + 1:j + 2],
                   aft[:, j + 1, :], ALU.mult, ALU.add)
        em.barrier(["ac"], ["ve"])
        for j in range(4):
            em.act(aft[:, j, :], aft[:, j, :], AF.Exp)
        em.barrier(["ve"], ["ac"])
        em.memset(S_in[:, :], 0.0)
        for j in range(4):
            em.ts(wb16[:, :], aft[:, j, :], pm[:, j:j + 1], None, ALU.mult)
            em.barrier(["sy"], ["ve"])
            if gout is None:
                em.dma("sy", stage_S[:, :], ios["Sall_d"][j])
            else:
                em.dma("sy", stage_S[:, :], gout[j * 128:(j + 1) * 128, 0:4096])
            em.barrier(["ve"], ["sy"])
            Sv = stage_S[:, :].rearrange("p (a b) -> p a b", a=16)
            em.tt(Sv, Sv, rep_ap(wb16[:, :], 2, 256), ALU.mult)
            em.tt(S_in[:, :], S_in[:, :], stage_S[:, :], ALU.add)
        em.vcopy(S_in_b[:, :], S_in[:, :])
        if DBG and sin_dbg is not None:
            em.barrier(["sy"], ["ve"])
            em.dma("sy", sin_dbg, S_in[:, :])
        em.barrier()

        for ti in range(2):
            t0 = ti * 512
            for tc in range(4):
                c = ti * 4 + tc
                em.barrier()
                if c + 1 < GCH:
                    em.dma("sy", qe_2s[:, (c + 1) % 2, :, :].rearrange("p a b -> p (a b)"),
                           qe_d[c + 1])
                em.wait_cnt("pe", "dsy", em.cnt["dsy"] - 16)
                for h in range(GH):
                    bk = 4 + h // 2
                    off = (h % 2) * 256
                    for dkc in range(4):
                        em.mm(em.bank(bk)[:, off:off + 256],
                              qe_2s[:, c % 2, h * 4 + dkc, :],
                              S_in_b[:, (h * 4 + dkc) * 256:(h * 4 + dkc + 1) * 256],
                              start=(dkc == 0), stop=(dkc == 3))
                em.barrier(["ve"], ["pe"])
                em.tt(ob[:, 0:512], o_sb[:, c, 0:512], em.bank(4), ALU.add)
                em.tt(ob[:, 512:1024], o_sb[:, c, 512:1024], em.bank(5), ALU.add)
                if DBG and c == 0 and ob0_dbg is not None:
                    em.barrier(["sy"], ["ve"])
                    em.dma("sy", ob0_dbg, ob[:, :])
                    em.barrier(["ve"], ["sy"])
                obv = ob[:, :].rearrange("p (h v) -> p h v", h=4)
                sqv = o_sb[:, c, :].rearrange("p (h v) -> p h v", h=4)
                em.tt(sqv, obv, obv, ALU.mult)
                em.memset(ms[:, 0, :], 1.0)
                for hh in range(GH):
                    em.reduce(ms[:, 0, hh:hh + 1], sqv[:, hh, :], ALU.add)
                em.barrier(["ve"], ["ve"])
                emit_rsqrt(em, ms[:, 1, :], ms[:, 0, :], ms[:, 2, :], ms[:, 3, :],
                           scale=1.0 / GDV, eps=EPS)
                em.tt(obv, obv, rep_ap(ms[:, 1, 0:4], 2, GDV), ALU.mult)
                em.barrier(["pe"], ["ve"])
                for vc in range(8):
                    em.trans(em.bank_b(0 + (vc % 2))[:, 0:128], ob[:, vc * 128:(vc + 1) * 128],
                             cst["ident_b"])
                    em.barrier(["ve"], ["pe"])
                    em.vcopy(ofm[:, vc, :], em.bank_b(0 + (vc % 2))[:, 0:128])
                    em.barrier(["pe"], ["ve"])
                if DBG and c == 0 and ob_dbg is not None:
                    em.barrier(["sy"], ["ve"])
                    em.dma("sy", ob_dbg, ob[:, :])
                    em.dma("sy", ms_dbg, ms[:, :, :])
                for vc in range(8):
                    em.stt(on_s[:, vc, tc * 128:(tc + 1) * 128], ofm[:, vc, :],
                           wn[:, vc % 2:vc % 2 + 1], sg_s[:, ti, vc, tc * 128:(tc + 1) * 128],
                           ALU.mult, ALU.mult)
                em.barrier(["pe", "sy"], ["ve"])
            if DBG and ti == 0 and ons_dbg is not None:
                em.barrier(["sy"], ["ve"])
                em.dma("sy", ons_dbg, on_s[:, :, :])
            em.barrier()
            em.dma("po", wogr[:, 0, :], wo_d[0])
            nd0 = em.cnt["dpo"] - 16
            ve_done, sy_done = {}, {}
            for m in range(KC):
                if m + 1 < KC:
                    em.wait_cnt("po", "pe", pe_done.get(300 + m - 1, 0))
                    em.dma("po", wogr[:, (m + 1) % 2, :], wo_d[m + 1])
                em.wait_cnt("sy", "ve", ve_done.get(m - 2, 0))
                em.dma("sy", xstg[:, m % 2, :], xT_d[m * 128:(m + 1) * 128, t0:t0 + 512])
                sy_done[m] = em.cnt["dsy"]
                em.wait_cnt("pe", "dpo", nd0 + 16 * (m + 1))
                em.wait_cnt("pe", "ve", ve_done.get(m - 2, 0))
                for k in range(8):
                    em.mm(em.bank(4 + m % 2), wogr[:, m % 2, k * 128:(k + 1) * 128],
                          on_s[:, k, :], start=(k == 0), stop=(k == 7))
                pe_done[300 + m] = em.mark("pe")
                em.wait_cnt("ve", "pe", pe_done[300 + m])
                em.wait_cnt("ve", "dsy", sy_done[m])
                em.tt(ostg[:, m % 2, :], em.bank(4 + m % 2), xstg[:, m % 2, :], ALU.add)
                ve_done[m] = em.mark("ve")
                em.wait_cnt("sy", "ve", ve_done[m])
                em.dma("sy", x3_o[m * 128:(m + 1) * 128, t0:t0 + 512], ostg[:, m % 2, :])
            em.barrier()


def prep_launchE(inputs, x2s, resD):
    wg = prep_lhsT(np.asarray(inputs["L1_attn_Wg"], np.float32))
    wo = prep_lhsT(np.asarray(inputs["L1_attn_Wo"], np.float32))
    wn = np.ascontiguousarray(
        np.asarray(inputs["L1_attn_Wnorm"], np.float32).reshape(2, 128).T)
    base = {"wg": wg, "wo": wo, "wn": wn}
    base.update(make_consts())
    maps = []
    for k in range(8):
        grp = SEQ_GROUPS[k // 4]
        S_all = np.stack([resD[j]["Sg_o"] for j in grp])
        L_all = np.stack([resD[j]["L_o"] for j in grp])
        pmv = np.zeros((128, 4), np.float32)
        pmv[:, :k % 4] = 1.0
        maps.append(dict(base, xT=x2s[k], ln2_i=resD[k]["ln2_o"], o_i=resD[k]["o_o"],
                         qe_i=resD[k]["qe_o"], Sg_all=S_all, L_all=L_all, posmask=pmv))
    return maps


def kernel(**inputs):
    return kernel_fused(**inputs)


def kernel_unfused(**inputs):
    inputs = {k: np.asarray(v) for k, v in inputs.items()}
    resA = run_bass_kernel_spmd(build_launchA(), prep_launchA(inputs),
                                core_ids=list(range(8))).results
    resB = run_bass_kernel_spmd(build_launchB(), prep_launchB(inputs, resA),
                                core_ids=list(range(8))).results
    x1s = [resB[k]["x1_o"] for k in range(8)]
    resC = run_bass_kernel_spmd(build_ffn("sign"), prep_ffn0(inputs, x1s),
                                core_ids=list(range(8))).results
    x2s = [resC[k]["xo"] for k in range(8)]
    resD = run_bass_kernel_spmd(build_launchD(), prep_launchD(inputs, x2s),
                                core_ids=list(range(8))).results
    resE = run_bass_kernel_spmd(build_launchE(), prep_launchE(inputs, x2s, resD),
                                core_ids=list(range(8))).results
    x3s = [resE[k]["x3_o"] for k in range(8)]
    resF = run_bass_kernel_spmd(build_ffn("swiglu"), prep_ffn1(inputs, x3s),
                                core_ids=list(range(8))).results
    out = np.empty((2, 4096, 2048), np.float32)
    for k in range(8):
        b, pos = k // 4, k % 4
        out[b, pos * T:(pos + 1) * T, :] = np.asarray(resF[k]["xo"]).T
    return out


def build_fused():
    nc = bass.Bass("TRN2", target_bir_lowering=False)
    stack = ExitStack()
    with stack:
        em = Em(nc, stack)
        cnames = sorted(set(CONSTS_A + CONSTS_B + CONSTS_D + CONSTS_E +
                            ["c_ones_col", "c_ones_col_b", "c_ones_row"]))
        cst = declare_consts(em, nc, names=cnames)
        TL = T + TPAD

        iosA = declare_ioA(nc, ext=False)
        kdbg = dict(kind="ExternalOutput") if DBG else {}
        x1_d = nc.dram_tensor("x1_int", [D, T], F32, **kdbg).ap()
        x2_d = nc.dram_tensor("x2_int", [D, T], F32, **kdbg).ap()
        x3_d = nc.dram_tensor("x3_int", [D, T], F32, **kdbg).ap()
        out_d = nc.dram_tensor("xo", [D, T], F32, kind="ExternalOutput").ap()
        gin_m = nc.dram_tensor("gin_m", [128, 4096 + NH], F32).ap()
        gout_m = nc.dram_tensor("gout_m", [512, 4096 + NH], F32).ap()
        if DBG:
            iosA["gdbg"] = nc.dram_tensor("gdbg", [512, 4096 + NH], F32, kind="ExternalOutput").ap()
            iosA["gindbg"] = nc.dram_tensor("gindbg", [128, 4096 + NH], F32, kind="ExternalOutput").ap()
        gin_g = nc.dram_tensor("gin_g", [128, 4096 + 16], F32).ap()
        gout_g = nc.dram_tensor("gout_g", [512, 4096 + 16], F32).ap()

        sc = em.push_scope()
        emit_stageA(em, nc, cst, iosA, "full", gin=gin_m)
        em.barrier()
        em.pop_scope()
        inst = nc.gpsimd.collective_compute(
            "AllGather", ALU.bypass, replica_groups=SEQ_GROUPS,
            ins=[gin_m], outs=[gout_m])
        inst.then_inc(em.sem["cc"])
        em.cnt["cc"] += 1
        ccv = em.cnt["cc"]
        if DBG and iosA.get("gindbg") is not None:
            em.dma("sy", iosA["gindbg"], gin_m)

        def wait_cc1():
            for e in Em.ENGS:
                em.eng[e].wait_ge(em.sem["cc"], ccv)
        iosB = {"xT_d": iosA["xT_d"], "ln0_o": iosA["ln0_o"], "y_o": iosA["y_o"],
                "csave_o": iosA["csave_o"], "expg_o": iosA["expg_o"],
                "gdbg": iosA.get("gdbg"),
                "wz_d": nc.dram_tensor("wz", [32, 128, D], F32, kind="ExternalInput").ap(),
                "wout_d": nc.dram_tensor("wout", [KC, 128, DIN], F32, kind="ExternalInput").ap(),
                "nw_d": nc.dram_tensor("nw", [128, 32], F32, kind="ExternalInput").ap(),
                "pmb_d": nc.dram_tensor("pmb", [1, 4], F32, kind="ExternalInput").ap(),
                "x1_o": x1_d}
        wait_cc1()
        sc = em.push_scope()
        emit_stageB(em, nc, cst, iosB, gout=gout_m, wait_cc=None)
        em.barrier()
        em.pop_scope()
        iosC = {"xT_d": x1_d, "xo_d": x2_d,
                "wup_d": nc.dram_tensor("wup0", [FH // 128, 128, D], F32, kind="ExternalInput").ap(),
                "wdn_d": nc.dram_tensor("wdn0", [KC, 128, FH], F32, kind="ExternalInput").ap(),
                "upsc_d": nc.dram_tensor("upsc", [128, FH // 128], F32, kind="ExternalInput").ap(),
                "dnsc_d": nc.dram_tensor("dnsc", [128, KC], F32, kind="ExternalInput").ap(),
                "osc_d": nc.dram_tensor("osc", [1, 1], F32, kind="ExternalInput").ap()}
        sc = em.push_scope()
        emit_ffn(em, nc, cst, iosC, "sign")
        em.barrier()
        em.pop_scope()
        iosD = declare_ioD(nc, ext=False)
        iosD["xT_d"] = x2_d
        sc = em.push_scope()
        emit_stageD(em, nc, cst, iosD, gin=gin_g)
        em.barrier()
        em.pop_scope()
        inst = nc.gpsimd.collective_compute(
            "AllGather", ALU.bypass, replica_groups=SEQ_GROUPS,
            ins=[gin_g], outs=[gout_g])
        inst.then_inc(em.sem["cc"])
        em.cnt["cc"] += 1
        ccv2 = em.cnt["cc"]

        def wait_cc2():
            for e in Em.ENGS:
                em.eng[e].wait_ge(em.sem["cc"], ccv2)
        iosE = {"xT_d": x2_d, "ln2_o": iosD["ln2_o"], "o_o": iosD["o_o"],
                "qe_om": iosD["qe_om"],
                "pme_d": nc.dram_tensor("pme", [128, 4], F32, kind="ExternalInput").ap(),
                "wg_d": nc.dram_tensor("wg", [8, 128, D], F32, kind="ExternalInput").ap(),
                "wn_d": nc.dram_tensor("wn", [128, 2], F32, kind="ExternalInput").ap(),
                "wo_d": nc.dram_tensor("wo", [KC, 128, 1024], F32, kind="ExternalInput").ap(),
                "x3_o": x3_d}
        wait_cc2()
        sc = em.push_scope()
        emit_stageE(em, nc, cst, iosE, gout=gout_g, wait_cc=None)
        em.barrier()
        em.pop_scope()
        iosF = {"xT_d": x3_d, "xo_d": out_d,
                "wga_d": nc.dram_tensor("wga1", [FH // 128, 128, D], F32, kind="ExternalInput").ap(),
                "wup_d": nc.dram_tensor("wup1", [FH // 128, 128, D], F32, kind="ExternalInput").ap(),
                "wdn_d": nc.dram_tensor("wdn1", [KC, 128, FH], F32, kind="ExternalInput").ap()}
        sc = em.push_scope()
        emit_ffn(em, nc, cst, iosF, "swiglu")
        em.pop_scope()
        em.fence_out()
    return nc


def prep_fused(inputs):
    mapsA = prep_launchA(inputs)
    W_in = np.asarray(inputs["L0_attn_W_in"], np.float32)
    base = {
        "wz": prep_lhsT(W_in[:, 0:DIN]),
        "wout": prep_lhsT(np.asarray(inputs["L0_attn_W_out"], np.float32)),
        "nw": np.ascontiguousarray(
            np.asarray(inputs["L0_attn_norm_w"], np.float32).reshape(32, 128).T),
        "wup0": prep_lhsT(np.asarray(inputs["L0_ffn_up_weight"], np.float32).T),
        "wdn0": prep_lhsT(np.asarray(inputs["L0_ffn_down_weight"], np.float32).T),
        "upsc": np.ascontiguousarray(
            np.asarray(inputs["L0_ffn_up_scale"], np.float32).reshape(FH // 128, 128).T),
        "dnsc": np.ascontiguousarray(
            np.asarray(inputs["L0_ffn_down_scale"], np.float32).reshape(KC, 128).T),
        "osc": np.asarray(inputs["L0_ffn_output_scale"], np.float32).reshape(1, 1),
        "wq": prep_lhsT(np.asarray(inputs["L1_attn_Wq"], np.float32)),
        "wk": prep_lhsT(np.asarray(inputs["L1_attn_Wk"], np.float32)),
        "wv": prep_lhsT(np.asarray(inputs["L1_attn_Wv"], np.float32)),
        "wgk1": prep_lhsT(np.asarray(inputs["L1_attn_Wgk1"], np.float32), pad_to=128),
        "bg": np.ascontiguousarray(
            np.asarray(inputs["L1_attn_bgk2"], np.float32).reshape(16, 128).T),
        "wg": prep_lhsT(np.asarray(inputs["L1_attn_Wg"], np.float32)),
        "wn": np.ascontiguousarray(
            np.asarray(inputs["L1_attn_Wnorm"], np.float32).reshape(2, 128).T),
        "wo": prep_lhsT(np.asarray(inputs["L1_attn_Wo"], np.float32)),
        "wga1": prep_lhsT(np.asarray(inputs["L1_ffn_w_gate"], np.float32).T),
        "wup1": prep_lhsT(np.asarray(inputs["L1_ffn_w_up"], np.float32).T),
        "wdn1": prep_lhsT(np.asarray(inputs["L1_ffn_w_down"], np.float32).T),
    }
    wgk2 = np.zeros((16, 128, 128), np.float32)
    wgk2[:, 0:16, :] = np.asarray(
        inputs["L1_attn_Wgk2"], np.float32).reshape(16, 16, 128).transpose(1, 0, 2)
    base["wgk2"] = wgk2
    maps = []
    for k in range(8):
        pmb = np.zeros((1, 4), np.float32)
        pmb[0, :k % 4] = 1.0
        pme = np.zeros((128, 4), np.float32)
        pme[:, :k % 4] = 1.0
        maps.append(dict(mapsA[k], **base, pmb=pmb, pme=pme))
    return maps


def kernel_fused(**inputs):
    inputs = {k: np.asarray(v) for k, v in inputs.items()}
    res = run_bass_kernel_spmd(build_fused(), prep_fused(inputs),
                               core_ids=list(range(8))).results
    out = np.empty((2, 4096, 2048), np.float32)
    for k in range(8):
        b, pos = k // 4, k % 4
        out[b, pos * T:(pos + 1) * T, :] = np.asarray(res[k]["xo"]).T
    return out
